# version 02
# baseline (speedup 1.0000x reference)
from contextlib import ExitStack

import numpy as np
import ml_dtypes
import concourse.bass as bass
import concourse.mybir as mybir
from concourse.bass_utils import run_bass_kernel_spmd
from concourse.bass import IndirectOffsetOnAxis

F32 = mybir.dt.float32
BF16 = mybir.dt.bfloat16
I32 = mybir.dt.int32
AF = mybir.ActivationFunctionType
ALU = mybir.AluOpType
AX = mybir.AxisListType

B_GLOBAL = 16
SEQ = 4096
DM = 1024
TPC = 8192
NBATCH = 2
N_TOK = B_GLOBAL * SEQ
CAPACITY = int(1.2 * (N_TOK // 8 + 1))
CPAD = 10240
HID = 4096
SLOTROWS = 131328
STAB = 1e-6
LN_EPS = 1e-5
REPS = 1e-9

bfdt = ml_dtypes.bfloat16


class K:

    def __init__(self, nc, ctx, block):
        self.nc = nc
        self.block = block
        self.sem = {}
        self.cnt = {}
        for name in ("Q", "P", "T", "A", "V", "O", "G"):
            self.sem[name] = ctx.enter_context(nc.semaphore("s" + name))
            self.cnt[name] = 0

    def bar(self, eng, skip=()):
        for name, h in self.sem.items():
            if name in skip:
                continue
            if self.cnt[name] > 0:
                eng.wait_ge(h, self.cnt[name])

    def done(self, inst, q, k=1):
        inst.then_inc(self.sem[q], k)
        self.cnt[q] += k



def build_L1():
    nc = bass.Bass("TRN2", target_bir_lowering=False, debug=False)
    xfm = nc.dram_tensor("xfm", [DM, TPC], F32, kind="ExternalInput")
    wqkv = nc.dram_tensor("wqkv", [DM, 3 * DM], F32, kind="ExternalInput")
    wout = nc.dram_tensor("wout", [DM, DM], F32, kind="ExternalInput")
    boutT = nc.dram_tensor("boutT", [128, 8], F32, kind="ExternalInput")
    rfbd = nc.dram_tensor("rfbd", [8 * 128, 128], F32, kind="ExternalInput")
    rwt = nc.dram_tensor("rwt", [DM, 8], F32, kind="ExternalInput")
    rbrow = nc.dram_tensor("rbrow", [1, 8], F32, kind="ExternalInput")
    tokid = nc.dram_tensor("tokid", [128, 64], F32, kind="ExternalInput")
    c_lt = nc.dram_tensor("c_lt", [128, 128], BF16, kind="ExternalInput")
    c_idb = nc.dram_tensor("c_idb", [128, 128], BF16, kind="ExternalInput")
    c_idf = nc.dram_tensor("c_idf", [128, 128], F32, kind="ExternalInput")

    xn2tm = nc.dram_tensor("xn2tm", [TPC, DM], BF16, kind="ExternalOutput")
    x2tm = nc.dram_tensor("x2tm", [TPC, DM], BF16, kind="ExternalOutput")
    meta = nc.dram_tensor("meta", [128, 64 * 8], F32, kind="ExternalOutput")
    counts = nc.dram_tensor("counts", [1, 16], F32, kind="ExternalOutput")
    slotmap = nc.dram_tensor("slotmap", [SLOTROWS, 2], F32, kind="ExternalOutput")

    with ExitStack() as ctx:
        e = ctx.enter_context
        wq_b = e(nc.sbuf_tensor("wq_b", [128, 8 * 2048], BF16))
        wo_b = e(nc.sbuf_tensor("wo_b", [128, 8 * 1024], BF16))
        rf_b = e(nc.sbuf_tensor("rf_b", [128, 8 * 128], BF16))
        rw_b = e(nc.sbuf_tensor("rw_b", [128, 64], BF16))
        bout_s = e(nc.sbuf_tensor("bout_s", [128, 8], F32))
        rb_s = e(nc.sbuf_tensor("rb_s", [128, 8], F32))
        lt_s = e(nc.sbuf_tensor("lt_s", [128, 128], BF16))
        idb_s = e(nc.sbuf_tensor("idb_s", [128, 128], BF16))
        idf_s = e(nc.sbuf_tensor("idf_s", [128, 128], F32))
        ones_c = e(nc.sbuf_tensor("ones_c", [128, 1], BF16))
        ones_r = e(nc.sbuf_tensor("ones_r", [1, 128], BF16))
        onesf_r = e(nc.sbuf_tensor("onesf_r", [1, 128], F32))
        epscol = e(nc.sbuf_tensor("epscol", [128, 1], F32))
        stabcol = e(nc.sbuf_tensor("stabcol", [128, 1], F32))
        tokid_s = e(nc.sbuf_tensor("tokid_s", [128, 64], F32))

        xbf = e(nc.sbuf_tensor("xbf", [128, 8 * 512], BF16))
        xn1 = e(nc.sbuf_tensor("xn1", [128, 8 * 512], BF16))

        xres = e(nc.sbuf_tensor("xres", [128, 512], BF16))
        mrow = e(nc.sbuf_tensor("mrow", [1, 4096], F32))
        vrow = e(nc.sbuf_tensor("vrow", [1, 4096], F32))
        rrow = e(nc.sbuf_tensor("rrow", [1, 4096], F32))
        m_bf = e(nc.sbuf_tensor("m_bf", [1, 512], BF16))
        r_bf = e(nc.sbuf_tensor("r_bf", [1, 512], BF16))
        Mb = e(nc.sbuf_tensor("Mb", [128, 4096], BF16))
        Rb = e(nc.sbuf_tensor("Rb", [128, 4096], BF16))
        xsq = xn1
        m2row = mrow
        v2row = vrow
        r2row = rrow
        rm2row = vrow
        m2col = e(nc.sbuf_tensor("m2col", [128, 64], F32))
        r2col = e(nc.sbuf_tensor("r2col", [128, 64], F32))
        rm2col = e(nc.sbuf_tensor("rm2col", [128, 64], F32))

        kbf = e(nc.sbuf_tensor("kbf", [128, 512], BF16))
        vbf = e(nc.sbuf_tensor("vbf", [128, 512], BF16))
        v_tm = e(nc.sbuf_tensor("v_tm", [128, 4 * 130], BF16))
        phik = e(nc.sbuf_tensor("phik", [128, 512], BF16))
        kvacc = e(nc.sbuf_tensor("kvacc", [128, 8 * 130], F32))
        kvbd = e(nc.sbuf_tensor("kvbd", [128, 8 * 128], BF16))
        ksr = e(nc.sbuf_tensor("ksr", [128, 8 * 128], BF16))
        qbf = kbf
        phiq = phik
        phiqn = vbf
        attn_nt = e(nc.sbuf_tensor("attn_nt", [128, 8 * 512], BF16))
        x2f = e(nc.sbuf_tensor("x2f", [128, 512], F32))
        recp = x2f
        x2b = e(nc.sbuf_tensor("x2b", [128, 512], BF16))
        x2s = e(nc.sbuf_tensor("x2s", [128, 512], BF16))
        x2tmt = e(nc.sbuf_tensor("x2tmt", [128, 4 * 1024], BF16))
        rawl = e(nc.sbuf_tensor("rawl", [128, 64 * 8], F32))
        csum_b = e(nc.sbuf_tensor("csum_b", [128, 8], F32))
        xn2t = e(nc.sbuf_tensor("xn2t", [128, 1024], BF16))

        lgt = e(nc.sbuf_tensor("lgt", [128, 64 * 8], F32))
        epat = e(nc.sbuf_tensor("epat", [128, 64 * 8], F32))
        u_t = e(nc.sbuf_tensor("u_t", [128, 64 * 8], F32))
        scr8 = e(nc.sbuf_tensor("scr8", [128, 64 * 8], F32))
        scr8b = e(nc.sbuf_tensor("scr8b", [128, 64 * 8], F32))
        red1 = e(nc.sbuf_tensor("red1", [128, 64], F32))
        mx_t = e(nc.sbuf_tensor("mx_t", [128, 64], F32))
        s_t = e(nc.sbuf_tensor("s_t", [128, 64], F32))
        u1_t = e(nc.sbuf_tensor("u1_t", [128, 64], F32))
        u2_t = e(nc.sbuf_tensor("u2_t", [128, 64], F32))
        i1_t = e(nc.sbuf_tensor("i1_t", [128, 64], F32))
        i2_t = e(nc.sbuf_tensor("i2_t", [128, 64], F32))
        w1_t = e(nc.sbuf_tensor("w1_t", [128, 64], F32))
        w2_t = e(nc.sbuf_tensor("w2_t", [128, 64], F32))
        va1_t = e(nc.sbuf_tensor("va1_t", [128, 64], F32))
        va2_t = e(nc.sbuf_tensor("va2_t", [128, 64], F32))
        oh1 = e(nc.sbuf_tensor("oh1", [128, 64 * 8], BF16))
        oh2 = e(nc.sbuf_tensor("oh2", [128, 64 * 8], BF16))
        pos1 = lgt
        pos2 = u_t
        po1 = e(nc.sbuf_tensor("po1", [128, 64], F32))
        po2 = e(nc.sbuf_tensor("po2", [128, 64], F32))
        epat2 = e(nc.sbuf_tensor("epat2", [128, 64 * 8], F32))
        ohM1 = e(nc.sbuf_tensor("ohM1", [128, 64 * 8], BF16))
        ohM2 = e(nc.sbuf_tensor("ohM2", [128, 64 * 8], BF16))
        totb = e(nc.sbuf_tensor("totb", [128, 2 * 512], F32))
        tscab = e(nc.sbuf_tensor("tscab", [128, 2 * 512], F32))
        onescol_b = e(nc.sbuf_tensor("onescol_b", [128, 64], F32))
        cntrow = e(nc.sbuf_tensor("cntrow", [1, 16], F32))
        meta_s = rawl
        scv = e(nc.sbuf_tensor("scv", [128, 2 * 128], F32))
        eaff = e(nc.sbuf_tensor("eaff", [128, 64 * 8], F32))
        offi = e(nc.sbuf_tensor("offi", [128, 2 * 64], I32))

        P0 = e(nc.psum_tensor("P0", [128, 512], F32))
        P1 = e(nc.psum_tensor("P1", [128, 512], F32))
        P2 = e(nc.psum_tensor("P2", [128, 512], F32))
        P3 = e(nc.psum_tensor("P3", [128, 512], F32))
        P4 = e(nc.psum_tensor("P4", [128, 512], F32))
        PT = e(nc.psum_tensor("PT", [128, 1024], BF16))
        PRa = e(nc.psum_tensor("PRa", [1, 512], F32))
        PRb = e(nc.psum_tensor("PRb", [1, 512], F32))

        block = e(nc.Block())
        S = K(nc, ctx, block)

        def gp(f):
            block.gpsimd(f)

        def tn(f):
            block.tensor(f)

        def sc(f):
            block.scalar(f)

        def ve(f):
            block.vector(f)

        def sy(f):
            block.sync(f)

        def _(sync):
            sync.dma_start(out=lt_s[:], in_=c_lt[:, :]).then_inc(S.sem["Q"], 16)
            sync.dma_start(out=idb_s[:], in_=c_idb[:, :]).then_inc(S.sem["Q"], 16)
            sync.dma_start(out=idf_s[:], in_=c_idf[:, :]).then_inc(S.sem["Q"], 16)
            sync.dma_start(out=bout_s[:], in_=boutT[:, :]).then_inc(S.sem["Q"], 16)
            sync.dma_start(out=tokid_s[:], in_=tokid[:, :]).then_inc(S.sem["Q"], 16)
            sync.dma_start(out=rb_s[0:1, :], in_=rbrow[:, :]).then_inc(S.sem["Q"], 16)
            S.cnt["Q"] += 6 * 16
        sy(_)

        def _(g):
            for qi in range(8):
                g.dma_start(out=wo_b[:, qi * 1024:(qi + 1) * 1024],
                            in_=wout[qi * 128:(qi + 1) * 128, :]).then_inc(S.sem["P"], 16)
            for a in range(8):
                g.dma_start(out=rf_b[:, a * 128:(a + 1) * 128],
                            in_=rfbd[a * 128:(a + 1) * 128, :]).then_inc(S.sem["P"], 16)
            for ci in range(8):
                g.dma_start(out=rw_b[:, ci * 8:(ci + 1) * 8],
                            in_=rwt[ci * 128:(ci + 1) * 128, :]).then_inc(S.sem["P"], 16)
            S.cnt["P"] += 24 * 16
            g.iota(epat[:], [[0, 64], [1, 8]], base=0, channel_multiplier=0,
                   allow_small_or_imprecise_dtypes=True).then_inc(S.sem["G"], 1)
            g.iota(epat2[:], [[1, 8], [0, 64]], base=0, channel_multiplier=0,
                   allow_small_or_imprecise_dtypes=True).then_inc(S.sem["G"], 1)
            g.iota(eaff[:], [[0, 64], [1, 8]], base=64, channel_multiplier=0,
                   allow_small_or_imprecise_dtypes=True).then_inc(S.sem["G"], 1)
            S.cnt["G"] += 3
        gp(_)

        def _(v):
            S.bar(v)
            v.memset(ones_c[:], 1.0)
            v.memset(ones_r[:], 1.0)
            v.memset(onesf_r[:], 1.0)
            v.memset(epscol[:], LN_EPS)
            v.memset(stabcol[:], STAB)
            v.memset(onescol_b[:], 1.0)
            v.memset(kvacc[:], 0.0)
            inst = v.memset(v_tm[:], 1.0)
            S.done(inst, "V")
        ve(_)

        def _(t):
            S.bar(t)
            inst = t.matmul(P0[:, 0:8], onesf_r[:], rb_s[0:1, :], start=True, stop=True)
            S.done(inst, "T")
        tn(_)

        def _(v):
            S.bar(v)
            inst = v.tensor_copy(rb_s[:, :], P0[:, 0:8])
            S.done(inst, "V")
        ve(_)

        def _(t):
            S.bar(t)
            for ci in range(8):
                inst = t.matmul(PRa[0:1, 0:8], ones_c[:], rw_b[:, ci * 8:(ci + 1) * 8],
                                start=(ci == 0), stop=(ci == 7))
            S.done(inst, "T")
        tn(_)

        def _(v):
            S.bar(v)
            inst = v.tensor_scalar_mul(csum_b[0:1, :], PRa[0:1, 0:8], -1.0)
            S.done(inst, "V")
        ve(_)

        def _(t):
            S.bar(t)
            inst = t.matmul(P0[:, 0:8], onesf_r[:], csum_b[0:1, 0:8], start=True, stop=True)
            S.done(inst, "T")
        tn(_)

        def _(v):
            S.bar(v)
            inst = v.tensor_copy(csum_b[:, :], P0[:, 0:8])
            S.done(inst, "V")
        ve(_)

        for b in range(NBATCH):
            for nt in range(8):
                def _(g, nt=nt, b=b):
                    S.bar(g)
                    for ci in range(8):
                        inst = g.dma_start(out=xbf[:, ci * 512:(ci + 1) * 512],
                                           in_=xfm[ci * 128:(ci + 1) * 128,
                                                   b * 4096 + nt * 512:b * 4096 + (nt + 1) * 512])
                        inst.then_inc(S.sem["P"], 16)
                    S.cnt["P"] += 8 * 16
                gp(_)

                def _(s, nt=nt):
                    S.bar(s)
                    for ci in range(8):
                        inst = s.activation(out=xsq[:, ci * 512:(ci + 1) * 512],
                                            in_=xbf[:, ci * 512:(ci + 1) * 512], func=AF.Square)
                    S.done(inst, "A")
                sc(_)

                def _(t, nt=nt):
                    S.bar(t)
                    for ci in range(8):
                        t.matmul(PRa[0:1, :], ones_c[:], xbf[:, ci * 512:(ci + 1) * 512],
                                 start=(ci == 0), stop=(ci == 7))
                    for ci in range(8):
                        inst = t.matmul(PRb[0:1, :], ones_c[:], xsq[:, ci * 512:(ci + 1) * 512],
                                        start=(ci == 0), stop=(ci == 7))
                    S.done(inst, "T")
                tn(_)

                def _(v, nt=nt):
                    S.bar(v)
                    v.tensor_copy(mrow[0:1, nt * 512:(nt + 1) * 512], PRa[0:1, :])
                    inst = v.tensor_copy(vrow[0:1, nt * 512:(nt + 1) * 512], PRb[0:1, :])
                    S.done(inst, "V")
                ve(_)

            def _(v, b=b):
                S.bar(v)
                v.tensor_scalar_mul(mrow[:], mrow[:], 1.0 / DM)
                inst = v.tensor_scalar_mul(vrow[:], vrow[:], 1.0 / DM)
                S.done(inst, "V")
            ve(_)

            def _(v, b=b):
                S.bar(v)
                inst = v.tensor_mul(rrow[:], mrow[:], mrow[:])
                S.done(inst, "V")
            ve(_)

            def _(v, b=b):
                S.bar(v)
                inst = v.tensor_sub(vrow[:], vrow[:], rrow[:])
                S.done(inst, "V")
            ve(_)

            def _(s, b=b):
                S.bar(s)
                inst = s.activation(out=rrow[:], in_=vrow[:], func=AF.Ln,
                                    bias=epscol[0:1, 0:1])
                S.done(inst, "A")
            sc(_)

            def _(s, b=b):
                S.bar(s)
                inst = s.activation(out=rrow[:], in_=rrow[:], func=AF.Exp, scale=-0.5)
                S.done(inst, "A")
            sc(_)

            for nt in range(8):
                def _(s, nt=nt):
                    S.bar(s)
                    s.copy(m_bf[:], mrow[0:1, nt * 512:(nt + 1) * 512])
                    inst = s.copy(r_bf[:], rrow[0:1, nt * 512:(nt + 1) * 512])
                    S.done(inst, "A")
                sc(_)

                def _(t, nt=nt):
                    S.bar(t)
                    t.matmul(P0[:, :], ones_r[:], m_bf[0:1, :], start=True, stop=True)
                    inst = t.matmul(P1[:, :], ones_r[:], r_bf[0:1, :], start=True, stop=True)
                    S.done(inst, "T")
                tn(_)

                def _(s, nt=nt):
                    S.bar(s)
                    s.copy(Mb[:, nt * 512:(nt + 1) * 512], P0[:, :])
                    inst = s.copy(Rb[:, nt * 512:(nt + 1) * 512], P1[:, :])
                    S.done(inst, "A")
                sc(_)

            def _(g, b=b):
                S.bar(g)
                for ci in range(8):
                    inst = g.dma_start(out=wq_b[:, ci * 2048:(ci + 1) * 2048],
                                       in_=wqkv[ci * 128:(ci + 1) * 128, 1024:3072])
                    inst.then_inc(S.sem["P"], 16)
                S.cnt["P"] += 8 * 16
            gp(_)

            for nt in range(8):
                def _(g, nt=nt, b=b):
                    S.bar(g)
                    for ci in range(8):
                        inst = g.dma_start(out=xbf[:, ci * 512:(ci + 1) * 512],
                                           in_=xfm[ci * 128:(ci + 1) * 128,
                                                   b * 4096 + nt * 512:b * 4096 + (nt + 1) * 512])
                        inst.then_inc(S.sem["P"], 16)
                    S.cnt["P"] += 8 * 16
                gp(_)

                def _(v, nt=nt):
                    S.bar(v)
                    x3 = xbf[:].rearrange("p (c t) -> p c t", c=8)
                    n3 = xn1[:].rearrange("p (c t) -> p c t", c=8)
                    mb3 = (Mb[:, nt * 512:(nt + 1) * 512]
                           .rearrange("p (u t) -> p u t", u=1)
                           .broadcast_to([128, 8, 512]))
                    inst = v.tensor_tensor(n3, x3, mb3, ALU.subtract)
                    S.done(inst, "V")
                ve(_)

                def _(v, nt=nt):
                    S.bar(v)
                    n3 = xn1[:].rearrange("p (c t) -> p c t", c=8)
                    rb3 = (Rb[:, nt * 512:(nt + 1) * 512]
                           .rearrange("p (u t) -> p u t", u=1)
                           .broadcast_to([128, 8, 512]))
                    inst = v.tensor_tensor(n3, n3, rb3, ALU.mult)
                    S.done(inst, "V")
                ve(_)

                for pr in range(8):
                    def _(t, nt=nt, pr=pr):
                        S.bar(t)
                        for ci in range(8):
                            t.matmul(P0[:, :], wq_b[:, ci * 2048 + pr * 128:
                                                    ci * 2048 + (pr + 1) * 128],
                                     xn1[:, ci * 512:(ci + 1) * 512],
                                     start=(ci == 0), stop=(ci == 7))
                        for ci in range(8):
                            inst = t.matmul(P1[:, :], wq_b[:, ci * 2048 + 1024 + pr * 128:
                                                           ci * 2048 + 1024 + (pr + 1) * 128],
                                            xn1[:, ci * 512:(ci + 1) * 512],
                                            start=(ci == 0), stop=(ci == 7))
                        S.done(inst, "T")
                    tn(_)

                    def _(s, nt=nt, pr=pr):
                        S.bar(s)
                        s.copy(kbf[:], P0[:, :])
                        inst = s.copy(vbf[:], P1[:, :])
                        S.done(inst, "A")
                    sc(_)

                    def _(t, nt=nt, pr=pr):
                        S.bar(t)
                        for j in range(4):
                            t.matmul(P2[:, j * 128:(j + 1) * 128],
                                     kbf[:, j * 128:(j + 1) * 128],
                                     rf_b[:, pr * 128:(pr + 1) * 128],
                                     start=(j == 0), stop=(j == 3))
                        for j in range(4):
                            inst = t.transpose(PT[:, j * 128:(j + 1) * 128],
                                               vbf[:, j * 128:(j + 1) * 128], idb_s[:])
                        S.done(inst, "T")
                    tn(_)

                    def _(s, nt=nt, pr=pr):
                        S.bar(s)
                        inst = s.activation(out=phik[:], in_=P2[:, :], func=AF.Relu)
                        S.done(inst, "A")
                    sc(_)

                    def _(v, nt=nt, pr=pr):
                        S.bar(v)
                        for j in range(4):
                            v.tensor_copy(v_tm[:, j * 130:j * 130 + 64],
                                          PT[:, j * 128:j * 128 + 64])
                            inst = v.tensor_copy(v_tm[:, j * 130 + 65:j * 130 + 129],
                                                 PT[:, j * 128 + 64:(j + 1) * 128])
                        S.done(inst, "V")
                    ve(_)

                    def _(t, nt=nt, pr=pr):
                        S.bar(t)
                        first = True
                        for j in range(4):
                            t.matmul(P3[0:64, 0:65], phik[:, j * 128:j * 128 + 64],
                                     v_tm[:, j * 130:j * 130 + 65],
                                     start=first, stop=False)
                            first = False
                            inst = t.matmul(P3[64:128, 65:130],
                                            phik[:, j * 128 + 64:(j + 1) * 128],
                                            v_tm[:, j * 130 + 65:j * 130 + 130],
                                            start=False, stop=(j == 3))
                        S.done(inst, "T")
                    tn(_)

                    def _(v, nt=nt, pr=pr):
                        S.bar(v)
                        sl = slice(pr * 130, (pr + 1) * 130)
                        inst = v.tensor_add(kvacc[:, sl], kvacc[:, sl], P3[:, 0:130])
                        S.done(inst, "V")
                    ve(_)

            def _(v, b=b):
                S.bar(v)
                v.memset(kvbd[:], 0.0)
                v.memset(ksr[:], 0.0)
                for pr in range(8):
                    base = pr * 130
                    v.tensor_copy(kvbd[0:64, pr * 128:pr * 128 + 64],
                                  kvacc[0:64, base:base + 64])
                    v.tensor_copy(kvbd[64:128, pr * 128 + 64:(pr + 1) * 128],
                                  kvacc[64:128, base + 65:base + 129])
                    v.tensor_copy(ksr[0:64, pr * 128:pr * 128 + 64],
                                  kvacc[0:64, base + 64:base + 65].broadcast_to([64, 64]))
                    inst = v.tensor_copy(ksr[64:128, pr * 128 + 64:(pr + 1) * 128],
                                         kvacc[64:128, base + 129:base + 130]
                                         .broadcast_to([64, 64]))
                v.memset(kvacc[:], 0.0)
                S.done(inst, "V")
            ve(_)

            def _(g, b=b):
                S.bar(g)
                for ci in range(8):
                    inst = g.dma_start(out=wq_b[:, ci * 2048:ci * 2048 + 1024],
                                       in_=wqkv[ci * 128:(ci + 1) * 128, 0:1024])
                    inst.then_inc(S.sem["P"], 16)
                S.cnt["P"] += 8 * 16
            gp(_)

            for nt in range(8):
                def _(g, nt=nt, b=b):
                    S.bar(g)
                    for ci in range(8):
                        inst = g.dma_start(out=xbf[:, ci * 512:(ci + 1) * 512],
                                           in_=xfm[ci * 128:(ci + 1) * 128,
                                                   b * 4096 + nt * 512:b * 4096 + (nt + 1) * 512])
                        inst.then_inc(S.sem["P"], 16)
                    S.cnt["P"] += 8 * 16
                gp(_)

                def _(v, nt=nt):
                    S.bar(v)
                    for ci in range(8):
                        sl = slice(ci * 512, (ci + 1) * 512)
                        v.tensor_sub(xn1[:, sl], xbf[:, sl], Mb[:, nt * 512:(nt + 1) * 512])
                        inst = v.tensor_mul(xn1[:, sl], xn1[:, sl],
                                            Rb[:, nt * 512:(nt + 1) * 512])
                    S.done(inst, "V")
                ve(_)

                for pr in range(8):
                    def _(t, nt=nt, pr=pr):
                        S.bar(t)
                        for ci in range(8):
                            inst = t.matmul(P0[:, :], wq_b[:, ci * 2048 + pr * 128:
                                                          ci * 2048 + (pr + 1) * 128],
                                            xn1[:, ci * 512:(ci + 1) * 512],
                                            start=(ci == 0), stop=(ci == 7))
                        S.done(inst, "T")
                    tn(_)

                    def _(s, nt=nt, pr=pr):
                        S.bar(s)
                        inst = s.copy(qbf[:], P0[:, :])
                        S.done(inst, "A")
                    sc(_)

                    def _(t, nt=nt, pr=pr):
                        S.bar(t)
                        inst = t.matmul(P1[:, :], rf_b[:, pr * 128:(pr + 1) * 128], qbf[:],
                                        start=True, stop=True)
                        S.done(inst, "T")
                    tn(_)

                    def _(s, nt=nt, pr=pr):
                        S.bar(s)
                        inst = s.activation(out=phiq[:], in_=P1[:, :], func=AF.Relu)
                        S.done(inst, "A")
                    sc(_)

                    def _(t, nt=nt, pr=pr):
                        S.bar(t)
                        inst = t.matmul(P2[:, :], ksr[:, pr * 128:(pr + 1) * 128], phiq[:],
                                        start=True, stop=True)
                        S.done(inst, "T")
                    tn(_)

                    def _(s, nt=nt, pr=pr):
                        S.bar(s)
                        inst = s.activation(out=recp[:], in_=P2[:, :], func=AF.Ln,
                                            bias=stabcol[:, 0:1])
                        S.done(inst, "A")
                    sc(_)

                    def _(s, nt=nt, pr=pr):
                        S.bar(s)
                        inst = s.activation(out=recp[:], in_=recp[:], func=AF.Exp,
                                            scale=-1.0)
                        S.done(inst, "A")
                    sc(_)

                    def _(v, nt=nt, pr=pr):
                        S.bar(v)
                        inst = v.tensor_mul(phiqn[:], phiq[:], recp[:])
                        S.done(inst, "V")
                    ve(_)

                    def _(t, nt=nt, pr=pr):
                        S.bar(t)
                        inst = t.matmul(P3[:, :], kvbd[:, pr * 128:(pr + 1) * 128], phiqn[:],
                                        start=True, stop=True)
                        S.done(inst, "T")
                    tn(_)

                    def _(s, nt=nt, pr=pr):
                        S.bar(s)
                        inst = s.copy(attn_nt[:, pr * 512:(pr + 1) * 512], P3[:, :])
                        S.done(inst, "A")
                    sc(_)

                for oi in range(8):
                    def _(g, nt=nt, oi=oi, b=b):
                        S.bar(g)
                        inst = g.dma_start(out=xres[:],
                                           in_=xfm[oi * 128:(oi + 1) * 128,
                                                   b * 4096 + nt * 512:b * 4096 + (nt + 1) * 512])
                        inst.then_inc(S.sem["P"], 16)
                        S.cnt["P"] += 16
                    gp(_)

                    def _(t, nt=nt, oi=oi):
                        S.bar(t)
                        for qi in range(8):
                            inst = t.matmul(P4[:, :], wo_b[:, qi * 1024 + oi * 128:
                                                          qi * 1024 + (oi + 1) * 128],
                                            attn_nt[:, qi * 512:(qi + 1) * 512],
                                            start=(qi == 0), stop=(qi == 7))
                        S.done(inst, "T")
                    tn(_)

                    def _(v, nt=nt, oi=oi):
                        S.bar(v)
                        inst = v.tensor_add(x2f[:], P4[:, :], xres[:])
                        S.done(inst, "V")
                    ve(_)

                    def _(s, nt=nt, oi=oi):
                        S.bar(s)
                        inst = s.activation(out=x2b[:], in_=x2f[:], func=AF.Identity,
                                            bias=bout_s[:, oi:oi + 1])
                        S.done(inst, "A")
                    sc(_)

                    def _(s, nt=nt, oi=oi):
                        S.bar(s)
                        inst = s.activation(out=x2s[:], in_=x2b[:], func=AF.Square)
                        S.done(inst, "A")
                    sc(_)

                    def _(t, nt=nt, oi=oi, b=b):
                        S.bar(t)
                        t.matmul(PRa[0:1, :], ones_c[:], x2b[:],
                                 start=(oi == 0), stop=(oi == 7))
                        t.matmul(PRb[0:1, :], ones_c[:], x2s[:],
                                 start=(oi == 0), stop=(oi == 7))
                        for j in range(4):
                            t.matmul(P2[:, j * 8:(j + 1) * 8],
                                     x2b[:, j * 128:(j + 1) * 128],
                                     rw_b[:, oi * 8:(oi + 1) * 8],
                                     start=(oi == 0 and j == 0), stop=(oi == 7 and j == 3))
                        for j in range(4):
                            inst = t.transpose(PT[:, j * 128:(j + 1) * 128],
                                               x2b[:, j * 128:(j + 1) * 128], idb_s[:])
                        S.done(inst, "T")
                    tn(_)

                    def _(v, nt=nt, oi=oi):
                        S.bar(v)
                        for j in range(4):
                            inst = v.tensor_copy(
                                x2tmt[:, j * 1024 + oi * 128:j * 1024 + (oi + 1) * 128],
                                PT[:, j * 128:(j + 1) * 128])
                        S.done(inst, "V")
                    ve(_)

                def _(v, nt=nt, b=b):
                    S.bar(v)
                    v.tensor_copy(m2row[0:1, nt * 512:(nt + 1) * 512], PRa[0:1, :])
                    v.tensor_copy(v2row[0:1, nt * 512:(nt + 1) * 512], PRb[0:1, :])
                    for j in range(4):
                        inst = v.tensor_copy(rawl[:, (b * 32 + nt * 4 + j) * 8:
                                                   (b * 32 + nt * 4 + j + 1) * 8],
                                             P2[:, j * 8:(j + 1) * 8])
                    S.done(inst, "V")
                ve(_)

                def _(sync, nt=nt, b=b):
                    S.bar(sync)
                    for j in range(4):
                        r0 = b * 4096 + nt * 512 + j * 128
                        inst = sync.dma_start(out=x2tm[r0:r0 + 128, :],
                                              in_=x2tmt[:, j * 1024:(j + 1) * 1024])
                        inst.then_inc(S.sem["Q"], 16)
                    S.cnt["Q"] += 4 * 16
                sy(_)

            def _(v, b=b):
                S.bar(v)
                v.tensor_scalar_mul(m2row[:], m2row[:], 1.0 / DM)
                inst = v.tensor_scalar_mul(v2row[:], v2row[:], 1.0 / DM)
                S.done(inst, "V")
            ve(_)

            def _(v, b=b):
                S.bar(v)
                inst = v.tensor_mul(r2row[:], m2row[:], m2row[:])
                S.done(inst, "V")
            ve(_)

            def _(v, b=b):
                S.bar(v)
                inst = v.tensor_sub(v2row[:], v2row[:], r2row[:])
                S.done(inst, "V")
            ve(_)

            def _(s, b=b):
                S.bar(s)
                inst = s.activation(out=r2row[:], in_=v2row[:], func=AF.Ln,
                                    bias=epscol[0:1, 0:1])
                S.done(inst, "A")
            sc(_)

            def _(s, b=b):
                S.bar(s)
                inst = s.activation(out=r2row[:], in_=r2row[:], func=AF.Exp, scale=-0.5)
                S.done(inst, "A")
            sc(_)

            def _(v, b=b):
                S.bar(v)
                inst = v.tensor_mul(rm2row[:], r2row[:], m2row[:])
                S.done(inst, "V")
            ve(_)

            def _(t, b=b):
                S.bar(t)
                for ch in range(32):
                    t.transpose(P0[:, ch:ch + 1], m2row[0:1, ch * 128:(ch + 1) * 128],
                                idf_s[0:1, 0:1])
                for ch in range(32):
                    t.transpose(P1[:, ch:ch + 1], r2row[0:1, ch * 128:(ch + 1) * 128],
                                idf_s[0:1, 0:1])
                for ch in range(32):
                    inst = t.transpose(P2[:, ch:ch + 1], rm2row[0:1, ch * 128:(ch + 1) * 128],
                                       idf_s[0:1, 0:1])
                S.done(inst, "T")
            tn(_)

            def _(v, b=b):
                S.bar(v)
                v.tensor_copy(m2col[:, b * 32:(b + 1) * 32], P0[:, 0:32])
                v.tensor_copy(r2col[:, b * 32:(b + 1) * 32], P1[:, 0:32])
                inst = v.tensor_copy(rm2col[:, b * 32:(b + 1) * 32], P2[:, 0:32])
                S.done(inst, "V")
            ve(_)

            def _(v, b=b):
                S.bar(v)
                sl = slice(b * 32 * 8, (b + 1) * 32 * 8)
                c3 = (csum_b[:].rearrange("p (u e) -> p u e", u=1)
                      .broadcast_to([128, 32, 8]))
                rm3 = (rm2col[:, b * 32:(b + 1) * 32]
                       .rearrange("p (c u) -> p c u", u=1).broadcast_to([128, 32, 8]))
                r23 = (r2col[:, b * 32:(b + 1) * 32]
                       .rearrange("p (c u) -> p c u", u=1).broadcast_to([128, 32, 8]))
                u3 = u_t[:, sl].rearrange("p (c e) -> p c e", e=8)
                l3 = lgt[:, sl].rearrange("p (c e) -> p c e", e=8)
                raw3 = rawl[:, sl].rearrange("p (c e) -> p c e", e=8)
                v.tensor_tensor(u3, c3, rm3, ALU.mult)
                inst = v.tensor_tensor(l3, raw3, r23, ALU.mult)
                S.done(inst, "V")
            ve(_)

            def _(v, b=b):
                S.bar(v)
                sl = slice(b * 32 * 8, (b + 1) * 32 * 8)
                inst = v.tensor_add(lgt[:, sl], lgt[:, sl], u_t[:, sl])
                S.done(inst, "V")
            ve(_)

            def _(v, b=b):
                S.bar(v)
                sl = slice(b * 32 * 8, (b + 1) * 32 * 8)
                l3 = lgt[:, sl].rearrange("p (c e) -> p c e", e=8)
                rb3 = (rb_s[:].rearrange("p (u e) -> p u e", u=1)
                       .broadcast_to([128, 32, 8]))
                inst = v.tensor_tensor(l3, l3, rb3, ALU.add)
                S.done(inst, "V")
            ve(_)

            for ch in range(32):
                def _(sync, ch=ch, b=b):
                    S.bar(sync)
                    r0 = b * 4096 + ch * 128
                    inst = sync.dma_start(out=xn2t[:], in_=x2tm[r0:r0 + 128, :])
                    inst.then_inc(S.sem["Q"], 16)
                    S.cnt["Q"] += 16
                sy(_)

                def _(v, ch=ch, b=b):
                    S.bar(v)
                    gch = b * 32 + ch
                    inst = v.tensor_scalar(out=xn2t[:], in0=xn2t[:],
                                           scalar1=m2col[:, gch:gch + 1],
                                           scalar2=r2col[:, gch:gch + 1],
                                           op0=ALU.subtract, op1=ALU.mult)
                    S.done(inst, "V")
                ve(_)

                def _(sync, ch=ch, b=b):
                    S.bar(sync)
                    r0 = b * 4096 + ch * 128
                    inst = sync.dma_start(out=xn2tm[r0:r0 + 128, :], in_=xn2t[:])
                    inst.then_inc(S.sem["Q"], 16)
                    S.cnt["Q"] += 16
                sy(_)

        def _(v):
            S.bar(v)
            l3 = lgt[:].rearrange("p (c e) -> p c e", e=8)
            inst = v.tensor_reduce(mx_t[:], l3, AX.X, ALU.max)
            S.done(inst, "V")
        ve(_)

        def _(v):
            S.bar(v)
            u3 = u_t[:].rearrange("p (c e) -> p c e", e=8)
            l3 = lgt[:].rearrange("p (c e) -> p c e", e=8)

            def bc(t64):
                return t64[:].rearrange("p (c u) -> p c u", u=1).broadcast_to([128, 64, 8])

            inst = v.tensor_tensor(u3, l3, bc(mx_t), ALU.subtract)
            S.done(inst, "V")
        ve(_)

        def _(s):
            S.bar(s)
            inst = s.activation(out=u_t[:], in_=u_t[:], func=AF.Exp)
            S.done(inst, "A")
        sc(_)

        def _(v):
            S.bar(v)
            u3 = u_t[:].rearrange("p (c e) -> p c e", e=8)
            e3 = epat[:].rearrange("p (c e) -> p c e", e=8)
            s3 = scr8[:].rearrange("p (c e) -> p c e", e=8)
            s3b = scr8b[:].rearrange("p (c e) -> p c e", e=8)

            def bc(t64):
                return t64[:].rearrange("p (c u) -> p c u", u=1).broadcast_to([128, 64, 8])

            v.tensor_reduce(s_t[:], u3, AX.X, ALU.add)
            inst = v.tensor_reduce(u1_t[:], u3, AX.X, ALU.max)
            S.done(inst, "V")
        ve(_)

        def bc(t64):
            return t64[:].rearrange("p (c u) -> p c u", u=1).broadcast_to([128, 64, 8])

        def bcM(t64):
            return t64[:].rearrange("p (u c) -> p u c", u=1).broadcast_to([128, 8, 64])

        U3 = u_t[:].rearrange("p (c e) -> p c e", e=8)
        E3 = epat[:].rearrange("p (c e) -> p c e", e=8)
        S3 = scr8[:].rearrange("p (c e) -> p c e", e=8)
        S3B = scr8b[:].rearrange("p (c e) -> p c e", e=8)
        L3V = lgt[:].rearrange("p (c e) -> p c e", e=8)
        EM = epat2[:].rearrange("p (e c) -> p e c", e=8)

        def _(v):
            S.bar(v)
            inst = v.tensor_tensor(S3, U3, bc(u1_t), ALU.is_ge)
            S.done(inst, "V")
        ve(_)

        def _(v):
            S.bar(v)
            inst = v.scalar_tensor_tensor(scr8b[:], scr8[:], -64.0, eaff[:],
                                          ALU.mult, ALU.add)
            S.done(inst, "V")
        ve(_)

        def _(v):
            S.bar(v)
            inst = v.tensor_reduce(i1_t[:], S3B, AX.X, ALU.min)
            S.done(inst, "V")
        ve(_)

        def _(v):
            S.bar(v)
            inst = v.tensor_tensor(S3, E3, bc(i1_t), ALU.not_equal)
            S.done(inst, "V")
        ve(_)

        def _(v):
            S.bar(v)
            inst = v.tensor_tensor(S3B, U3, S3, ALU.mult)
            S.done(inst, "V")
        ve(_)

        def _(v):
            S.bar(v)
            inst = v.tensor_reduce(u2_t[:], S3B, AX.X, ALU.max)
            S.done(inst, "V")
        ve(_)

        def _(v):
            S.bar(v)
            inst = v.tensor_tensor(L3V, U3, bc(u2_t), ALU.is_ge)
            S.done(inst, "V")
        ve(_)

        def _(v):
            S.bar(v)
            inst = v.tensor_mul(lgt[:], lgt[:], scr8[:])
            S.done(inst, "V")
        ve(_)

        def _(v):
            S.bar(v)
            inst = v.scalar_tensor_tensor(scr8b[:], lgt[:], -64.0, eaff[:],
                                          ALU.mult, ALU.add)
            S.done(inst, "V")
        ve(_)

        def _(v):
            S.bar(v)
            inst = v.tensor_reduce(i2_t[:], S3B, AX.X, ALU.min)
            S.done(inst, "V")
        ve(_)

        def _(v):
            S.bar(v)
            inst = v.tensor_add(red1[:], u1_t[:], u2_t[:])
            S.done(inst, "V")
        ve(_)

        def _(v):
            S.bar(v)
            inst = v.scalar_tensor_tensor(red1[:], s_t[:], REPS, red1[:],
                                          ALU.mult, ALU.add)
            S.done(inst, "V")
        ve(_)

        def _(s):
            S.bar(s)
            inst = s.activation(out=red1[:], in_=red1[:], func=AF.Ln)
            S.done(inst, "A")
        sc(_)

        def _(s):
            S.bar(s)
            inst = s.activation(out=red1[:], in_=red1[:], func=AF.Exp, scale=-1.0)
            S.done(inst, "A")
        sc(_)

        def _(v):
            S.bar(v)
            v.tensor_mul(w1_t[:], u1_t[:], red1[:])
            inst = v.tensor_mul(w2_t[:], u2_t[:], red1[:])
            S.done(inst, "V")
        ve(_)

        def _(v):
            S.bar(v)
            v.tensor_single_scalar(va1_t[:], w1_t[:], REPS, ALU.is_gt)
            inst = v.tensor_single_scalar(va2_t[:], w2_t[:], REPS, ALU.is_gt)
            S.done(inst, "V")
        ve(_)

        def _(v):
            S.bar(v)
            v.tensor_tensor(S3, E3, bc(i1_t), ALU.is_equal)
            v.tensor_tensor(S3B, E3, bc(i2_t), ALU.is_equal)
            v.tensor_tensor(lgt[:].rearrange("p (e c) -> p e c", e=8), EM,
                            bcM(i1_t), ALU.is_equal)
            inst = v.tensor_tensor(u_t[:].rearrange("p (e c) -> p e c", e=8), EM,
                                   bcM(i2_t), ALU.is_equal)
            S.done(inst, "V")
        ve(_)

        def _(v):
            S.bar(v)
            v.tensor_tensor(oh1[:].rearrange("p (c e) -> p c e", e=8), S3,
                            bc(va1_t), ALU.mult)
            v.tensor_tensor(oh2[:].rearrange("p (c e) -> p c e", e=8), S3B,
                            bc(va2_t), ALU.mult)
            v.tensor_tensor(ohM1[:].rearrange("p (e c) -> p e c", e=8),
                            lgt[:].rearrange("p (e c) -> p e c", e=8),
                            bcM(va1_t), ALU.mult)
            inst = v.tensor_tensor(ohM2[:].rearrange("p (e c) -> p e c", e=8),
                                   u_t[:].rearrange("p (e c) -> p e c", e=8),
                                   bcM(va2_t), ALU.mult)
            S.done(inst, "V")
        ve(_)

        def _(t):
            S.bar(t)
            t.matmul(P0[:, :], lt_s[:], oh1[:], start=True, stop=True)
            t.matmul(P1[:, :], lt_s[:], oh2[:], start=True, stop=True)
            t.matmul(P2[0:1, :], ones_c[:], ohM1[:], start=True, stop=True)
            inst = t.matmul(P3[0:1, :], ones_c[:], ohM2[:], start=True, stop=True)
            S.done(inst, "T")
        tn(_)

        def _(v):
            S.bar(v)
            v.tensor_copy(totb[0:1, 0:512], P2[0:1, :])
            inst = v.tensor_copy(totb[0:1, 512:1024], P3[0:1, :])
            S.done(inst, "V")
        ve(_)

        def _(v):
            S.bar(v)
            for k in range(2):
                tv = totb[0:1, k * 512:(k + 1) * 512].rearrange("o (e c) -> o e c", e=8)
                inst = v.tensor_reduce(cntrow[0:1, k * 8:(k + 1) * 8], tv, AX.X, ALU.add)
            S.done(inst, "V")
        ve(_)

        def _(t):
            S.bar(t)
            t.matmul(P2[:, :], onesf_r[:], totb[0:1, 0:512], start=True, stop=True)
            inst = t.matmul(P3[:, :], onesf_r[:], totb[0:1, 512:1024], start=True, stop=True)
            S.done(inst, "T")
        tn(_)

        def _(v):
            S.bar(v)
            v.tensor_copy(totb[:, 0:512], P2[:, :])
            inst = v.tensor_copy(totb[:, 512:1024], P3[:, :])
            S.done(inst, "V")
        ve(_)

        def _(v):
            S.bar(v)
            for k in range(2):
                for ee in range(8):
                    sl = slice(k * 512 + ee * 64, k * 512 + (ee + 1) * 64)
                    inst = v.tensor_tensor_scan(tscab[:, sl], onescol_b[:], totb[:, sl],
                                                0.0, ALU.mult, ALU.add)
            S.done(inst, "V")
        ve(_)

        def _(v):
            S.bar(v)
            inst = v.tensor_sub(tscab[:], tscab[:], totb[:])
            S.done(inst, "V")
        ve(_)

        for k in range(2):
            def _(v, k=k):
                S.bar(v)
                dst = pos1 if k == 0 else pos2
                psrc = P0 if k == 0 else P1
                exv = tscab[:, k * 512:(k + 1) * 512].rearrange("p (e c) -> p c e", e=8)
                inst = v.tensor_tensor(dst[:].rearrange("p (c e) -> p c e", e=8),
                                       psrc[:, :].rearrange("p (c e) -> p c e", e=8),
                                       exv, ALU.add)
                S.done(inst, "V")
            ve(_)

        def _(v):
            S.bar(v)
            v.tensor_tensor(S3, pos1[:].rearrange("p (c e) -> p c e", e=8),
                            oh1[:].rearrange("p (c e) -> p c e", e=8), ALU.mult)
            inst = v.tensor_tensor(S3B, pos2[:].rearrange("p (c e) -> p c e", e=8),
                                   oh2[:].rearrange("p (c e) -> p c e", e=8), ALU.mult)
            S.done(inst, "V")
        ve(_)

        def _(v):
            S.bar(v)
            v.tensor_reduce(po1[:], S3, AX.X, ALU.add)
            inst = v.tensor_reduce(po2[:], S3B, AX.X, ALU.add)
            S.done(inst, "V")
        ve(_)

        def _(v):
            S.bar(v)
            m3 = meta_s[:].rearrange("p (c q) -> p c q", q=8)
            for qix, src in enumerate([i1_t, i2_t, po1, po2, va1_t, va2_t, w1_t, w2_t]):
                v.tensor_copy(m3[:, :, qix:qix + 1],
                              src[:].rearrange("p (c u) -> p c u", u=1))
            v.tensor_scalar(out=red1[:], in0=i1_t[:], scalar1=8192.0,
                            scalar2=0.0, op0=ALU.mult, op1=ALU.add)
            v.tensor_scalar(out=mx_t[:], in0=i2_t[:], scalar1=8192.0,
                            scalar2=65536.0, op0=ALU.mult, op1=ALU.add)
            v.tensor_scalar(out=s_t[:], in0=va1_t[:], scalar1=-131200.0,
                            scalar2=131200.0, op0=ALU.mult, op1=ALU.add)
            inst = v.tensor_scalar(out=u1_t[:], in0=va2_t[:], scalar1=-131200.0,
                                   scalar2=131200.0, op0=ALU.mult, op1=ALU.add)
            S.done(inst, "V")
        ve(_)

        def _(v):
            S.bar(v)
            v.tensor_add(red1[:], red1[:], po1[:])
            inst = v.tensor_add(mx_t[:], mx_t[:], po2[:])
            S.done(inst, "V")
        ve(_)

        def _(v):
            S.bar(v)
            v.tensor_mul(red1[:], red1[:], va1_t[:])
            inst = v.tensor_mul(mx_t[:], mx_t[:], va2_t[:])
            S.done(inst, "V")
        ve(_)

        def _(v):
            S.bar(v)
            v.tensor_add(red1[:], red1[:], s_t[:])
            inst = v.tensor_add(mx_t[:], mx_t[:], u1_t[:])
            S.done(inst, "V")
        ve(_)

        def _(v):
            S.bar(v)
            v.tensor_copy(offi[:, 0:64], red1[:])
            v.tensor_copy(offi[:, 64:128], mx_t[:])
            s20 = scv[:, 0:128].rearrange("p (c u) -> p c u", u=2)
            s21 = scv[:, 128:256].rearrange("p (c u) -> p c u", u=2)
            v.tensor_copy(s20[:, :, 0:1], tokid_s[:].rearrange("p (c u) -> p c u", u=1))
            v.tensor_copy(s21[:, :, 0:1], tokid_s[:].rearrange("p (c u) -> p c u", u=1))
            v.tensor_copy(s20[:, :, 1:2], w1_t[:].rearrange("p (c u) -> p c u", u=1))
            inst = v.tensor_copy(s21[:, :, 1:2], w2_t[:].rearrange("p (c u) -> p c u", u=1))
            S.done(inst, "V")
        ve(_)

        def _(sync):
            S.bar(sync)
            sync.dma_start(out=meta[:, :], in_=meta_s[:]).then_inc(S.sem["O"], 16)
            sync.dma_start(out=counts[:, :], in_=cntrow[:]).then_inc(S.sem["O"], 16)
            S.cnt["O"] += 32
        sy(_)

        def _(g):
            S.bar(g)
            for k in range(2):
                for ch in range(64):
                    inst = g.indirect_dma_start(
                        slotmap[:, :],
                        IndirectOffsetOnAxis(offi[:, k * 64 + ch:k * 64 + ch + 1], 0),
                        scv[:, k * 128 + 2 * ch:k * 128 + 2 * ch + 2], None)
                    inst.then_inc(S.sem["O"], 16)
            S.cnt["O"] += 128 * 16
        gp(_)

        def _(sync):
            S.bar(sync)
            sync.nop()
        sy(_)
    return nc



def build_L2():
    nc = bass.Bass("TRN2", target_bir_lowering=False, debug=False)
    SMALL = 8 * SLOTROWS + 16384
    xn2a = nc.dram_tensor("xn2a", [N_TOK, DM], BF16, kind="ExternalInput")
    smap = nc.dram_tensor("smap", [SMALL, 2], F32, kind="ExternalInput")
    cnta = nc.dram_tensor("cnta", [1, 8 * 16], F32, kind="ExternalInput")
    w1t = nc.dram_tensor("w1t", [DM, HID], F32, kind="ExternalInput")
    w2t = nc.dram_tensor("w2t", [HID, DM], F32, kind="ExternalInput")
    b1c = nc.dram_tensor("b1c", [128, 32], F32, kind="ExternalInput")
    lsr = nc.dram_tensor("lsr", [1, DM], F32, kind="ExternalInput")
    b2r = nc.dram_tensor("b2r", [1, DM], F32, kind="ExternalInput")
    c_gpre = nc.dram_tensor("c_gpre", [16, 16], F32, kind="ExternalInput")
    c_ehot = nc.dram_tensor("c_ehot", [1, 8], F32, kind="ExternalInput")
    c_base = nc.dram_tensor("c_base", [1, 16], F32, kind="ExternalInput")
    c_idb = nc.dram_tensor("c_idb", [128, 128], BF16, kind="ExternalInput")
    slotres = nc.dram_tensor("slotres", [CPAD, DM], BF16, kind="ExternalOutput")

    with ExitStack() as ctx:
        e = ctx.enter_context
        w1_b = e(nc.sbuf_tensor("w1_b", [128, 8 * 4096], BF16))
        w2_b = e(nc.sbuf_tensor("w2_b", [128, 32 * 1024], BF16))
        h_b = e(nc.sbuf_tensor("h_b", [128, 32 * 512], BF16))
        toks = e(nc.sbuf_tensor("toks", [128, 4096], BF16))
        tfm = e(nc.sbuf_tensor("tfm", [128, 8 * 512], BF16))
        sres = e(nc.sbuf_tensor("sres", [128, 4 * 1024], BF16))
        idb_s = e(nc.sbuf_tensor("idb_s", [128, 128], BF16))
        b1_s = e(nc.sbuf_tensor("b1_s", [128, 32], F32))
        lsb = e(nc.sbuf_tensor("lsb", [128, 1024], F32))
        b2lsb = e(nc.sbuf_tensor("b2lsb", [1, 1024], BF16))
        lsr_s = e(nc.sbuf_tensor("lsr_s", [1, 1024], F32))
        b2_s = e(nc.sbuf_tensor("b2_s", [1, 1024], F32))
        onesr = e(nc.sbuf_tensor("onesr", [1, 128], BF16))
        onesf_r = e(nc.sbuf_tensor("onesf_r", [1, 128], F32))
        onesf16 = e(nc.sbuf_tensor("onesf16", [1, 16], F32))
        onesc16 = e(nc.sbuf_tensor("onesc16", [16, 1], F32))
        tvrow = e(nc.sbuf_tensor("tvrow", [1, 1], F32))
        cnt_sb = e(nc.sbuf_tensor("cnt_sb", [16, 8], F32))
        gpre_sb = e(nc.sbuf_tensor("gpre_sb", [16, 16], F32))
        ehot_sb = e(nc.sbuf_tensor("ehot_sb", [16, 8], F32))
        base_sb = e(nc.sbuf_tensor("base_sb", [1, 16], F32))
        gch_sb = e(nc.sbuf_tensor("gch_sb", [16, 1], F32))
        ccol_sb = e(nc.sbuf_tensor("ccol_sb", [16, 1], F32))
        grow_sb = e(nc.sbuf_tensor("grow_sb", [1, 16], F32))
        gb_sb = e(nc.sbuf_tensor("gb_sb", [128, 16], F32))
        bb_sb = e(nc.sbuf_tensor("bb_sb", [128, 16], F32))
        jpat = e(nc.sbuf_tensor("jpat", [128, 16], F32))
        tv_sb = e(nc.sbuf_tensor("tv_sb", [128, 1], F32))
        slots = e(nc.sbuf_tensor("slots", [128, 80], F32))
        bsel = e(nc.sbuf_tensor("bsel", [128, 40 * 16], F32))
        gsel = e(nc.sbuf_tensor("gsel", [128, 80], F32))
        srcr = e(nc.sbuf_tensor("srcr", [128, 80], F32))
        wmask = e(nc.sbuf_tensor("wmask", [128, 80], F32))
        srci = e(nc.sbuf_tensor("srci", [128, 80], I32))
        tokw = e(nc.sbuf_tensor("tokw", [128, 80 * 2], F32))
        tokf = e(nc.sbuf_tensor("tokf", [128, 80], F32))
        wf = e(nc.sbuf_tensor("wf", [128, 80], F32))
        toki = e(nc.sbuf_tensor("toki", [128, 80], I32))

        P0 = e(nc.psum_tensor("P0", [128, 512], F32))
        P1 = e(nc.psum_tensor("P1", [128, 512], F32))
        P2 = e(nc.psum_tensor("P2", [128, 512], F32))
        P3 = e(nc.psum_tensor("P3", [128, 512], F32))
        PT = e(nc.psum_tensor("PT", [128, 1024], BF16))
        PS = e(nc.psum_tensor("PS", [16, 16], F32))
        PS3 = e(nc.psum_tensor("PS3", [1, 16], F32))

        block = e(nc.Block())
        S = K(nc, ctx, block)
        gscr = nc.dram_tensor("gscr", [16, 1], F32, kind="Internal")

        def _(sync):
            sync.dma_start(out=cnt_sb[:], in_=cnta.rearrange("u (r c) -> (u r) c", c=8)[:, :]
                           ).then_inc(S.sem["Q"], 16)
            sync.dma_start(out=gpre_sb[:], in_=c_gpre[:, :]).then_inc(S.sem["Q"], 16)
            sync.dma_start(out=ehot_sb[0:1, :], in_=c_ehot[:, :]).then_inc(S.sem["Q"], 16)
            sync.dma_start(out=base_sb[:], in_=c_base[:, :]).then_inc(S.sem["Q"], 16)
            sync.dma_start(out=idb_s[:], in_=c_idb[:, :]).then_inc(S.sem["Q"], 16)
            sync.dma_start(out=b1_s[:], in_=b1c[:, :]).then_inc(S.sem["Q"], 16)
            sync.dma_start(out=lsr_s[:], in_=lsr[:, :]).then_inc(S.sem["Q"], 16)
            sync.dma_start(out=b2_s[:], in_=b2r[:, :]).then_inc(S.sem["Q"], 16)
            S.cnt["Q"] += 8 * 16
        block.sync(_)

        def _(g):
            for ci in range(8):
                g.dma_start(out=w1_b[:, ci * 4096:(ci + 1) * 4096],
                            in_=w1t[ci * 128:(ci + 1) * 128, :]).then_inc(S.sem["P"], 16)
            for hi in range(32):
                g.dma_start(out=w2_b[:, hi * 1024:(hi + 1) * 1024],
                            in_=w2t[hi * 128:(hi + 1) * 128, :]).then_inc(S.sem["P"], 16)
            S.cnt["P"] += 40 * 16
            g.iota(slots[:], [[128, 80]], base=0, channel_multiplier=1,
                   allow_small_or_imprecise_dtypes=True).then_inc(S.sem["G"], 1)
            g.iota(jpat[:], [[0, 1], [1, 16]], base=0, channel_multiplier=0,
                   allow_small_or_imprecise_dtypes=True).then_inc(S.sem["G"], 1)
            S.cnt["G"] += 2
        block.gpsimd(_)

        def _(v):
            S.bar(v)
            v.memset(onesr[:], 1.0)
            v.memset(onesf_r[:], 1.0)
            v.memset(onesf16[:], 1.0)
            v.memset(onesc16[:], 1.0)
            inst = v.tensor_mul(b2_s[:], b2_s[:], lsr_s[:])
            S.done(inst, "V")
        block.vector(_)

        def _(v):
            S.bar(v)
            inst = v.tensor_copy(b2lsb[:], b2_s[:])
            S.done(inst, "V")
        block.vector(_)

        def _(t):
            S.bar(t)
            t.matmul(P0[:, :], onesf_r[:], lsr_s[0:1, 0:512], start=True, stop=True)
            t.matmul(P1[:, :], onesf_r[:], lsr_s[0:1, 512:1024], start=True, stop=True)
            inst = t.matmul(PS[0:16, 0:8], onesf16[:], ehot_sb[0:1, :], start=True, stop=True)
            S.done(inst, "T")
        block.tensor(_)

        def _(v):
            S.bar(v)
            v.tensor_copy(lsb[:, 0:512], P0[:, :])
            v.tensor_copy(lsb[:, 512:1024], P1[:, :])
            inst = v.tensor_copy(ehot_sb[:, :], PS[0:16, 0:8])
            S.done(inst, "V")
        block.vector(_)

        def _(v):
            S.bar(v)
            for hi in range(32):
                inst = v.tensor_mul(w2_b[:, hi * 1024:(hi + 1) * 1024],
                                    w2_b[:, hi * 1024:(hi + 1) * 1024], lsb[:])
            S.done(inst, "V")
        block.vector(_)

        def _(t):
            S.bar(t)
            inst = t.matmul(PS[:, 0:8], gpre_sb[:], cnt_sb[:], start=True, stop=True)
            S.done(inst, "T")
        block.tensor(_)

        def _(v):
            S.bar(v)
            v.tensor_tensor(bsel[0:16, 0:8], PS[:, 0:8], ehot_sb[:, :], ALU.mult)
            inst = v.tensor_tensor(bsel[0:16, 8:16], cnt_sb[:], ehot_sb[:, :], ALU.mult)
            S.done(inst, "V")
        block.vector(_)

        def _(v):
            S.bar(v)
            v.tensor_reduce(gch_sb[:], bsel[0:16, 0:8].rearrange("p (u e) -> p u e", u=1),
                            AX.X, ALU.add)
            inst = v.tensor_reduce(ccol_sb[:], bsel[0:16, 8:16].rearrange("p (u e) -> p u e", u=1),
                                   AX.X, ALU.add)
            S.done(inst, "V")
        block.vector(_)

        def _(t):
            S.bar(t)
            inst = t.matmul(PS3[0:1, 0:1], onesc16[:], ccol_sb[:], start=True, stop=True)
            S.done(inst, "T")
        block.tensor(_)

        def _(v):
            S.bar(v)
            inst = v.tensor_copy(tvrow[:], PS3[0:1, 0:1])
            S.done(inst, "V")
        block.vector(_)

        def _(t):
            S.bar(t)
            inst = t.matmul(P0[:, 0:1], onesf_r[:], tvrow[0:1, :], start=True, stop=True)
            S.done(inst, "T")
        block.tensor(_)

        def _(v):
            S.bar(v)
            inst = v.tensor_copy(tv_sb[:], P0[:, 0:1])
            S.done(inst, "V")
        block.vector(_)

        def _(sync):
            S.bar(sync)
            sync.dma_start(out=gscr[:, :], in_=gch_sb[:]).then_inc(S.sem["Q"], 16)
            S.cnt["Q"] += 16
        block.sync(_)

        def _(sync):
            S.bar(sync)
            inst = sync.dma_start(out=grow_sb[:], in_=gscr.rearrange("r u -> u r")[:, :],
                                  allow_slow_non_contiguous=True)
            inst.then_inc(S.sem["Q"], 16)
            S.cnt["Q"] += 16
        block.sync(_)

        def _(t):
            S.bar(t)
            t.matmul(P0[:, 0:16], onesf_r[:], grow_sb[0:1, :], start=True, stop=True)
            inst = t.matmul(P1[:, 0:16], onesf_r[:], base_sb[0:1, :], start=True, stop=True)
            S.done(inst, "T")
        block.tensor(_)

        def _(v):
            S.bar(v)
            v.tensor_copy(gb_sb[:], P0[:, 0:16])
            inst = v.tensor_copy(bb_sb[:], P1[:, 0:16])
            S.done(inst, "V")
        block.vector(_)

        for hh in range(2):
            cs = slice(hh * 40, (hh + 1) * 40)
            b3 = bsel[:].rearrange("p (c j) -> p c j", j=16)
            sl3 = slots[:, cs].rearrange("p (c u) -> p c u", u=1).broadcast_to([128, 40, 16])
            g3 = gb_sb[:].rearrange("p (u j) -> p u j", u=1).broadcast_to([128, 40, 16])
            j3 = jpat[:].rearrange("p (u j) -> p u j", u=1).broadcast_to([128, 40, 16])
            i3 = srcr[:, cs].rearrange("p (c u) -> p c u", u=1).broadcast_to([128, 40, 16])
            bb3 = bb_sb[:].rearrange("p (u j) -> p u j", u=1).broadcast_to([128, 40, 16])

            def _(v, b3=b3, sl3=sl3, g3=g3):
                S.bar(v)
                inst = v.tensor_tensor(b3, sl3, g3, ALU.is_ge)
                S.done(inst, "V")
            block.vector(_)

            def _(v, b3=b3, cs=cs):
                S.bar(v)
                inst = v.tensor_reduce(srcr[:, cs], b3, AX.X, ALU.add)
                S.done(inst, "V")
            block.vector(_)

            def _(v, b3=b3, g3=g3, cs=cs):
                S.bar(v)
                v.tensor_scalar_add(srcr[:, cs], srcr[:, cs], -1.0)
                inst = v.tensor_tensor(b3, b3, g3, ALU.mult)
                S.done(inst, "V")
            block.vector(_)

            def _(v, b3=b3, j3=j3, i3=i3, cs=cs):
                S.bar(v)
                v.tensor_reduce(gsel[:, cs], b3, AX.X, ALU.max)
                inst = v.tensor_tensor(b3, j3, i3, ALU.is_equal)
                S.done(inst, "V")
            block.vector(_)

            def _(v, b3=b3, bb3=bb3):
                S.bar(v)
                inst = v.tensor_tensor(b3, b3, bb3, ALU.mult)
                S.done(inst, "V")
            block.vector(_)

            def _(v, b3=b3, cs=cs):
                S.bar(v)
                inst = v.tensor_reduce(wmask[:, cs], b3, AX.X, ALU.add)
                S.done(inst, "V")
            block.vector(_)

        def _(v):
            S.bar(v)
            inst = v.tensor_sub(gsel[:], slots[:], gsel[:])
            S.done(inst, "V")
        block.vector(_)

        def _(v):
            S.bar(v)
            inst = v.tensor_add(srcr[:], wmask[:], gsel[:])
            S.done(inst, "V")
        block.vector(_)

        def _(v):
            S.bar(v)
            v.tensor_scalar(out=wmask[:], in0=slots[:], scalar1=tv_sb[:, 0:1],
                            scalar2=None, op0=ALU.is_lt)
            inst = v.tensor_single_scalar(tokf[:], slots[:], float(CAPACITY), ALU.is_lt)
            S.done(inst, "V")
        block.vector(_)

        def _(v):
            S.bar(v)
            inst = v.tensor_mul(wmask[:], wmask[:], tokf[:])
            S.done(inst, "V")
        block.vector(_)

        def _(v):
            S.bar(v)
            inst = v.tensor_mul(srcr[:], srcr[:], wmask[:])
            S.done(inst, "V")
        block.vector(_)

        def _(v):
            S.bar(v)
            inst = v.tensor_copy(srci[:], srcr[:])
            S.done(inst, "V")
        block.vector(_)

        def _(g):
            S.bar(g)
            for ch in range(80):
                inst = g.indirect_dma_start(
                    tokw[:, 2 * ch:2 * ch + 2], None,
                    smap[:, :], IndirectOffsetOnAxis(srci[:, ch:ch + 1], 0))
                inst.then_inc(S.sem["P"], 16)
            S.cnt["P"] += 80 * 16
        block.gpsimd(_)

        def _(v):
            S.bar(v)
            t3 = tokw[:].rearrange("p (c u) -> p c u", u=2)
            w3 = wmask[:].rearrange("p (c u) -> p c u", u=1)
            v.tensor_tensor(tokf[:].rearrange("p (c u) -> p c u", u=1), t3[:, :, 0:1], w3, ALU.mult)
            inst = v.tensor_tensor(wf[:].rearrange("p (c u) -> p c u", u=1), t3[:, :, 1:2], w3, ALU.mult)
            S.done(inst, "V")
        block.vector(_)

        def _(v):
            S.bar(v)
            inst = v.tensor_copy(toki[:], tokf[:])
            S.done(inst, "V")
        block.vector(_)

        for grp in range(20):
            def _(g, grp=grp):
                S.bar(g)
                for j in range(4):
                    inst = g.indirect_dma_start(
                        toks[:, j * 1024:(j + 1) * 1024], None,
                        xn2a[:, :],
                        IndirectOffsetOnAxis(toki[:, grp * 4 + j:grp * 4 + j + 1], 0))
                    inst.then_inc(S.sem["P"], 16)
                S.cnt["P"] += 4 * 16
            block.gpsimd(_)

            for j in range(4):
                def _(t, grp=grp, j=j):
                    S.bar(t)
                    for ci in range(8):
                        inst = t.transpose(PT[:, ci * 128:(ci + 1) * 128],
                                           toks[:, j * 1024 + ci * 128:j * 1024 + (ci + 1) * 128],
                                           idb_s[:])
                    S.done(inst, "T")
                block.tensor(_)

                def _(s, grp=grp, j=j):
                    S.bar(s)
                    for ci in range(8):
                        inst = s.copy(tfm[:, ci * 512 + j * 128:ci * 512 + (j + 1) * 128],
                                      PT[:, ci * 128:(ci + 1) * 128])
                    S.done(inst, "A")
                block.scalar(_)

            for hi in range(32):
                def _(t, grp=grp, hi=hi):
                    S.bar(t)
                    p = P0 if hi % 2 == 0 else P1
                    for ci in range(8):
                        inst = t.matmul(p[:, :], w1_b[:, ci * 4096 + hi * 128:
                                                      ci * 4096 + (hi + 1) * 128],
                                        tfm[:, ci * 512:(ci + 1) * 512],
                                        start=(ci == 0), stop=(ci == 7))
                    S.done(inst, "T")
                block.tensor(_)

                def _(s, grp=grp, hi=hi):
                    S.bar(s)
                    p = P0 if hi % 2 == 0 else P1
                    inst = s.activation(out=h_b[:, hi * 512:(hi + 1) * 512], in_=p[:, :],
                                        func=AF.Gelu, bias=b1_s[:, hi:hi + 1])
                    S.done(inst, "A")
                block.scalar(_)

            for j in range(4):
                for hf in range(2):
                    def _(t, grp=grp, j=j, hf=hf):
                        S.bar(t)
                        p = P2 if (j * 2 + hf) % 2 == 0 else P3
                        for hi in range(32):
                            t.matmul(p[:, :],
                                     h_b[:, hi * 512 + j * 128:hi * 512 + (j + 1) * 128],
                                     w2_b[:, hi * 1024 + hf * 512:hi * 1024 + (hf + 1) * 512],
                                     start=(hi == 0), stop=False)
                        inst = t.matmul(p[:, :], onesr[:],
                                        b2lsb[0:1, hf * 512:(hf + 1) * 512],
                                        start=False, stop=True)
                        S.done(inst, "T")
                    block.tensor(_)

                    def _(v, grp=grp, j=j, hf=hf):
                        S.bar(v)
                        p = P2 if (j * 2 + hf) % 2 == 0 else P3
                        sl = slice(j * 1024 + hf * 512, j * 1024 + (hf + 1) * 512)
                        inst = v.tensor_add(bsel[:, 0:512], p[:, :], toks[:, sl])
                        S.done(inst, "V")
                    block.vector(_)

                    def _(s, grp=grp, j=j, hf=hf):
                        S.bar(s)
                        sl = slice(j * 1024 + hf * 512, j * 1024 + (hf + 1) * 512)
                        inst = s.activation(out=sres[:, sl], in_=bsel[:, 0:512],
                                            func=AF.Identity,
                                            scale=wf[:, grp * 4 + j:grp * 4 + j + 1])
                        S.done(inst, "A")
                    block.scalar(_)

            def _(sync, grp=grp):
                S.bar(sync)
                for j in range(4):
                    r0 = grp * 512 + j * 128
                    inst = sync.dma_start(out=slotres[r0:r0 + 128, :],
                                          in_=sres[:, j * 1024:(j + 1) * 1024])
                    inst.then_inc(S.sem["O"], 16)
                S.cnt["O"] += 4 * 16
            block.sync(_)

        def _(sync):
            S.bar(sync)
            sync.nop()
        block.sync(_)
    return nc



def build_L3():
    nc = bass.Bass("TRN2", target_bir_lowering=False, debug=False)
    sresd = nc.dram_tensor("sresd", [8 * CPAD + 128, DM], BF16, kind="ExternalInput")
    x2t = nc.dram_tensor("x2t", [TPC, DM], BF16, kind="ExternalInput")
    metai = nc.dram_tensor("metai", [128, 512], F32, kind="ExternalInput")
    cnta = nc.dram_tensor("cnta", [1, 8 * 16], F32, kind="ExternalInput")
    s16own = nc.dram_tensor("s16own", [16, 2], F32, kind="ExternalInput")
    yout = nc.dram_tensor("yout", [TPC, DM], F32, kind="ExternalOutput")

    with ExitStack() as ctx:
        e = ctx.enter_context
        meta_s = e(nc.sbuf_tensor("meta_s", [128, 512], F32))
        cnt_sb = e(nc.sbuf_tensor("cnt_sb", [16, 8], F32))
        s16_sb = e(nc.sbuf_tensor("s16_sb", [16, 2], F32))
        g1row = e(nc.sbuf_tensor("g1row", [1, 8], F32))
        g2row = e(nc.sbuf_tensor("g2row", [1, 8], F32))
        gb1 = e(nc.sbuf_tensor("gb1", [128, 8], F32))
        gb2 = e(nc.sbuf_tensor("gb2", [128, 8], F32))
        onesf_r = e(nc.sbuf_tensor("onesf_r", [1, 128], F32))
        PB = e(nc.psum_tensor("PB", [128, 16], F32))
        epat = e(nc.sbuf_tensor("epat", [128, 512], F32))
        scr = e(nc.sbuf_tensor("scr", [128, 512], F32))
        scrb = e(nc.sbuf_tensor("scrb", [128, 512], F32))
        keep = e(nc.sbuf_tensor("keep", [128, 64], F32))
        off1 = e(nc.sbuf_tensor("off1", [128, 64], F32))
        off2 = e(nc.sbuf_tensor("off2", [128, 64], F32))
        oi1 = e(nc.sbuf_tensor("oi1", [128, 64], I32))
        oi2 = e(nc.sbuf_tensor("oi2", [128, 64], I32))
        gt = e(nc.sbuf_tensor("gt", [128, 1024], BF16))
        x2tile = e(nc.sbuf_tensor("x2tile", [128, 1024], BF16))
        ytile = e(nc.sbuf_tensor("ytile", [128, 1024], F32))
        PS = e(nc.psum_tensor("PS", [1, 16], F32))

        block = e(nc.Block())
        S = K(nc, ctx, block)

        def _(sync):
            sync.dma_start(out=meta_s[:], in_=metai[:, :]).then_inc(S.sem["Q"], 16)
            sync.dma_start(out=cnt_sb[:], in_=cnta.rearrange("u (r c) -> (u r) c", c=8)[:, :]
                           ).then_inc(S.sem["Q"], 16)
            sync.dma_start(out=s16_sb[:], in_=s16own[:, :]).then_inc(S.sem["Q"], 16)
            S.cnt["Q"] += 3 * 16
        block.sync(_)

        def _(g):
            inst = g.iota(epat[:], [[0, 64], [1, 8]], base=0, channel_multiplier=0,
                          allow_small_or_imprecise_dtypes=True)
            inst.then_inc(S.sem["G"], 1)
            S.cnt["G"] += 1
        block.gpsimd(_)

        def _(t):
            S.bar(t)
            inst = t.matmul(PS[0:1, 0:8], s16_sb[:, 0:1], cnt_sb[:], start=True, stop=True)
            S.done(inst, "T")
        block.tensor(_)

        def _(v):
            S.bar(v)
            inst = v.tensor_copy(g1row[:], PS[0:1, 0:8])
            S.done(inst, "V")
        block.vector(_)

        def _(t):
            S.bar(t)
            inst = t.matmul(PS[0:1, 0:8], s16_sb[:, 1:2], cnt_sb[:], start=True, stop=True)
            S.done(inst, "T")
        block.tensor(_)

        def _(v):
            S.bar(v)
            v.memset(onesf_r[:], 1.0)
            inst = v.tensor_copy(g2row[:], PS[0:1, 0:8])
            S.done(inst, "V")
        block.vector(_)

        def _(t):
            S.bar(t)
            t.matmul(PB[:, 0:8], onesf_r[:], g1row[0:1, :], start=True, stop=False)
            inst = t.matmul(PB[:, 8:16], onesf_r[:], g2row[0:1, :], start=False, stop=True)
            S.done(inst, "T")
        block.tensor(_)

        def _(v):
            S.bar(v)
            v.tensor_copy(gb1[:], PB[:, 0:8])
            inst = v.tensor_copy(gb2[:], PB[:, 8:16])
            S.done(inst, "V")
        block.vector(_)

        for k in range(2):
            m3 = meta_s[:].rearrange("p (c q) -> p c q", q=8)
            e3 = epat[:].rearrange("p (c e) -> p c e", e=8)
            s3 = scr[:].rearrange("p (c e) -> p c e", e=8)
            s3b = scrb[:].rearrange("p (c e) -> p c e", e=8)
            idx = m3[:, :, k:k + 1]
            po = m3[:, :, 2 + k:3 + k]
            va = m3[:, :, 4 + k:5 + k]
            gb = gb1 if k == 0 else gb2
            off = off1 if k == 0 else off2
            oi = oi1 if k == 0 else oi2
            o3 = off[:].rearrange("p (c u) -> p c u", u=1)
            k3 = keep[:].rearrange("p (c u) -> p c u", u=1)

            def _(v, s3=s3, e3=e3, idx=idx):
                S.bar(v)
                inst = v.tensor_tensor(s3, e3, idx.broadcast_to([128, 64, 8]),
                                       ALU.is_equal)
                S.done(inst, "V")
            block.vector(_)

            def _(v, s3=s3, s3b=s3b, gb=gb):
                S.bar(v)
                inst = v.tensor_tensor(s3b, s3,
                                       gb[:].rearrange("p (u e) -> p u e", u=1)
                                       .broadcast_to([128, 64, 8]), ALU.mult)
                S.done(inst, "V")
            block.vector(_)

            def _(v, off=off, s3b=s3b):
                S.bar(v)
                inst = v.tensor_reduce(off[:], s3b, AX.X, ALU.add)
                S.done(inst, "V")
            block.vector(_)

            def _(v, o3=o3, po=po, idx=idx):
                S.bar(v)
                v.tensor_tensor(o3, o3, po, ALU.add)
                inst = v.tensor_scalar(
                    out=scrb[:, 0:64].rearrange("p (c u) -> p c u", u=1), in0=idx,
                    scalar1=float(CPAD), scalar2=None, op0=ALU.mult)
                S.done(inst, "V")
            block.vector(_)

            def _(v, off=off):
                S.bar(v)
                v.tensor_single_scalar(keep[:], off[:], float(CAPACITY), ALU.is_lt)
                inst = v.tensor_add(off[:], off[:], scrb[:, 0:64])
                S.done(inst, "V")
            block.vector(_)

            def _(v, k3=k3, va=va):
                S.bar(v)
                inst = v.tensor_tensor(k3, k3, va, ALU.mult)
                S.done(inst, "V")
            block.vector(_)

            def _(v, off=off):
                S.bar(v)
                v.tensor_mul(off[:], off[:], keep[:])
                inst = v.tensor_scalar(out=scrb[:, 64:128], in0=keep[:],
                                       scalar1=-float(8 * CPAD), scalar2=float(8 * CPAD),
                                       op0=ALU.mult, op1=ALU.add)
                S.done(inst, "V")
            block.vector(_)

            def _(v, off=off):
                S.bar(v)
                inst = v.tensor_add(off[:], off[:], scrb[:, 64:128])
                S.done(inst, "V")
            block.vector(_)

            def _(v, oi=oi, off=off):
                S.bar(v)
                inst = v.tensor_copy(oi[:], off[:])
                S.done(inst, "V")
            block.vector(_)

        for ch in range(64):
            def _(g, ch=ch):
                S.bar(g)
                g.indirect_dma_start(
                    gt[:, :], None,
                    sresd[:, :], IndirectOffsetOnAxis(oi1[:, ch:ch + 1], 0)
                ).then_inc(S.sem["P"], 16)
                S.cnt["P"] += 16
                g.wait_ge(S.sem["P"], S.cnt["P"])
                inst = g.indirect_dma_start(
                    gt[:, :], None,
                    sresd[:, :], IndirectOffsetOnAxis(oi2[:, ch:ch + 1], 0),
                    compute_op=ALU.add)
                inst.then_inc(S.sem["P"], 16)
                S.cnt["P"] += 16
            block.gpsimd(_)

            def _(sync, ch=ch):
                S.bar(sync, skip=("P",))
                inst = sync.dma_start(out=x2tile[:], in_=x2t[ch * 128:(ch + 1) * 128, :])
                inst.then_inc(S.sem["Q"], 16)
                S.cnt["Q"] += 16
            block.sync(_)

            def _(v, ch=ch):
                S.bar(v)
                inst = v.tensor_add(ytile[:], gt[:], x2tile[:])
                S.done(inst, "V")
            block.vector(_)

            def _(sync, ch=ch):
                S.bar(sync)
                inst = sync.dma_start(out=yout[ch * 128:(ch + 1) * 128, :], in_=ytile[:])
                inst.then_inc(S.sem["O"], 16)
                S.cnt["O"] += 16
            block.sync(_)

        def _(sync):
            S.bar(sync)
            sync.nop()
        block.sync(_)
    return nc



_NC_CACHE = {}


def _get_nc(name, builder):
    if name not in _NC_CACHE:
        _NC_CACHE[name] = builder()
    return _NC_CACHE[name]


def kernel(L0_attn_x, L0_attn_W_qkv, L0_attn_RF, L0_attn_W_out, L0_attn_b_out,
           L0_ffn_router_w, L0_ffn_router_b, L0_ffn_fc1_w, L0_ffn_fc1_b,
           L0_ffn_fc2_w, L0_ffn_fc2_b, L0_ffn_layer_scale):
    x = np.asarray(L0_attn_x)
    wqkv = np.ascontiguousarray(np.asarray(L0_attn_W_qkv))
    rf = np.asarray(L0_attn_RF)
    wout = np.ascontiguousarray(np.asarray(L0_attn_W_out))
    bout = np.asarray(L0_attn_b_out)
    rw = np.asarray(L0_ffn_router_w)
    rb = np.asarray(L0_ffn_router_b)
    fc1w = np.asarray(L0_ffn_fc1_w)
    fc1b = np.asarray(L0_ffn_fc1_b)
    fc2w = np.asarray(L0_ffn_fc2_w)
    fc2b = np.asarray(L0_ffn_fc2_b)
    ls = np.asarray(L0_ffn_layer_scale)

    rfbd = np.zeros((8, 128, 128), np.float32)
    for pr in range(8):
        rfbd[pr, 0:64, 0:64] = rf[2 * pr]
        rfbd[pr, 64:128, 64:128] = rf[2 * pr + 1]
    rfbd = rfbd.reshape(8 * 128, 128)
    lt = np.triu(np.ones((128, 128), np.float32), 1).astype(bfdt)
    idb = np.eye(128, dtype=np.float32).astype(bfdt)
    idf = np.eye(128, dtype=np.float32)
    boutT = np.ascontiguousarray(bout.reshape(8, 128).T)
    rwt = np.ascontiguousarray(rw.T)
    rbrow = np.ascontiguousarray(rb.reshape(1, 8))

    nc1 = _get_nc("L1", build_L1)
    in1 = []
    for c in range(8):
        xs = np.ascontiguousarray(x[2 * c:2 * c + 2].reshape(TPC, DM).T)
        tok = (c * TPC + np.arange(64)[None, :] * 128 +
               np.arange(128)[:, None]).astype(np.float32)
        in1.append({
            "xfm": xs, "wqkv": wqkv, "wout": wout, "boutT": boutT,
            "rfbd": rfbd, "rwt": rwt, "rbrow": rbrow, "tokid": tok,
            "c_lt": lt, "c_idb": idb, "c_idf": idf,
        })
    r1 = run_bass_kernel_spmd(nc1, in1, core_ids=list(range(8))).results

    xn2a = np.ascontiguousarray(np.concatenate([r1[c]["xn2tm"] for c in range(8)], axis=0))
    smap = np.ascontiguousarray(np.concatenate(
        [r1[c]["slotmap"] for c in range(8)] + [np.zeros((16384, 2), np.float32)], axis=0))
    cnta = np.ascontiguousarray(np.concatenate(
        [r1[c]["counts"] for c in range(8)], axis=1))

    gpre = np.zeros((16, 16), np.float32)
    for cp in range(8):
        for kp in range(2):
            for k in range(2):
                for c in range(8):
                    if (kp < k) or (kp == k and cp < c):
                        gpre[cp * 2 + kp, k * 8 + c] = 1.0
    nc2 = _get_nc("L2", build_L2)
    in2 = []
    for ee in range(8):
        basr = np.zeros((1, 16), np.float32)
        for k in range(2):
            for c in range(8):
                basr[0, k * 8 + c] = c * SLOTROWS + (k * 8 + ee) * 8192
        ehot = np.zeros((1, 8), np.float32)
        ehot[0, ee] = 1.0
        in2.append({
            "xn2a": xn2a, "smap": smap, "cnta": cnta,
            "w1t": np.ascontiguousarray(fc1w[ee].T),
            "w2t": np.ascontiguousarray(fc2w[ee].T),
            "b1c": np.ascontiguousarray(fc1b[ee].reshape(32, 128).T),
            "lsr": np.ascontiguousarray(ls[ee:ee + 1]),
            "b2r": np.ascontiguousarray(fc2b[ee:ee + 1]),
            "c_gpre": gpre, "c_ehot": ehot, "c_base": basr, "c_idb": idb,
        })
    r2 = run_bass_kernel_spmd(nc2, in2, core_ids=list(range(8))).results

    sresa = np.ascontiguousarray(np.concatenate(
        [np.asarray(r2[ee]["slotres"]) for ee in range(8)] +
        [np.zeros((128, DM), bfdt)], axis=0))
    nc3 = _get_nc("L3", build_L3)
    in3 = []
    for c in range(8):
        s16o = np.zeros((16, 2), np.float32)
        for cp in range(8):
            for kp in range(2):
                for k in range(2):
                    if (kp < k) or (kp == k and cp < c):
                        s16o[cp * 2 + kp, k] = 1.0
        in3.append({
            "sresd": sresa, "x2t": np.asarray(r1[c]["x2tm"]),
            "metai": np.asarray(r1[c]["meta"]), "cnta": cnta, "s16own": s16o,
        })
    r3 = run_bass_kernel_spmd(nc3, in3, core_ids=list(range(8))).results

    y = np.concatenate([np.asarray(r3[c]["yout"]) for c in range(8)], axis=0)
    return y.reshape(B_GLOBAL, SEQ, DM)


# version 05
# speedup vs baseline: 1.1056x; 1.1056x over previous
from contextlib import ExitStack

import numpy as np
import ml_dtypes
import concourse.bass as bass
import concourse.mybir as mybir
from concourse.bass_utils import run_bass_kernel_spmd
from concourse.bass import IndirectOffsetOnAxis

F32 = mybir.dt.float32
BF16 = mybir.dt.bfloat16
I32 = mybir.dt.int32
AF = mybir.ActivationFunctionType
ALU = mybir.AluOpType
AX = mybir.AxisListType

B_GLOBAL = 16
SEQ = 4096
DM = 1024
TPC = 8192
NBATCH = 2
N_TOK = B_GLOBAL * SEQ
CAPACITY = int(1.2 * (N_TOK // 8 + 1))
CPAD = 10240
HID = 4096
SLOTROWS = 131328
STAB = 1e-6
LN_EPS = 1e-5
REPS = 1e-9
RG = [list(range(8))]

bfdt = ml_dtypes.bfloat16


class K:

    def __init__(self, nc, ctx, block):
        self.nc = nc
        self.block = block
        self.sem = {}
        self.cnt = {}
        for name in ("Q", "P", "T", "A", "V", "O", "G", "C"):
            self.sem[name] = ctx.enter_context(nc.semaphore("s" + name))
            self.cnt[name] = 0

    def bar(self, eng, skip=()):
        for name, h in self.sem.items():
            if name in skip:
                continue
            if self.cnt[name] > 0:
                eng.wait_ge(h, self.cnt[name])

    def done(self, inst, q, k=1):
        inst.then_inc(self.sem[q], k)
        self.cnt[q] += k


def build_fused():
    nc = bass.Bass("TRN2", target_bir_lowering=False, debug=False)
    xfm = nc.dram_tensor("xfm", [DM, TPC], F32, kind="ExternalInput")
    wqkv = nc.dram_tensor("wqkv", [DM, 3 * DM], F32, kind="ExternalInput")
    wout = nc.dram_tensor("wout", [DM, DM], F32, kind="ExternalInput")
    boutT = nc.dram_tensor("boutT", [128, 8], F32, kind="ExternalInput")
    rfbd = nc.dram_tensor("rfbd", [8 * 128, 128], F32, kind="ExternalInput")
    rwt = nc.dram_tensor("rwt", [DM, 8], F32, kind="ExternalInput")
    rbrow = nc.dram_tensor("rbrow", [1, 8], F32, kind="ExternalInput")
    tokid = nc.dram_tensor("tokid", [128, 64], F32, kind="ExternalInput")
    c_lt = nc.dram_tensor("c_lt", [128, 128], BF16, kind="ExternalInput")
    c_idb = nc.dram_tensor("c_idb", [128, 128], BF16, kind="ExternalInput")
    c_idf = nc.dram_tensor("c_idf", [128, 128], F32, kind="ExternalInput")
    w1t = nc.dram_tensor("w1t", [DM, HID], F32, kind="ExternalInput")
    w2t = nc.dram_tensor("w2t", [HID, DM], F32, kind="ExternalInput")
    b1c = nc.dram_tensor("b1c", [128, 32], F32, kind="ExternalInput")
    lsr = nc.dram_tensor("lsr", [1, DM], F32, kind="ExternalInput")
    b2r = nc.dram_tensor("b2r", [1, DM], F32, kind="ExternalInput")
    c_gpre = nc.dram_tensor("c_gpre", [16, 16], F32, kind="ExternalInput")
    c_ehot = nc.dram_tensor("c_ehot", [1, 8], F32, kind="ExternalInput")
    c_base = nc.dram_tensor("c_base", [1, 16], F32, kind="ExternalInput")
    s16own = nc.dram_tensor("s16own", [16, 2], F32, kind="ExternalInput")
    xn2_loc = nc.dram_tensor("xn2_loc", [TPC, DM], BF16)
    xn2_all = nc.dram_tensor("xn2_all", [N_TOK, DM], BF16)
    x2_loc = nc.dram_tensor("x2_loc", [TPC, DM], BF16)
    smap_loc = nc.dram_tensor("smap_loc", [SLOTROWS, 2], F32)
    smap_all = nc.dram_tensor("smap_all", [8 * SLOTROWS, 2], F32)
    cnt_loc = nc.dram_tensor("cnt_loc", [1, 16], F32)
    cnt_all = nc.dram_tensor("cnt_all", [8, 16], F32)
    sres_loc = nc.dram_tensor("sres_loc", [CPAD, DM], BF16)
    sres_all = nc.dram_tensor("sres_all", [8 * CPAD, DM], BF16)
    gscr = nc.dram_tensor("gscr", [16, 1], F32)
    yout = nc.dram_tensor("yout", [TPC, DM], F32, kind="ExternalOutput")

    with ExitStack() as ctx:
        e = ctx.enter_context
        idb_s = e(nc.sbuf_tensor("idb_s", [128, 128], BF16))
        ones_c = e(nc.sbuf_tensor("ones_c", [128, 1], BF16))
        ones_r = e(nc.sbuf_tensor("ones_r", [1, 128], BF16))
        onesf_r = e(nc.sbuf_tensor("onesf_r", [1, 128], F32))
        meta_sb = e(nc.sbuf_tensor("meta_sb", [128, 512], F32))
        cnt_sb = e(nc.sbuf_tensor("cnt_sb", [16, 8], F32))
        P0 = e(nc.psum_tensor("P0", [128, 512], F32))
        P1 = e(nc.psum_tensor("P1", [128, 512], F32))
        P2 = e(nc.psum_tensor("P2", [128, 512], F32))
        P3 = e(nc.psum_tensor("P3", [128, 512], F32))

        block = e(nc.Block())
        S = K(nc, ctx, block)

        def gp(f):
            block.gpsimd(f)

        def tn(f):
            block.tensor(f)

        def sc(f):
            block.scalar(f)

        def ve(f):
            block.vector(f)

        def sy(f):
            block.sync(f)

        with ExitStack() as ctx1:
            e1 = ctx1.enter_context
            wq_b = e1(nc.sbuf_tensor("wq_b", [128, 8 * 2048], BF16))
            wo_b = e1(nc.sbuf_tensor("wo_b", [128, 8 * 1024], BF16))
            rf_b = e1(nc.sbuf_tensor("rf_b", [128, 8 * 128], BF16))
            rw_b = e1(nc.sbuf_tensor("rw_b", [128, 64], BF16))
            bout_s = e1(nc.sbuf_tensor("bout_s", [128, 8], F32))
            rb_s = e1(nc.sbuf_tensor("rb_s", [128, 8], F32))
            lt_s = e1(nc.sbuf_tensor("lt_s", [128, 128], BF16))
            idf_s = e1(nc.sbuf_tensor("idf_s", [128, 128], F32))
            epscol = e1(nc.sbuf_tensor("epscol", [128, 1], F32))
            stabcol = e1(nc.sbuf_tensor("stabcol", [128, 1], F32))
            tokid_s = e1(nc.sbuf_tensor("tokid_s", [128, 64], F32))
            zrow2 = e1(nc.sbuf_tensor("zrow2", [1, 2], F32))

            xbf = e1(nc.sbuf_tensor("xbf", [128, 8 * 512], BF16))
            xn1 = e1(nc.sbuf_tensor("xn1", [128, 8 * 512], BF16))

            xres = e1(nc.sbuf_tensor("xres", [128, 512], BF16))
            mrow = e1(nc.sbuf_tensor("mrow", [1, 4096], F32))
            vrow = e1(nc.sbuf_tensor("vrow", [1, 4096], F32))
            rrow = e1(nc.sbuf_tensor("rrow", [1, 4096], F32))
            m_bf = e1(nc.sbuf_tensor("m_bf", [1, 512], BF16))
            r_bf = e1(nc.sbuf_tensor("r_bf", [1, 512], BF16))
            Mb = e1(nc.sbuf_tensor("Mb", [128, 4096], BF16))
            Rb = e1(nc.sbuf_tensor("Rb", [128, 4096], BF16))
            xsq = xn1
            m2row = mrow
            v2row = vrow
            r2row = rrow
            rm2row = vrow
            m2col = e1(nc.sbuf_tensor("m2col", [128, 64], F32))
            r2col = e1(nc.sbuf_tensor("r2col", [128, 64], F32))
            rm2col = e1(nc.sbuf_tensor("rm2col", [128, 64], F32))

            kbf = e1(nc.sbuf_tensor("kbf", [128, 512], BF16))
            vbf = e1(nc.sbuf_tensor("vbf", [128, 512], BF16))
            v_tm = e1(nc.sbuf_tensor("v_tm", [128, 4 * 130], BF16))
            phik = e1(nc.sbuf_tensor("phik", [128, 512], BF16))
            kvacc = e1(nc.sbuf_tensor("kvacc", [128, 8 * 130], F32))
            kvbd = e1(nc.sbuf_tensor("kvbd", [128, 8 * 128], BF16))
            ksr = e1(nc.sbuf_tensor("ksr", [128, 8 * 128], BF16))
            qbf = kbf
            phiq = phik
            phiqn = vbf
            attn_nt = e1(nc.sbuf_tensor("attn_nt", [128, 8 * 512], BF16))
            x2f = e1(nc.sbuf_tensor("x2f", [128, 512], F32))
            recp = x2f
            x2b = e1(nc.sbuf_tensor("x2b", [128, 512], BF16))
            x2s = e1(nc.sbuf_tensor("x2s", [128, 512], BF16))
            x2tmt = e1(nc.sbuf_tensor("x2tmt", [128, 4 * 1024], BF16))
            rawl = e1(nc.sbuf_tensor("rawl", [128, 64 * 8], F32))
            csum_b = e1(nc.sbuf_tensor("csum_b", [128, 8], F32))
            xn2t = e1(nc.sbuf_tensor("xn2t", [128, 1024], BF16))

            lgt = e1(nc.sbuf_tensor("lgt", [128, 64 * 8], F32))
            epat = e1(nc.sbuf_tensor("epat", [128, 64 * 8], F32))
            u_t = e1(nc.sbuf_tensor("u_t", [128, 64 * 8], F32))
            scr8 = e1(nc.sbuf_tensor("scr8", [128, 64 * 8], F32))
            scr8b = e1(nc.sbuf_tensor("scr8b", [128, 64 * 8], F32))
            red1 = e1(nc.sbuf_tensor("red1", [128, 64], F32))
            mx_t = e1(nc.sbuf_tensor("mx_t", [128, 64], F32))
            s_t = e1(nc.sbuf_tensor("s_t", [128, 64], F32))
            u1_t = e1(nc.sbuf_tensor("u1_t", [128, 64], F32))
            u2_t = e1(nc.sbuf_tensor("u2_t", [128, 64], F32))
            i1_t = e1(nc.sbuf_tensor("i1_t", [128, 64], F32))
            i2_t = e1(nc.sbuf_tensor("i2_t", [128, 64], F32))
            w1_t = e1(nc.sbuf_tensor("w1_t", [128, 64], F32))
            w2_t = e1(nc.sbuf_tensor("w2_t", [128, 64], F32))
            va1_t = e1(nc.sbuf_tensor("va1_t", [128, 64], F32))
            va2_t = e1(nc.sbuf_tensor("va2_t", [128, 64], F32))
            oh1 = e1(nc.sbuf_tensor("oh1", [128, 64 * 8], BF16))
            oh2 = e1(nc.sbuf_tensor("oh2", [128, 64 * 8], BF16))
            pos1 = lgt
            pos2 = u_t
            po1 = e1(nc.sbuf_tensor("po1", [128, 64], F32))
            po2 = e1(nc.sbuf_tensor("po2", [128, 64], F32))
            epat2 = e1(nc.sbuf_tensor("epat2", [128, 64 * 8], F32))
            ohM1 = e1(nc.sbuf_tensor("ohM1", [128, 64 * 8], BF16))
            ohM2 = e1(nc.sbuf_tensor("ohM2", [128, 64 * 8], BF16))
            totb = e1(nc.sbuf_tensor("totb", [128, 2 * 512], F32))
            tscab = e1(nc.sbuf_tensor("tscab", [128, 2 * 512], F32))
            onescol_b = e1(nc.sbuf_tensor("onescol_b", [128, 64], F32))
            cntrow = e1(nc.sbuf_tensor("cntrow", [1, 16], F32))
            scv = e1(nc.sbuf_tensor("scv", [128, 2 * 128], F32))
            eaff = e1(nc.sbuf_tensor("eaff", [128, 64 * 8], F32))
            offi = e1(nc.sbuf_tensor("offi", [128, 2 * 64], I32))

            PRa = e1(nc.psum_tensor("PRa", [1, 512], F32))
            PRb = e1(nc.psum_tensor("PRb", [1, 512], F32))
            P4 = e1(nc.psum_tensor("P4", [128, 512], F32))
            PT = e1(nc.psum_tensor("PT", [128, 1024], BF16))

            def _(sync):
                sync.dma_start(out=lt_s[:], in_=c_lt[:, :]).then_inc(S.sem["Q"], 16)
                sync.dma_start(out=idb_s[:], in_=c_idb[:, :]).then_inc(S.sem["Q"], 16)
                sync.dma_start(out=idf_s[:], in_=c_idf[:, :]).then_inc(S.sem["Q"], 16)
                sync.dma_start(out=bout_s[:], in_=boutT[:, :]).then_inc(S.sem["Q"], 16)
                sync.dma_start(out=tokid_s[:], in_=tokid[:, :]).then_inc(S.sem["Q"], 16)
                sync.dma_start(out=rb_s[0:1, :], in_=rbrow[:, :]).then_inc(S.sem["Q"], 16)
                S.cnt["Q"] += 6 * 16
            sy(_)

            def _(g):
                for qi in range(8):
                    g.dma_start(out=wo_b[:, qi * 1024:(qi + 1) * 1024],
                                in_=wout[qi * 128:(qi + 1) * 128, :]).then_inc(S.sem["P"], 16)
                for a in range(8):
                    g.dma_start(out=rf_b[:, a * 128:(a + 1) * 128],
                                in_=rfbd[a * 128:(a + 1) * 128, :]).then_inc(S.sem["P"], 16)
                for ci in range(8):
                    g.dma_start(out=rw_b[:, ci * 8:(ci + 1) * 8],
                                in_=rwt[ci * 128:(ci + 1) * 128, :]).then_inc(S.sem["P"], 16)
                S.cnt["P"] += 24 * 16
                g.iota(epat[:], [[0, 64], [1, 8]], base=0, channel_multiplier=0,
                       allow_small_or_imprecise_dtypes=True).then_inc(S.sem["G"], 1)
                g.iota(epat2[:], [[1, 8], [0, 64]], base=0, channel_multiplier=0,
                       allow_small_or_imprecise_dtypes=True).then_inc(S.sem["G"], 1)
                g.iota(eaff[:], [[0, 64], [1, 8]], base=64, channel_multiplier=0,
                       allow_small_or_imprecise_dtypes=True).then_inc(S.sem["G"], 1)
                S.cnt["G"] += 3
            gp(_)

            def _(v):
                S.bar(v)
                v.memset(ones_c[:], 1.0)
                v.memset(ones_r[:], 1.0)
                v.memset(onesf_r[:], 1.0)
                v.memset(epscol[:], LN_EPS)
                v.memset(stabcol[:], STAB)
                v.memset(onescol_b[:], 1.0)
                v.memset(kvacc[:], 0.0)
                v.memset(zrow2[:], 0.0)
                inst = v.memset(v_tm[:], 1.0)
                S.done(inst, "V")
            ve(_)

            def _(sync):
                S.bar(sync)
                inst = sync.dma_start(out=smap_loc[0:1, :], in_=zrow2[0:1, :])
                inst.then_inc(S.sem["Q"], 16)
                S.cnt["Q"] += 16
            sy(_)

            def _(t):
                S.bar(t)
                inst = t.matmul(P0[:, 0:8], onesf_r[:], rb_s[0:1, :], start=True, stop=True)
                S.done(inst, "T")
            tn(_)

            def _(v):
                S.bar(v)
                inst = v.tensor_copy(rb_s[:, :], P0[:, 0:8])
                S.done(inst, "V")
            ve(_)

            def _(t):
                S.bar(t)
                for ci in range(8):
                    inst = t.matmul(PRa[0:1, 0:8], ones_c[:], rw_b[:, ci * 8:(ci + 1) * 8],
                                    start=(ci == 0), stop=(ci == 7))
                S.done(inst, "T")
            tn(_)

            def _(v):
                S.bar(v)
                inst = v.tensor_scalar_mul(csum_b[0:1, :], PRa[0:1, 0:8], -1.0)
                S.done(inst, "V")
            ve(_)

            def _(t):
                S.bar(t)
                inst = t.matmul(P0[:, 0:8], onesf_r[:], csum_b[0:1, 0:8], start=True, stop=True)
                S.done(inst, "T")
            tn(_)

            def _(v):
                S.bar(v)
                inst = v.tensor_copy(csum_b[:, :], P0[:, 0:8])
                S.done(inst, "V")
            ve(_)

            for b in range(NBATCH):
                for nt in range(8):
                    def _(g, nt=nt, b=b):
                        S.bar(g)
                        for ci in range(8):
                            inst = g.dma_start(out=xbf[:, ci * 512:(ci + 1) * 512],
                                               in_=xfm[ci * 128:(ci + 1) * 128,
                                                       b * 4096 + nt * 512:b * 4096 + (nt + 1) * 512])
                            inst.then_inc(S.sem["P"], 16)
                        S.cnt["P"] += 8 * 16
                    gp(_)

                    def _(s, nt=nt):
                        S.bar(s)
                        for ci in range(8):
                            inst = s.activation(out=xsq[:, ci * 512:(ci + 1) * 512],
                                                in_=xbf[:, ci * 512:(ci + 1) * 512], func=AF.Square)
                        S.done(inst, "A")
                    sc(_)

                    def _(t, nt=nt):
                        S.bar(t)
                        for ci in range(8):
                            t.matmul(PRa[0:1, :], ones_c[:], xbf[:, ci * 512:(ci + 1) * 512],
                                     start=(ci == 0), stop=(ci == 7))
                        for ci in range(8):
                            inst = t.matmul(PRb[0:1, :], ones_c[:], xsq[:, ci * 512:(ci + 1) * 512],
                                            start=(ci == 0), stop=(ci == 7))
                        S.done(inst, "T")
                    tn(_)

                    def _(v, nt=nt):
                        S.bar(v)
                        v.tensor_copy(mrow[0:1, nt * 512:(nt + 1) * 512], PRa[0:1, :])
                        inst = v.tensor_copy(vrow[0:1, nt * 512:(nt + 1) * 512], PRb[0:1, :])
                        S.done(inst, "V")
                    ve(_)

                def _(v, b=b):
                    S.bar(v)
                    v.tensor_scalar_mul(mrow[:], mrow[:], 1.0 / DM)
                    inst = v.tensor_scalar_mul(vrow[:], vrow[:], 1.0 / DM)
                    S.done(inst, "V")
                ve(_)

                def _(v, b=b):
                    S.bar(v)
                    inst = v.tensor_mul(rrow[:], mrow[:], mrow[:])
                    S.done(inst, "V")
                ve(_)

                def _(v, b=b):
                    S.bar(v)
                    inst = v.tensor_sub(vrow[:], vrow[:], rrow[:])
                    S.done(inst, "V")
                ve(_)

                def _(s, b=b):
                    S.bar(s)
                    inst = s.activation(out=rrow[:], in_=vrow[:], func=AF.Ln,
                                        bias=epscol[0:1, 0:1])
                    S.done(inst, "A")
                sc(_)

                def _(s, b=b):
                    S.bar(s)
                    inst = s.activation(out=rrow[:], in_=rrow[:], func=AF.Exp, scale=-0.5)
                    S.done(inst, "A")
                sc(_)

                for nt in range(8):
                    def _(s, nt=nt):
                        S.bar(s)
                        s.copy(m_bf[:], mrow[0:1, nt * 512:(nt + 1) * 512])
                        inst = s.copy(r_bf[:], rrow[0:1, nt * 512:(nt + 1) * 512])
                        S.done(inst, "A")
                    sc(_)

                    def _(t, nt=nt):
                        S.bar(t)
                        t.matmul(P0[:, :], ones_r[:], m_bf[0:1, :], start=True, stop=True)
                        inst = t.matmul(P1[:, :], ones_r[:], r_bf[0:1, :], start=True, stop=True)
                        S.done(inst, "T")
                    tn(_)

                    def _(s, nt=nt):
                        S.bar(s)
                        s.copy(Mb[:, nt * 512:(nt + 1) * 512], P0[:, :])
                        inst = s.copy(Rb[:, nt * 512:(nt + 1) * 512], P1[:, :])
                        S.done(inst, "A")
                    sc(_)

                def _(g, b=b):
                    S.bar(g)
                    for ci in range(8):
                        inst = g.dma_start(out=wq_b[:, ci * 2048:(ci + 1) * 2048],
                                           in_=wqkv[ci * 128:(ci + 1) * 128, 1024:3072])
                        inst.then_inc(S.sem["P"], 16)
                    S.cnt["P"] += 8 * 16
                gp(_)

                for nt in range(8):
                    def _(g, nt=nt, b=b):
                        S.bar(g)
                        for ci in range(8):
                            inst = g.dma_start(out=xbf[:, ci * 512:(ci + 1) * 512],
                                               in_=xfm[ci * 128:(ci + 1) * 128,
                                                       b * 4096 + nt * 512:b * 4096 + (nt + 1) * 512])
                            inst.then_inc(S.sem["P"], 16)
                        S.cnt["P"] += 8 * 16
                    gp(_)

                    def _(v, nt=nt):
                        S.bar(v)
                        x3 = xbf[:].rearrange("p (c t) -> p c t", c=8)
                        n3 = xn1[:].rearrange("p (c t) -> p c t", c=8)
                        mb3 = (Mb[:, nt * 512:(nt + 1) * 512]
                               .rearrange("p (u t) -> p u t", u=1)
                               .broadcast_to([128, 8, 512]))
                        inst = v.tensor_tensor(n3, x3, mb3, ALU.subtract)
                        S.done(inst, "V")
                    ve(_)

                    def _(v, nt=nt):
                        S.bar(v)
                        n3 = xn1[:].rearrange("p (c t) -> p c t", c=8)
                        rb3 = (Rb[:, nt * 512:(nt + 1) * 512]
                               .rearrange("p (u t) -> p u t", u=1)
                               .broadcast_to([128, 8, 512]))
                        inst = v.tensor_tensor(n3, n3, rb3, ALU.mult)
                        S.done(inst, "V")
                    ve(_)

                    for pr in range(8):
                        def _(t, nt=nt, pr=pr):
                            S.bar(t)
                            for ci in range(8):
                                t.matmul(P0[:, :], wq_b[:, ci * 2048 + pr * 128:
                                                        ci * 2048 + (pr + 1) * 128],
                                         xn1[:, ci * 512:(ci + 1) * 512],
                                         start=(ci == 0), stop=(ci == 7))
                            for ci in range(8):
                                inst = t.matmul(P1[:, :], wq_b[:, ci * 2048 + 1024 + pr * 128:
                                                               ci * 2048 + 1024 + (pr + 1) * 128],
                                                xn1[:, ci * 512:(ci + 1) * 512],
                                                start=(ci == 0), stop=(ci == 7))
                            S.done(inst, "T")
                        tn(_)

                        def _(s, nt=nt, pr=pr):
                            S.bar(s)
                            s.copy(kbf[:], P0[:, :])
                            inst = s.copy(vbf[:], P1[:, :])
                            S.done(inst, "A")
                        sc(_)

                        def _(t, nt=nt, pr=pr):
                            S.bar(t)
                            for j in range(4):
                                t.matmul(P2[:, j * 128:(j + 1) * 128],
                                         kbf[:, j * 128:(j + 1) * 128],
                                         rf_b[:, pr * 128:(pr + 1) * 128],
                                         start=(j == 0), stop=(j == 3))
                            for j in range(4):
                                inst = t.transpose(PT[:, j * 128:(j + 1) * 128],
                                                   vbf[:, j * 128:(j + 1) * 128], idb_s[:])
                            S.done(inst, "T")
                        tn(_)

                        def _(s, nt=nt, pr=pr):
                            S.bar(s)
                            inst = s.activation(out=phik[:], in_=P2[:, :], func=AF.Relu)
                            S.done(inst, "A")
                        sc(_)

                        def _(v, nt=nt, pr=pr):
                            S.bar(v)
                            for j in range(4):
                                v.tensor_copy(v_tm[:, j * 130:j * 130 + 64],
                                              PT[:, j * 128:j * 128 + 64])
                                inst = v.tensor_copy(v_tm[:, j * 130 + 65:j * 130 + 129],
                                                     PT[:, j * 128 + 64:(j + 1) * 128])
                            S.done(inst, "V")
                        ve(_)

                        def _(t, nt=nt, pr=pr):
                            S.bar(t)
                            first = True
                            for j in range(4):
                                t.matmul(P3[0:64, 0:65], phik[:, j * 128:j * 128 + 64],
                                         v_tm[:, j * 130:j * 130 + 65],
                                         start=first, stop=False)
                                first = False
                                inst = t.matmul(P3[64:128, 65:130],
                                                phik[:, j * 128 + 64:(j + 1) * 128],
                                                v_tm[:, j * 130 + 65:j * 130 + 130],
                                                start=False, stop=(j == 3))
                            S.done(inst, "T")
                        tn(_)

                        def _(v, nt=nt, pr=pr):
                            S.bar(v)
                            sl = slice(pr * 130, (pr + 1) * 130)
                            inst = v.tensor_add(kvacc[:, sl], kvacc[:, sl], P3[:, 0:130])
                            S.done(inst, "V")
                        ve(_)

                def _(v, b=b):
                    S.bar(v)
                    v.memset(kvbd[:], 0.0)
                    v.memset(ksr[:], 0.0)
                    for pr in range(8):
                        base = pr * 130
                        v.tensor_copy(kvbd[0:64, pr * 128:pr * 128 + 64],
                                      kvacc[0:64, base:base + 64])
                        v.tensor_copy(kvbd[64:128, pr * 128 + 64:(pr + 1) * 128],
                                      kvacc[64:128, base + 65:base + 129])
                        v.tensor_copy(ksr[0:64, pr * 128:pr * 128 + 64],
                                      kvacc[0:64, base + 64:base + 65].broadcast_to([64, 64]))
                        inst = v.tensor_copy(ksr[64:128, pr * 128 + 64:(pr + 1) * 128],
                                             kvacc[64:128, base + 129:base + 130]
                                             .broadcast_to([64, 64]))
                    v.memset(kvacc[:], 0.0)
                    S.done(inst, "V")
                ve(_)

                def _(g, b=b):
                    S.bar(g)
                    for ci in range(8):
                        inst = g.dma_start(out=wq_b[:, ci * 2048:ci * 2048 + 1024],
                                           in_=wqkv[ci * 128:(ci + 1) * 128, 0:1024])
                        inst.then_inc(S.sem["P"], 16)
                    S.cnt["P"] += 8 * 16
                gp(_)

                for nt in range(8):
                    def _(g, nt=nt, b=b):
                        S.bar(g)
                        for ci in range(8):
                            inst = g.dma_start(out=xbf[:, ci * 512:(ci + 1) * 512],
                                               in_=xfm[ci * 128:(ci + 1) * 128,
                                                       b * 4096 + nt * 512:b * 4096 + (nt + 1) * 512])
                            inst.then_inc(S.sem["P"], 16)
                        S.cnt["P"] += 8 * 16
                    gp(_)

                    def _(v, nt=nt):
                        S.bar(v)
                        for ci in range(8):
                            sl = slice(ci * 512, (ci + 1) * 512)
                            v.tensor_sub(xn1[:, sl], xbf[:, sl], Mb[:, nt * 512:(nt + 1) * 512])
                            inst = v.tensor_mul(xn1[:, sl], xn1[:, sl],
                                                Rb[:, nt * 512:(nt + 1) * 512])
                        S.done(inst, "V")
                    ve(_)

                    for pr in range(8):
                        def _(t, nt=nt, pr=pr):
                            S.bar(t)
                            for ci in range(8):
                                inst = t.matmul(P0[:, :], wq_b[:, ci * 2048 + pr * 128:
                                                              ci * 2048 + (pr + 1) * 128],
                                                xn1[:, ci * 512:(ci + 1) * 512],
                                                start=(ci == 0), stop=(ci == 7))
                            S.done(inst, "T")
                        tn(_)

                        def _(s, nt=nt, pr=pr):
                            S.bar(s)
                            inst = s.copy(qbf[:], P0[:, :])
                            S.done(inst, "A")
                        sc(_)

                        def _(t, nt=nt, pr=pr):
                            S.bar(t)
                            inst = t.matmul(P1[:, :], rf_b[:, pr * 128:(pr + 1) * 128], qbf[:],
                                            start=True, stop=True)
                            S.done(inst, "T")
                        tn(_)

                        def _(s, nt=nt, pr=pr):
                            S.bar(s)
                            inst = s.activation(out=phiq[:], in_=P1[:, :], func=AF.Relu)
                            S.done(inst, "A")
                        sc(_)

                        def _(t, nt=nt, pr=pr):
                            S.bar(t)
                            inst = t.matmul(P2[:, :], ksr[:, pr * 128:(pr + 1) * 128], phiq[:],
                                            start=True, stop=True)
                            S.done(inst, "T")
                        tn(_)

                        def _(s, nt=nt, pr=pr):
                            S.bar(s)
                            inst = s.activation(out=recp[:], in_=P2[:, :], func=AF.Ln,
                                                bias=stabcol[:, 0:1])
                            S.done(inst, "A")
                        sc(_)

                        def _(s, nt=nt, pr=pr):
                            S.bar(s)
                            inst = s.activation(out=recp[:], in_=recp[:], func=AF.Exp,
                                                scale=-1.0)
                            S.done(inst, "A")
                        sc(_)

                        def _(v, nt=nt, pr=pr):
                            S.bar(v)
                            inst = v.tensor_mul(phiqn[:], phiq[:], recp[:])
                            S.done(inst, "V")
                        ve(_)

                        def _(t, nt=nt, pr=pr):
                            S.bar(t)
                            inst = t.matmul(P3[:, :], kvbd[:, pr * 128:(pr + 1) * 128], phiqn[:],
                                            start=True, stop=True)
                            S.done(inst, "T")
                        tn(_)

                        def _(s, nt=nt, pr=pr):
                            S.bar(s)
                            inst = s.copy(attn_nt[:, pr * 512:(pr + 1) * 512], P3[:, :])
                            S.done(inst, "A")
                        sc(_)

                    for oi in range(8):
                        def _(g, nt=nt, oi=oi, b=b):
                            S.bar(g)
                            inst = g.dma_start(out=xres[:],
                                               in_=xfm[oi * 128:(oi + 1) * 128,
                                                       b * 4096 + nt * 512:b * 4096 + (nt + 1) * 512])
                            inst.then_inc(S.sem["P"], 16)
                            S.cnt["P"] += 16
                        gp(_)

                        def _(t, nt=nt, oi=oi):
                            S.bar(t)
                            for qi in range(8):
                                inst = t.matmul(P4[:, :], wo_b[:, qi * 1024 + oi * 128:
                                                              qi * 1024 + (oi + 1) * 128],
                                                attn_nt[:, qi * 512:(qi + 1) * 512],
                                                start=(qi == 0), stop=(qi == 7))
                            S.done(inst, "T")
                        tn(_)

                        def _(v, nt=nt, oi=oi):
                            S.bar(v)
                            inst = v.tensor_add(x2f[:], P4[:, :], xres[:])
                            S.done(inst, "V")
                        ve(_)

                        def _(s, nt=nt, oi=oi):
                            S.bar(s)
                            inst = s.activation(out=x2b[:], in_=x2f[:], func=AF.Identity,
                                                bias=bout_s[:, oi:oi + 1])
                            S.done(inst, "A")
                        sc(_)

                        def _(s, nt=nt, oi=oi):
                            S.bar(s)
                            inst = s.activation(out=x2s[:], in_=x2b[:], func=AF.Square)
                            S.done(inst, "A")
                        sc(_)

                        def _(t, nt=nt, oi=oi, b=b):
                            S.bar(t)
                            t.matmul(PRa[0:1, :], ones_c[:], x2b[:],
                                     start=(oi == 0), stop=(oi == 7))
                            t.matmul(PRb[0:1, :], ones_c[:], x2s[:],
                                     start=(oi == 0), stop=(oi == 7))
                            for j in range(4):
                                t.matmul(P2[:, j * 8:(j + 1) * 8],
                                         x2b[:, j * 128:(j + 1) * 128],
                                         rw_b[:, oi * 8:(oi + 1) * 8],
                                         start=(oi == 0 and j == 0), stop=(oi == 7 and j == 3))
                            for j in range(4):
                                inst = t.transpose(PT[:, j * 128:(j + 1) * 128],
                                                   x2b[:, j * 128:(j + 1) * 128], idb_s[:])
                            S.done(inst, "T")
                        tn(_)

                        def _(v, nt=nt, oi=oi):
                            S.bar(v)
                            for j in range(4):
                                inst = v.tensor_copy(
                                    x2tmt[:, j * 1024 + oi * 128:j * 1024 + (oi + 1) * 128],
                                    PT[:, j * 128:(j + 1) * 128])
                            S.done(inst, "V")
                        ve(_)

                    def _(v, nt=nt, b=b):
                        S.bar(v)
                        v.tensor_copy(m2row[0:1, nt * 512:(nt + 1) * 512], PRa[0:1, :])
                        v.tensor_copy(v2row[0:1, nt * 512:(nt + 1) * 512], PRb[0:1, :])
                        for j in range(4):
                            inst = v.tensor_copy(rawl[:, (b * 32 + nt * 4 + j) * 8:
                                                       (b * 32 + nt * 4 + j + 1) * 8],
                                                 P2[:, j * 8:(j + 1) * 8])
                        S.done(inst, "V")
                    ve(_)

                    def _(sync, nt=nt, b=b):
                        S.bar(sync)
                        for j in range(4):
                            r0 = b * 4096 + nt * 512 + j * 128
                            inst = sync.dma_start(out=x2_loc[r0:r0 + 128, :],
                                                  in_=x2tmt[:, j * 1024:(j + 1) * 1024])
                            inst.then_inc(S.sem["Q"], 16)
                        S.cnt["Q"] += 4 * 16
                    sy(_)

                def _(v, b=b):
                    S.bar(v)
                    v.tensor_scalar_mul(m2row[:], m2row[:], 1.0 / DM)
                    inst = v.tensor_scalar_mul(v2row[:], v2row[:], 1.0 / DM)
                    S.done(inst, "V")
                ve(_)

                def _(v, b=b):
                    S.bar(v)
                    inst = v.tensor_mul(r2row[:], m2row[:], m2row[:])
                    S.done(inst, "V")
                ve(_)

                def _(v, b=b):
                    S.bar(v)
                    inst = v.tensor_sub(v2row[:], v2row[:], r2row[:])
                    S.done(inst, "V")
                ve(_)

                def _(s, b=b):
                    S.bar(s)
                    inst = s.activation(out=r2row[:], in_=v2row[:], func=AF.Ln,
                                        bias=epscol[0:1, 0:1])
                    S.done(inst, "A")
                sc(_)

                def _(s, b=b):
                    S.bar(s)
                    inst = s.activation(out=r2row[:], in_=r2row[:], func=AF.Exp, scale=-0.5)
                    S.done(inst, "A")
                sc(_)

                def _(v, b=b):
                    S.bar(v)
                    inst = v.tensor_mul(rm2row[:], r2row[:], m2row[:])
                    S.done(inst, "V")
                ve(_)

                def _(t, b=b):
                    S.bar(t)
                    for ch in range(32):
                        t.transpose(P0[:, ch:ch + 1], m2row[0:1, ch * 128:(ch + 1) * 128],
                                    idf_s[0:1, 0:1])
                    for ch in range(32):
                        t.transpose(P1[:, ch:ch + 1], r2row[0:1, ch * 128:(ch + 1) * 128],
                                    idf_s[0:1, 0:1])
                    for ch in range(32):
                        inst = t.transpose(P2[:, ch:ch + 1], rm2row[0:1, ch * 128:(ch + 1) * 128],
                                           idf_s[0:1, 0:1])
                    S.done(inst, "T")
                tn(_)

                def _(v, b=b):
                    S.bar(v)
                    v.tensor_copy(m2col[:, b * 32:(b + 1) * 32], P0[:, 0:32])
                    v.tensor_copy(r2col[:, b * 32:(b + 1) * 32], P1[:, 0:32])
                    inst = v.tensor_copy(rm2col[:, b * 32:(b + 1) * 32], P2[:, 0:32])
                    S.done(inst, "V")
                ve(_)

                def _(v, b=b):
                    S.bar(v)
                    sl = slice(b * 32 * 8, (b + 1) * 32 * 8)
                    c3 = (csum_b[:].rearrange("p (u e) -> p u e", u=1)
                          .broadcast_to([128, 32, 8]))
                    rm3 = (rm2col[:, b * 32:(b + 1) * 32]
                           .rearrange("p (c u) -> p c u", u=1).broadcast_to([128, 32, 8]))
                    r23 = (r2col[:, b * 32:(b + 1) * 32]
                           .rearrange("p (c u) -> p c u", u=1).broadcast_to([128, 32, 8]))
                    u3 = u_t[:, sl].rearrange("p (c e) -> p c e", e=8)
                    l3 = lgt[:, sl].rearrange("p (c e) -> p c e", e=8)
                    raw3 = rawl[:, sl].rearrange("p (c e) -> p c e", e=8)
                    v.tensor_tensor(u3, c3, rm3, ALU.mult)
                    inst = v.tensor_tensor(l3, raw3, r23, ALU.mult)
                    S.done(inst, "V")
                ve(_)

                def _(v, b=b):
                    S.bar(v)
                    sl = slice(b * 32 * 8, (b + 1) * 32 * 8)
                    inst = v.tensor_add(lgt[:, sl], lgt[:, sl], u_t[:, sl])
                    S.done(inst, "V")
                ve(_)

                def _(v, b=b):
                    S.bar(v)
                    sl = slice(b * 32 * 8, (b + 1) * 32 * 8)
                    l3 = lgt[:, sl].rearrange("p (c e) -> p c e", e=8)
                    rb3 = (rb_s[:].rearrange("p (u e) -> p u e", u=1)
                           .broadcast_to([128, 32, 8]))
                    inst = v.tensor_tensor(l3, l3, rb3, ALU.add)
                    S.done(inst, "V")
                ve(_)

                for ch in range(32):
                    def _(sync, ch=ch, b=b):
                        S.bar(sync)
                        r0 = b * 4096 + ch * 128
                        inst = sync.dma_start(out=xn2t[:], in_=x2_loc[r0:r0 + 128, :])
                        inst.then_inc(S.sem["Q"], 16)
                        S.cnt["Q"] += 16
                    sy(_)

                    def _(v, ch=ch, b=b):
                        S.bar(v)
                        gch = b * 32 + ch
                        inst = v.tensor_scalar(out=xn2t[:], in0=xn2t[:],
                                               scalar1=m2col[:, gch:gch + 1],
                                               scalar2=r2col[:, gch:gch + 1],
                                               op0=ALU.subtract, op1=ALU.mult)
                        S.done(inst, "V")
                    ve(_)

                    def _(sync, ch=ch, b=b):
                        S.bar(sync)
                        r0 = b * 4096 + ch * 128
                        inst = sync.dma_start(out=xn2_loc[r0:r0 + 128, :], in_=xn2t[:])
                        inst.then_inc(S.sem["Q"], 16)
                        S.cnt["Q"] += 16
                    sy(_)

            def _(v):
                S.bar(v)
                l3 = lgt[:].rearrange("p (c e) -> p c e", e=8)
                inst = v.tensor_reduce(mx_t[:], l3, AX.X, ALU.max)
                S.done(inst, "V")
            ve(_)

            def _(v):
                S.bar(v)
                u3 = u_t[:].rearrange("p (c e) -> p c e", e=8)
                l3 = lgt[:].rearrange("p (c e) -> p c e", e=8)

                def bc(t64):
                    return t64[:].rearrange("p (c u) -> p c u", u=1).broadcast_to([128, 64, 8])

                inst = v.tensor_tensor(u3, l3, bc(mx_t), ALU.subtract)
                S.done(inst, "V")
            ve(_)

            def _(s):
                S.bar(s)
                inst = s.activation(out=u_t[:], in_=u_t[:], func=AF.Exp)
                S.done(inst, "A")
            sc(_)

            def _(v):
                S.bar(v)
                u3 = u_t[:].rearrange("p (c e) -> p c e", e=8)

                v.tensor_reduce(s_t[:], u3, AX.X, ALU.add)
                inst = v.tensor_reduce(u1_t[:], u3, AX.X, ALU.max)
                S.done(inst, "V")
            ve(_)

            def bc(t64):
                return t64[:].rearrange("p (c u) -> p c u", u=1).broadcast_to([128, 64, 8])

            def bcM(t64):
                return t64[:].rearrange("p (u c) -> p u c", u=1).broadcast_to([128, 8, 64])

            U3 = u_t[:].rearrange("p (c e) -> p c e", e=8)
            E3 = epat[:].rearrange("p (c e) -> p c e", e=8)
            S3 = scr8[:].rearrange("p (c e) -> p c e", e=8)
            S3B = scr8b[:].rearrange("p (c e) -> p c e", e=8)
            L3V = lgt[:].rearrange("p (c e) -> p c e", e=8)
            EM = epat2[:].rearrange("p (e c) -> p e c", e=8)

            def _(v):
                S.bar(v)
                inst = v.tensor_tensor(S3, U3, bc(u1_t), ALU.is_ge)
                S.done(inst, "V")
            ve(_)

            def _(v):
                S.bar(v)
                inst = v.scalar_tensor_tensor(scr8b[:], scr8[:], -64.0, eaff[:],
                                              ALU.mult, ALU.add)
                S.done(inst, "V")
            ve(_)

            def _(v):
                S.bar(v)
                inst = v.tensor_reduce(i1_t[:], S3B, AX.X, ALU.min)
                S.done(inst, "V")
            ve(_)

            def _(v):
                S.bar(v)
                inst = v.tensor_tensor(S3, E3, bc(i1_t), ALU.not_equal)
                S.done(inst, "V")
            ve(_)

            def _(v):
                S.bar(v)
                inst = v.tensor_tensor(S3B, U3, S3, ALU.mult)
                S.done(inst, "V")
            ve(_)

            def _(v):
                S.bar(v)
                inst = v.tensor_reduce(u2_t[:], S3B, AX.X, ALU.max)
                S.done(inst, "V")
            ve(_)

            def _(v):
                S.bar(v)
                inst = v.tensor_tensor(L3V, U3, bc(u2_t), ALU.is_ge)
                S.done(inst, "V")
            ve(_)

            def _(v):
                S.bar(v)
                inst = v.tensor_mul(lgt[:], lgt[:], scr8[:])
                S.done(inst, "V")
            ve(_)

            def _(v):
                S.bar(v)
                inst = v.scalar_tensor_tensor(scr8b[:], lgt[:], -64.0, eaff[:],
                                              ALU.mult, ALU.add)
                S.done(inst, "V")
            ve(_)

            def _(v):
                S.bar(v)
                inst = v.tensor_reduce(i2_t[:], S3B, AX.X, ALU.min)
                S.done(inst, "V")
            ve(_)

            def _(v):
                S.bar(v)
                inst = v.tensor_add(red1[:], u1_t[:], u2_t[:])
                S.done(inst, "V")
            ve(_)

            def _(v):
                S.bar(v)
                inst = v.scalar_tensor_tensor(red1[:], s_t[:], REPS, red1[:],
                                              ALU.mult, ALU.add)
                S.done(inst, "V")
            ve(_)

            def _(s):
                S.bar(s)
                inst = s.activation(out=red1[:], in_=red1[:], func=AF.Ln)
                S.done(inst, "A")
            sc(_)

            def _(s):
                S.bar(s)
                inst = s.activation(out=red1[:], in_=red1[:], func=AF.Exp, scale=-1.0)
                S.done(inst, "A")
            sc(_)

            def _(v):
                S.bar(v)
                v.tensor_mul(w1_t[:], u1_t[:], red1[:])
                inst = v.tensor_mul(w2_t[:], u2_t[:], red1[:])
                S.done(inst, "V")
            ve(_)

            def _(v):
                S.bar(v)
                v.tensor_single_scalar(va1_t[:], w1_t[:], REPS, ALU.is_gt)
                inst = v.tensor_single_scalar(va2_t[:], w2_t[:], REPS, ALU.is_gt)
                S.done(inst, "V")
            ve(_)

            def _(v):
                S.bar(v)
                v.tensor_tensor(S3, E3, bc(i1_t), ALU.is_equal)
                v.tensor_tensor(S3B, E3, bc(i2_t), ALU.is_equal)
                v.tensor_tensor(lgt[:].rearrange("p (e c) -> p e c", e=8), EM,
                                bcM(i1_t), ALU.is_equal)
                inst = v.tensor_tensor(u_t[:].rearrange("p (e c) -> p e c", e=8), EM,
                                       bcM(i2_t), ALU.is_equal)
                S.done(inst, "V")
            ve(_)

            def _(v):
                S.bar(v)
                v.tensor_tensor(oh1[:].rearrange("p (c e) -> p c e", e=8), S3,
                                bc(va1_t), ALU.mult)
                v.tensor_tensor(oh2[:].rearrange("p (c e) -> p c e", e=8), S3B,
                                bc(va2_t), ALU.mult)
                v.tensor_tensor(ohM1[:].rearrange("p (e c) -> p e c", e=8),
                                lgt[:].rearrange("p (e c) -> p e c", e=8),
                                bcM(va1_t), ALU.mult)
                inst = v.tensor_tensor(ohM2[:].rearrange("p (e c) -> p e c", e=8),
                                       u_t[:].rearrange("p (e c) -> p e c", e=8),
                                       bcM(va2_t), ALU.mult)
                S.done(inst, "V")
            ve(_)

            def _(t):
                S.bar(t)
                t.matmul(P0[:, :], lt_s[:], oh1[:], start=True, stop=True)
                t.matmul(P1[:, :], lt_s[:], oh2[:], start=True, stop=True)
                t.matmul(P2[0:1, :], ones_c[:], ohM1[:], start=True, stop=True)
                inst = t.matmul(P3[0:1, :], ones_c[:], ohM2[:], start=True, stop=True)
                S.done(inst, "T")
            tn(_)

            def _(v):
                S.bar(v)
                v.tensor_copy(totb[0:1, 0:512], P2[0:1, :])
                inst = v.tensor_copy(totb[0:1, 512:1024], P3[0:1, :])
                S.done(inst, "V")
            ve(_)

            def _(v):
                S.bar(v)
                for k in range(2):
                    tv = totb[0:1, k * 512:(k + 1) * 512].rearrange("o (e c) -> o e c", e=8)
                    inst = v.tensor_reduce(cntrow[0:1, k * 8:(k + 1) * 8], tv, AX.X, ALU.add)
                S.done(inst, "V")
            ve(_)

            def _(t):
                S.bar(t)
                t.matmul(P2[:, :], onesf_r[:], totb[0:1, 0:512], start=True, stop=True)
                inst = t.matmul(P3[:, :], onesf_r[:], totb[0:1, 512:1024], start=True, stop=True)
                S.done(inst, "T")
            tn(_)

            def _(v):
                S.bar(v)
                v.tensor_copy(totb[:, 0:512], P2[:, :])
                inst = v.tensor_copy(totb[:, 512:1024], P3[:, :])
                S.done(inst, "V")
            ve(_)

            def _(v):
                S.bar(v)
                for k in range(2):
                    for ee in range(8):
                        sl = slice(k * 512 + ee * 64, k * 512 + (ee + 1) * 64)
                        inst = v.tensor_tensor_scan(tscab[:, sl], onescol_b[:], totb[:, sl],
                                                    0.0, ALU.mult, ALU.add)
                S.done(inst, "V")
            ve(_)

            def _(v):
                S.bar(v)
                inst = v.tensor_sub(tscab[:], tscab[:], totb[:])
                S.done(inst, "V")
            ve(_)

            for k in range(2):
                def _(v, k=k):
                    S.bar(v)
                    dst = pos1 if k == 0 else pos2
                    psrc = P0 if k == 0 else P1
                    exv = tscab[:, k * 512:(k + 1) * 512].rearrange("p (e c) -> p c e", e=8)
                    inst = v.tensor_tensor(dst[:].rearrange("p (c e) -> p c e", e=8),
                                           psrc[:, :].rearrange("p (c e) -> p c e", e=8),
                                           exv, ALU.add)
                    S.done(inst, "V")
                ve(_)

            def _(v):
                S.bar(v)
                v.tensor_tensor(S3, pos1[:].rearrange("p (c e) -> p c e", e=8),
                                oh1[:].rearrange("p (c e) -> p c e", e=8), ALU.mult)
                inst = v.tensor_tensor(S3B, pos2[:].rearrange("p (c e) -> p c e", e=8),
                                       oh2[:].rearrange("p (c e) -> p c e", e=8), ALU.mult)
                S.done(inst, "V")
            ve(_)

            def _(v):
                S.bar(v)
                v.tensor_reduce(po1[:], S3, AX.X, ALU.add)
                inst = v.tensor_reduce(po2[:], S3B, AX.X, ALU.add)
                S.done(inst, "V")
            ve(_)

            def _(v):
                S.bar(v)
                m3 = meta_sb[:].rearrange("p (c q) -> p c q", q=8)
                for qix, src in enumerate([i1_t, i2_t, po1, po2, va1_t, va2_t, w1_t, w2_t]):
                    v.tensor_copy(m3[:, :, qix:qix + 1],
                                  src[:].rearrange("p (c u) -> p c u", u=1))
                v.tensor_scalar(out=red1[:], in0=i1_t[:], scalar1=8192.0,
                                scalar2=0.0, op0=ALU.mult, op1=ALU.add)
                v.tensor_scalar(out=mx_t[:], in0=i2_t[:], scalar1=8192.0,
                                scalar2=65536.0, op0=ALU.mult, op1=ALU.add)
                v.tensor_scalar(out=s_t[:], in0=va1_t[:], scalar1=-131200.0,
                                scalar2=131200.0, op0=ALU.mult, op1=ALU.add)
                inst = v.tensor_scalar(out=u1_t[:], in0=va2_t[:], scalar1=-131200.0,
                                       scalar2=131200.0, op0=ALU.mult, op1=ALU.add)
                S.done(inst, "V")
            ve(_)

            def _(v):
                S.bar(v)
                v.tensor_add(red1[:], red1[:], po1[:])
                inst = v.tensor_add(mx_t[:], mx_t[:], po2[:])
                S.done(inst, "V")
            ve(_)

            def _(v):
                S.bar(v)
                v.tensor_mul(red1[:], red1[:], va1_t[:])
                inst = v.tensor_mul(mx_t[:], mx_t[:], va2_t[:])
                S.done(inst, "V")
            ve(_)

            def _(v):
                S.bar(v)
                v.tensor_add(red1[:], red1[:], s_t[:])
                inst = v.tensor_add(mx_t[:], mx_t[:], u1_t[:])
                S.done(inst, "V")
            ve(_)

            def _(v):
                S.bar(v)
                v.tensor_copy(offi[:, 0:64], red1[:])
                v.tensor_copy(offi[:, 64:128], mx_t[:])
                s20 = scv[:, 0:128].rearrange("p (c u) -> p c u", u=2)
                s21 = scv[:, 128:256].rearrange("p (c u) -> p c u", u=2)
                v.tensor_copy(s20[:, :, 0:1], tokid_s[:].rearrange("p (c u) -> p c u", u=1))
                v.tensor_copy(s21[:, :, 0:1], tokid_s[:].rearrange("p (c u) -> p c u", u=1))
                v.tensor_copy(s20[:, :, 1:2], w1_t[:].rearrange("p (c u) -> p c u", u=1))
                inst = v.tensor_copy(s21[:, :, 1:2], w2_t[:].rearrange("p (c u) -> p c u", u=1))
                S.done(inst, "V")
            ve(_)

            def _(sync):
                S.bar(sync)
                inst = sync.dma_start(out=cnt_loc[:, :], in_=cntrow[:])
                inst.then_inc(S.sem["O"], 16)
                S.cnt["O"] += 16
            sy(_)

            def _(g):
                S.bar(g)
                for k in range(2):
                    for ch in range(64):
                        inst = g.indirect_dma_start(
                            smap_loc[:, :],
                            IndirectOffsetOnAxis(offi[:, k * 64 + ch:k * 64 + ch + 1], 0),
                            scv[:, k * 128 + 2 * ch:k * 128 + 2 * ch + 2], None)
                        inst.then_inc(S.sem["O"], 16)
                S.cnt["O"] += 128 * 16
            gp(_)

            def _(g):
                S.bar(g)
                g.collective_compute(
                    "AllGather", ALU.bypass, replica_groups=RG,
                    ins=[cnt_loc.ap().opt()], outs=[cnt_all.ap().opt()],
                ).then_inc(S.sem["C"])
                g.collective_compute(
                    "AllGather", ALU.bypass, replica_groups=RG,
                    ins=[smap_loc.ap().opt()], outs=[smap_all.ap().opt()],
                ).then_inc(S.sem["C"])
                g.collective_compute(
                    "AllGather", ALU.bypass, replica_groups=RG,
                    ins=[xn2_loc.ap().opt()], outs=[xn2_all.ap().opt()],
                ).then_inc(S.sem["C"])
                S.cnt["C"] += 3
            gp(_)

        with ExitStack() as ctx2:
            e2 = ctx2.enter_context
            w1_b = e2(nc.sbuf_tensor("w1_b", [128, 8 * 4096], BF16))
            w2_b = e2(nc.sbuf_tensor("w2_b", [128, 32 * 1024], BF16))
            b1_s = e2(nc.sbuf_tensor("b1_s", [128, 32], F32))
            b2lsb = e2(nc.sbuf_tensor("b2lsb", [1, 1024], BF16))
            onesf16 = e2(nc.sbuf_tensor("onesf16", [1, 16], F32))
            onesc16 = e2(nc.sbuf_tensor("onesc16", [16, 1], F32))
            tvrow = e2(nc.sbuf_tensor("tvrow", [1, 1], F32))
            gpre_sb = e2(nc.sbuf_tensor("gpre_sb", [16, 16], F32))
            ehot_sb = e2(nc.sbuf_tensor("ehot_sb", [16, 8], F32))
            base_sb = e2(nc.sbuf_tensor("base_sb", [1, 16], F32))
            gch_sb = e2(nc.sbuf_tensor("gch_sb", [16, 1], F32))
            ccol_sb = e2(nc.sbuf_tensor("ccol_sb", [16, 1], F32))
            grow_sb = e2(nc.sbuf_tensor("grow_sb", [1, 16], F32))
            gb_sb = e2(nc.sbuf_tensor("gb_sb", [128, 16], F32))
            bb_sb = e2(nc.sbuf_tensor("bb_sb", [128, 16], F32))
            jpat = e2(nc.sbuf_tensor("jpat", [128, 16], F32))
            tv_sb = e2(nc.sbuf_tensor("tv_sb", [128, 1], F32))
            slots = e2(nc.sbuf_tensor("slots", [128, 80], F32))
            bsel = e2(nc.sbuf_tensor("bsel", [128, 40 * 16], F32))
            gsel = e2(nc.sbuf_tensor("gsel", [128, 80], F32))
            srcr = e2(nc.sbuf_tensor("srcr", [128, 80], F32))
            wmask = e2(nc.sbuf_tensor("wmask", [128, 80], F32))
            srci = e2(nc.sbuf_tensor("srci", [128, 80], I32))
            tokw = e2(nc.sbuf_tensor("tokw", [128, 80 * 2], F32))
            tokf = e2(nc.sbuf_tensor("tokf", [128, 80], F32))
            wf = e2(nc.sbuf_tensor("wf", [128, 80], F32))
            toki = e2(nc.sbuf_tensor("toki", [128, 80], I32))
            ctx2b = ExitStack()
            e2b = ctx2b.enter_context
            lsb = e2b(nc.sbuf_tensor("lsb", [128, 1024], F32))
            lsr_s = e2b(nc.sbuf_tensor("lsr_s", [1, 1024], F32))
            b2_s = e2b(nc.sbuf_tensor("b2_s", [1, 1024], F32))

            PS = e2(nc.psum_tensor("PS", [16, 16], F32))
            PS3 = e2(nc.psum_tensor("PS3", [1, 16], F32))
            PT2a = e2(nc.psum_tensor("PT2a", [128, 1024], BF16))
            PT2b = e2(nc.psum_tensor("PT2b", [128, 1024], BF16))

            def _(sync):
                S.bar(sync)
                sync.dma_start(out=cnt_sb[:],
                               in_=cnt_all.rearrange("c (k e) -> (c k) e", e=8)[:, :]
                               ).then_inc(S.sem["Q"], 16)
                sync.dma_start(out=gpre_sb[:], in_=c_gpre[:, :]).then_inc(S.sem["Q"], 16)
                sync.dma_start(out=ehot_sb[0:1, :], in_=c_ehot[:, :]).then_inc(S.sem["Q"], 16)
                sync.dma_start(out=base_sb[:], in_=c_base[:, :]).then_inc(S.sem["Q"], 16)
                sync.dma_start(out=b1_s[:], in_=b1c[:, :]).then_inc(S.sem["Q"], 16)
                sync.dma_start(out=lsr_s[:], in_=lsr[:, :]).then_inc(S.sem["Q"], 16)
                sync.dma_start(out=b2_s[:], in_=b2r[:, :]).then_inc(S.sem["Q"], 16)
                S.cnt["Q"] += 7 * 16
            sy(_)

            def _(g):
                S.bar(g)
                for ci in range(8):
                    g.dma_start(out=w1_b[:, ci * 4096:(ci + 1) * 4096],
                                in_=w1t[ci * 128:(ci + 1) * 128, :]).then_inc(S.sem["P"], 16)
                for hi in range(32):
                    g.dma_start(out=w2_b[:, hi * 1024:(hi + 1) * 1024],
                                in_=w2t[hi * 128:(hi + 1) * 128, :]).then_inc(S.sem["P"], 16)
                S.cnt["P"] += 40 * 16
                g.iota(slots[:], [[128, 80]], base=0, channel_multiplier=1,
                       allow_small_or_imprecise_dtypes=True).then_inc(S.sem["G"], 1)
                g.iota(jpat[:], [[0, 1], [1, 16]], base=0, channel_multiplier=0,
                       allow_small_or_imprecise_dtypes=True).then_inc(S.sem["G"], 1)
                S.cnt["G"] += 2
            gp(_)

            def _(v):
                S.bar(v)
                v.memset(onesf16[:], 1.0)
                v.memset(onesc16[:], 1.0)
                inst = v.tensor_mul(b2_s[:], b2_s[:], lsr_s[:])
                S.done(inst, "V")
            ve(_)

            def _(v):
                S.bar(v)
                inst = v.tensor_copy(b2lsb[:], b2_s[:])
                S.done(inst, "V")
            ve(_)

            def _(t):
                S.bar(t)
                t.matmul(P0[:, :], onesf_r[:], lsr_s[0:1, 0:512], start=True, stop=True)
                t.matmul(P1[:, :], onesf_r[:], lsr_s[0:1, 512:1024], start=True, stop=True)
                inst = t.matmul(PS[0:16, 0:8], onesf16[:], ehot_sb[0:1, :], start=True, stop=True)
                S.done(inst, "T")
            tn(_)

            def _(v):
                S.bar(v)
                v.tensor_copy(lsb[:, 0:512], P0[:, :])
                v.tensor_copy(lsb[:, 512:1024], P1[:, :])
                inst = v.tensor_copy(ehot_sb[:, :], PS[0:16, 0:8])
                S.done(inst, "V")
            ve(_)

            def _(v):
                S.bar(v)
                for hi in range(32):
                    inst = v.tensor_mul(w2_b[:, hi * 1024:(hi + 1) * 1024],
                                        w2_b[:, hi * 1024:(hi + 1) * 1024], lsb[:])
                S.done(inst, "V")
            ve(_)

            def _(t):
                S.bar(t)
                inst = t.matmul(PS[:, 0:8], gpre_sb[:], cnt_sb[:], start=True, stop=True)
                S.done(inst, "T")
            tn(_)

            def _(v):
                S.bar(v)
                v.tensor_tensor(bsel[0:16, 0:8], PS[:, 0:8], ehot_sb[:, :], ALU.mult)
                inst = v.tensor_tensor(bsel[0:16, 8:16], cnt_sb[:], ehot_sb[:, :], ALU.mult)
                S.done(inst, "V")
            ve(_)

            def _(v):
                S.bar(v)
                v.tensor_reduce(gch_sb[:], bsel[0:16, 0:8].rearrange("p (u e) -> p u e", u=1),
                                AX.X, ALU.add)
                inst = v.tensor_reduce(ccol_sb[:], bsel[0:16, 8:16].rearrange("p (u e) -> p u e", u=1),
                                       AX.X, ALU.add)
                S.done(inst, "V")
            ve(_)

            def _(t):
                S.bar(t)
                inst = t.matmul(PS3[0:1, 0:1], onesc16[:], ccol_sb[:], start=True, stop=True)
                S.done(inst, "T")
            tn(_)

            def _(v):
                S.bar(v)
                inst = v.tensor_copy(tvrow[:], PS3[0:1, 0:1])
                S.done(inst, "V")
            ve(_)

            def _(t):
                S.bar(t)
                inst = t.matmul(P0[:, 0:1], onesf_r[:], tvrow[0:1, :], start=True, stop=True)
                S.done(inst, "T")
            tn(_)

            def _(v):
                S.bar(v)
                inst = v.tensor_copy(tv_sb[:], P0[:, 0:1])
                S.done(inst, "V")
            ve(_)

            def _(sync):
                S.bar(sync)
                sync.dma_start(out=gscr[:, :], in_=gch_sb[:]).then_inc(S.sem["Q"], 16)
                S.cnt["Q"] += 16
            sy(_)

            def _(sync):
                S.bar(sync)
                inst = sync.dma_start(out=grow_sb[:], in_=gscr.rearrange("r u -> u r")[:, :],
                                      allow_slow_non_contiguous=True)
                inst.then_inc(S.sem["Q"], 16)
                S.cnt["Q"] += 16
            sy(_)

            def _(t):
                S.bar(t)
                t.matmul(P0[:, 0:16], onesf_r[:], grow_sb[0:1, :], start=True, stop=True)
                inst = t.matmul(P1[:, 0:16], onesf_r[:], base_sb[0:1, :], start=True, stop=True)
                S.done(inst, "T")
            tn(_)

            def _(v):
                S.bar(v)
                v.tensor_copy(gb_sb[:], P0[:, 0:16])
                inst = v.tensor_copy(bb_sb[:], P1[:, 0:16])
                S.done(inst, "V")
            ve(_)

            for hh in range(2):
                cs = slice(hh * 40, (hh + 1) * 40)
                b3 = bsel[:].rearrange("p (c j) -> p c j", j=16)
                sl3 = slots[:, cs].rearrange("p (c u) -> p c u", u=1).broadcast_to([128, 40, 16])
                g3 = gb_sb[:].rearrange("p (u j) -> p u j", u=1).broadcast_to([128, 40, 16])
                j3 = jpat[:].rearrange("p (u j) -> p u j", u=1).broadcast_to([128, 40, 16])
                i3 = srcr[:, cs].rearrange("p (c u) -> p c u", u=1).broadcast_to([128, 40, 16])
                bb3 = bb_sb[:].rearrange("p (u j) -> p u j", u=1).broadcast_to([128, 40, 16])

                def _(v, b3=b3, sl3=sl3, g3=g3):
                    S.bar(v)
                    inst = v.tensor_tensor(b3, sl3, g3, ALU.is_ge)
                    S.done(inst, "V")
                ve(_)

                def _(v, b3=b3, cs=cs):
                    S.bar(v)
                    inst = v.tensor_reduce(srcr[:, cs], b3, AX.X, ALU.add)
                    S.done(inst, "V")
                ve(_)

                def _(v, b3=b3, g3=g3, cs=cs):
                    S.bar(v)
                    v.tensor_scalar_add(srcr[:, cs], srcr[:, cs], -1.0)
                    inst = v.tensor_tensor(b3, b3, g3, ALU.mult)
                    S.done(inst, "V")
                ve(_)

                def _(v, b3=b3, j3=j3, i3=i3, cs=cs):
                    S.bar(v)
                    v.tensor_reduce(gsel[:, cs], b3, AX.X, ALU.max)
                    inst = v.tensor_tensor(b3, j3, i3, ALU.is_equal)
                    S.done(inst, "V")
                ve(_)

                def _(v, b3=b3, bb3=bb3):
                    S.bar(v)
                    inst = v.tensor_tensor(b3, b3, bb3, ALU.mult)
                    S.done(inst, "V")
                ve(_)

                def _(v, b3=b3, cs=cs):
                    S.bar(v)
                    inst = v.tensor_reduce(wmask[:, cs], b3, AX.X, ALU.add)
                    S.done(inst, "V")
                ve(_)

            def _(v):
                S.bar(v)
                inst = v.tensor_sub(gsel[:], slots[:], gsel[:])
                S.done(inst, "V")
            ve(_)

            def _(v):
                S.bar(v)
                inst = v.tensor_add(srcr[:], wmask[:], gsel[:])
                S.done(inst, "V")
            ve(_)

            def _(v):
                S.bar(v)
                v.tensor_scalar(out=wmask[:], in0=slots[:], scalar1=tv_sb[:, 0:1],
                                scalar2=None, op0=ALU.is_lt)
                inst = v.tensor_single_scalar(tokf[:], slots[:], float(CAPACITY), ALU.is_lt)
                S.done(inst, "V")
            ve(_)

            def _(v):
                S.bar(v)
                inst = v.tensor_mul(wmask[:], wmask[:], tokf[:])
                S.done(inst, "V")
            ve(_)

            def _(v):
                S.bar(v)
                inst = v.tensor_mul(srcr[:], srcr[:], wmask[:])
                S.done(inst, "V")
            ve(_)

            def _(v):
                S.bar(v)
                inst = v.tensor_copy(srci[:], srcr[:])
                S.done(inst, "V")
            ve(_)

            def _(g):
                S.bar(g)
                for ch in range(80):
                    inst = g.indirect_dma_start(
                        tokw[:, 2 * ch:2 * ch + 2], None,
                        smap_all[:, :], IndirectOffsetOnAxis(srci[:, ch:ch + 1], 0))
                    inst.then_inc(S.sem["P"], 16)
                S.cnt["P"] += 80 * 16
            gp(_)

            def _(v):
                S.bar(v)
                t3 = tokw[:].rearrange("p (c u) -> p c u", u=2)
                w3 = wmask[:].rearrange("p (c u) -> p c u", u=1)
                v.tensor_tensor(tokf[:].rearrange("p (c u) -> p c u", u=1), t3[:, :, 0:1], w3, ALU.mult)
                inst = v.tensor_tensor(wf[:].rearrange("p (c u) -> p c u", u=1), t3[:, :, 1:2], w3, ALU.mult)
                S.done(inst, "V")
            ve(_)

            def _(v):
                S.bar(v)
                inst = v.tensor_copy(toki[:], tokf[:])
                S.done(inst, "V")
            ve(_)

            ctx2b.close()
            h_b = e2(nc.sbuf_tensor("h_b", [128, 32 * 512], BF16))
            toks0 = e2(nc.sbuf_tensor("toks0", [128, 4096], BF16))
            toks1 = e2(nc.sbuf_tensor("toks1", [128, 4096], BF16))
            tfm = e2(nc.sbuf_tensor("tfm", [128, 8 * 512], BF16))
            sres = e2(nc.sbuf_tensor("sres", [128, 4 * 1024], BF16))

            NGRP = 20
            T0 = S.cnt["T"]
            A0 = S.cnt["A"]
            V0 = S.cnt["V"]
            O0 = S.cnt["O"]

            def t_round_m(g, r):
                return T0 + g * 44 + r + 1

            def t_fc1_m(g, hi):
                return T0 + g * 44 + 4 + hi + 1

            def t_fc2_m(g, s):
                return T0 + g * 44 + 36 + s + 1

            def a_gelu_m(g, hi):
                return A0 + g * 32 + hi + 1

            def v_evac_m(g, r):
                return V0 + g * 12 + r + 1

            def v_epi_m(g, s):
                return V0 + g * 12 + 4 + s + 1

            def _(g):
                S.bar(g)
                for j in range(4):
                    inst = g.indirect_dma_start(
                        toks0[:, j * 1024:(j + 1) * 1024], None,
                        xn2_all[:, :],
                        IndirectOffsetOnAxis(toki[:, j:j + 1], 0))
                    inst.then_inc(S.sem["P"], 16)
                S.cnt["P"] += 4 * 16
            gp(_)
            P0c = S.cnt["P"]

            def p_gath_m(g):
                return P0c + g * 64

            def o_store_m(g):
                return O0 + (g + 1) * 64

            for grp in range(NGRP):
                tok_cur = toks0 if grp % 2 == 0 else toks1
                tok_nxt = toks1 if grp % 2 == 0 else toks0

                if grp + 1 < NGRP:
                    def _(g, grp=grp, tok_nxt=tok_nxt):
                        if grp >= 1:
                            g.wait_ge(S.sem["V"], v_epi_m(grp - 1, 7))
                        for j in range(4):
                            inst = g.indirect_dma_start(
                                tok_nxt[:, j * 1024:(j + 1) * 1024], None,
                                xn2_all[:, :],
                                IndirectOffsetOnAxis(
                                    toki[:, (grp + 1) * 4 + j:(grp + 1) * 4 + j + 1], 0))
                            inst.then_inc(S.sem["P"], 16)
                        S.cnt["P"] += 4 * 16
                    gp(_)

                def _(t, grp=grp, tok_cur=tok_cur):
                    t.wait_ge(S.sem["P"], p_gath_m(grp))
                    for r in range(4):
                        gi = grp * 4 + r
                        if gi >= 2:
                            g2, r2 = divmod(gi - 2, 4)
                            t.wait_ge(S.sem["V"], v_evac_m(g2, r2))
                        pt = PT2a if gi % 2 == 0 else PT2b
                        for ci in range(8):
                            inst = t.transpose(
                                pt[:, ci * 128:(ci + 1) * 128],
                                tok_cur[:, r * 1024 + ci * 128:r * 1024 + (ci + 1) * 128],
                                idb_s[:])
                        S.done(inst, "T")
                        assert S.cnt["T"] == t_round_m(grp, r)
                tn(_)

                def _(v, grp=grp):
                    for r in range(4):
                        gi = grp * 4 + r
                        v.wait_ge(S.sem["T"], t_round_m(grp, r))
                        pt = PT2a if gi % 2 == 0 else PT2b
                        for ci in range(8):
                            inst = v.tensor_copy(
                                tfm[:, ci * 512 + r * 128:ci * 512 + (r + 1) * 128],
                                pt[:, ci * 128:(ci + 1) * 128])
                        S.done(inst, "V")
                        assert S.cnt["V"] == v_evac_m(grp, r)
                ve(_)

                def _(t, grp=grp):
                    t.wait_ge(S.sem["V"], v_evac_m(grp, 3))
                    for hi in range(32):
                        b = grp * 32 + hi
                        if b >= 2:
                            g2, h2 = divmod(b - 2, 32)
                            t.wait_ge(S.sem["A"], a_gelu_m(g2, h2))
                        p = P0 if hi % 2 == 0 else P1
                        for ci in range(8):
                            inst = t.matmul(p[:, :], w1_b[:, ci * 4096 + hi * 128:
                                                          ci * 4096 + (hi + 1) * 128],
                                            tfm[:, ci * 512:(ci + 1) * 512],
                                            start=(ci == 0), stop=(ci == 7))
                        S.done(inst, "T")
                        assert S.cnt["T"] == t_fc1_m(grp, hi)
                    t.wait_ge(S.sem["A"], a_gelu_m(grp, 31))
                    for s in range(8):
                        j, hf = divmod(s, 2)
                        c = grp * 8 + s
                        if c >= 2:
                            g2, s2 = divmod(c - 2, 8)
                            t.wait_ge(S.sem["V"], v_epi_m(g2, s2))
                        p = P2 if s % 2 == 0 else P3
                        for hi in range(32):
                            t.matmul(p[:, :],
                                     h_b[:, hi * 512 + j * 128:hi * 512 + (j + 1) * 128],
                                     w2_b[:, hi * 1024 + hf * 512:hi * 1024 + (hf + 1) * 512],
                                     start=(hi == 0), stop=False)
                        inst = t.matmul(p[:, :], ones_r[:],
                                        b2lsb[0:1, hf * 512:(hf + 1) * 512],
                                        start=False, stop=True)
                        S.done(inst, "T")
                        assert S.cnt["T"] == t_fc2_m(grp, s)
                tn(_)

                def _(s_, grp=grp):
                    for hi in range(32):
                        s_.wait_ge(S.sem["T"], t_fc1_m(grp, hi))
                        p = P0 if hi % 2 == 0 else P1
                        inst = s_.activation(out=h_b[:, hi * 512:(hi + 1) * 512], in_=p[:, :],
                                             func=AF.Gelu, bias=b1_s[:, hi:hi + 1])
                        S.done(inst, "A")
                        assert S.cnt["A"] == a_gelu_m(grp, hi)
                sc(_)

                def _(v, grp=grp, tok_cur=tok_cur):
                    for s in range(8):
                        j, hf = divmod(s, 2)
                        v.wait_ge(S.sem["T"], t_fc2_m(grp, s))
                        if s == 0 and grp >= 1:
                            v.wait_ge(S.sem["O"], o_store_m(grp - 1))
                        p = P2 if s % 2 == 0 else P3
                        sl = slice(j * 1024 + hf * 512, j * 1024 + (hf + 1) * 512)
                        v.tensor_add(bsel[:, 0:512], p[:, :], tok_cur[:, sl])
                        inst = v.tensor_scalar(out=sres[:, sl], in0=bsel[:, 0:512],
                                               scalar1=wf[:, grp * 4 + j:grp * 4 + j + 1],
                                               scalar2=None, op0=ALU.mult)
                        S.done(inst, "V")
                        assert S.cnt["V"] == v_epi_m(grp, s)
                ve(_)

                def _(sync, grp=grp):
                    sync.wait_ge(S.sem["V"], v_epi_m(grp, 7))
                    for j in range(4):
                        r0 = grp * 512 + j * 128
                        inst = sync.dma_start(out=sres_loc[r0:r0 + 128, :],
                                              in_=sres[:, j * 1024:(j + 1) * 1024])
                        inst.then_inc(S.sem["O"], 16)
                    S.cnt["O"] += 4 * 16
                    assert S.cnt["O"] == o_store_m(grp)
                sy(_)

            def _(g):
                S.bar(g)
                g.collective_compute(
                    "AllGather", ALU.bypass, replica_groups=RG,
                    ins=[sres_loc.ap().opt()], outs=[sres_all.ap().opt()],
                ).then_inc(S.sem["C"])
                S.cnt["C"] += 1
            gp(_)

        with ExitStack() as ctx3:
            e3x = ctx3.enter_context
            s16_sb = e3x(nc.sbuf_tensor("s16_sb", [16, 2], F32))
            g1row = e3x(nc.sbuf_tensor("g1row", [1, 8], F32))
            g2row = e3x(nc.sbuf_tensor("g2row", [1, 8], F32))
            gb1 = e3x(nc.sbuf_tensor("gb1", [128, 8], F32))
            gb2 = e3x(nc.sbuf_tensor("gb2", [128, 8], F32))
            epat3 = e3x(nc.sbuf_tensor("epat3", [128, 512], F32))
            scr3 = e3x(nc.sbuf_tensor("scr3", [128, 512], F32))
            scrb3 = e3x(nc.sbuf_tensor("scrb3", [128, 512], F32))
            keep3 = e3x(nc.sbuf_tensor("keep3", [128, 64], F32))
            off1 = e3x(nc.sbuf_tensor("off1", [128, 64], F32))
            off2 = e3x(nc.sbuf_tensor("off2", [128, 64], F32))
            oi1 = e3x(nc.sbuf_tensor("oi1", [128, 64], I32))
            oi2 = e3x(nc.sbuf_tensor("oi2", [128, 64], I32))
            gt = e3x(nc.sbuf_tensor("gt", [128, 1024], BF16))
            x2tile = e3x(nc.sbuf_tensor("x2tile", [128, 1024], BF16))
            ytile = e3x(nc.sbuf_tensor("ytile", [128, 1024], F32))
            PB = e3x(nc.psum_tensor("PB", [128, 16], F32))
            PSb = e3x(nc.psum_tensor("PSb", [1, 16], F32))

            def _(sync):
                S.bar(sync)
                sync.dma_start(out=s16_sb[:], in_=s16own[:, :]).then_inc(S.sem["Q"], 16)
                S.cnt["Q"] += 16
            sy(_)

            def _(g):
                S.bar(g)
                inst = g.iota(epat3[:], [[0, 64], [1, 8]], base=0, channel_multiplier=0,
                              allow_small_or_imprecise_dtypes=True)
                inst.then_inc(S.sem["G"], 1)
                S.cnt["G"] += 1
            gp(_)

            def _(t):
                S.bar(t)
                inst = t.matmul(PSb[0:1, 0:8], s16_sb[:, 0:1], cnt_sb[:], start=True, stop=True)
                S.done(inst, "T")
            tn(_)

            def _(v):
                S.bar(v)
                inst = v.tensor_copy(g1row[:], PSb[0:1, 0:8])
                S.done(inst, "V")
            ve(_)

            def _(t):
                S.bar(t)
                inst = t.matmul(PSb[0:1, 0:8], s16_sb[:, 1:2], cnt_sb[:], start=True, stop=True)
                S.done(inst, "T")
            tn(_)

            def _(v):
                S.bar(v)
                inst = v.tensor_copy(g2row[:], PSb[0:1, 0:8])
                S.done(inst, "V")
            ve(_)

            def _(t):
                S.bar(t)
                t.matmul(PB[:, 0:8], onesf_r[:], g1row[0:1, :], start=True, stop=False)
                inst = t.matmul(PB[:, 8:16], onesf_r[:], g2row[0:1, :], start=False, stop=True)
                S.done(inst, "T")
            tn(_)

            def _(v):
                S.bar(v)
                v.tensor_copy(gb1[:], PB[:, 0:8])
                inst = v.tensor_copy(gb2[:], PB[:, 8:16])
                S.done(inst, "V")
            ve(_)

            for k in range(2):
                m3 = meta_sb[:].rearrange("p (c q) -> p c q", q=8)
                e3 = epat3[:].rearrange("p (c e) -> p c e", e=8)
                s3 = scr3[:].rearrange("p (c e) -> p c e", e=8)
                s3b = scrb3[:].rearrange("p (c e) -> p c e", e=8)
                idx = m3[:, :, k:k + 1]
                po = m3[:, :, 2 + k:3 + k]
                va = m3[:, :, 4 + k:5 + k]
                gb = gb1 if k == 0 else gb2
                off = off1 if k == 0 else off2
                oi = oi1 if k == 0 else oi2
                o3 = off[:].rearrange("p (c u) -> p c u", u=1)
                k3 = keep3[:].rearrange("p (c u) -> p c u", u=1)

                def _(v, s3=s3, e3=e3, idx=idx):
                    S.bar(v)
                    inst = v.tensor_tensor(s3, e3, idx.broadcast_to([128, 64, 8]),
                                           ALU.is_equal)
                    S.done(inst, "V")
                ve(_)

                def _(v, s3=s3, s3b=s3b, gb=gb):
                    S.bar(v)
                    inst = v.tensor_tensor(s3b, s3,
                                           gb[:].rearrange("p (u e) -> p u e", u=1)
                                           .broadcast_to([128, 64, 8]), ALU.mult)
                    S.done(inst, "V")
                ve(_)

                def _(v, off=off, s3b=s3b):
                    S.bar(v)
                    inst = v.tensor_reduce(off[:], s3b, AX.X, ALU.add)
                    S.done(inst, "V")
                ve(_)

                def _(v, o3=o3, po=po, idx=idx):
                    S.bar(v)
                    v.tensor_tensor(o3, o3, po, ALU.add)
                    inst = v.tensor_scalar(
                        out=scrb3[:, 0:64].rearrange("p (c u) -> p c u", u=1), in0=idx,
                        scalar1=float(CPAD), scalar2=None, op0=ALU.mult)
                    S.done(inst, "V")
                ve(_)

                def _(v, off=off):
                    S.bar(v)
                    v.tensor_single_scalar(keep3[:], off[:], float(CAPACITY), ALU.is_lt)
                    inst = v.tensor_add(off[:], off[:], scrb3[:, 0:64])
                    S.done(inst, "V")
                ve(_)

                def _(v, k3=k3, va=va):
                    S.bar(v)
                    inst = v.tensor_tensor(k3, k3, va, ALU.mult)
                    S.done(inst, "V")
                ve(_)

                def _(v, off=off):
                    S.bar(v)
                    v.tensor_mul(off[:], off[:], keep3[:])
                    inst = v.tensor_scalar(out=scrb3[:, 64:128], in0=keep3[:],
                                           scalar1=-float(CPAD - 1), scalar2=float(CPAD - 1),
                                           op0=ALU.mult, op1=ALU.add)
                    S.done(inst, "V")
                ve(_)

                def _(v, off=off):
                    S.bar(v)
                    inst = v.tensor_add(off[:], off[:], scrb3[:, 64:128])
                    S.done(inst, "V")
                ve(_)

                def _(v, oi=oi, off=off):
                    S.bar(v)
                    inst = v.tensor_copy(oi[:], off[:])
                    S.done(inst, "V")
                ve(_)

            for ch in range(64):
                def _(g, ch=ch):
                    S.bar(g)
                    g.indirect_dma_start(
                        gt[:, :], None,
                        sres_all[:, :], IndirectOffsetOnAxis(oi1[:, ch:ch + 1], 0)
                    ).then_inc(S.sem["P"], 16)
                    S.cnt["P"] += 16
                    g.wait_ge(S.sem["P"], S.cnt["P"])
                    inst = g.indirect_dma_start(
                        gt[:, :], None,
                        sres_all[:, :], IndirectOffsetOnAxis(oi2[:, ch:ch + 1], 0),
                        compute_op=ALU.add)
                    inst.then_inc(S.sem["P"], 16)
                    S.cnt["P"] += 16
                gp(_)

                def _(sync, ch=ch):
                    S.bar(sync, skip=("P",))
                    inst = sync.dma_start(out=x2tile[:], in_=x2_loc[ch * 128:(ch + 1) * 128, :])
                    inst.then_inc(S.sem["Q"], 16)
                    S.cnt["Q"] += 16
                sy(_)

                def _(v, ch=ch):
                    S.bar(v)
                    inst = v.tensor_add(ytile[:], gt[:], x2tile[:])
                    S.done(inst, "V")
                ve(_)

                def _(sync, ch=ch):
                    S.bar(sync)
                    inst = sync.dma_start(out=yout[ch * 128:(ch + 1) * 128, :], in_=ytile[:])
                    inst.then_inc(S.sem["O"], 16)
                    S.cnt["O"] += 16
                sy(_)

            def _(sync):
                S.bar(sync)
                sync.nop()
            sy(_)
    return nc



_NC_CACHE = {}


def _get_nc(name, builder):
    if name not in _NC_CACHE:
        _NC_CACHE[name] = builder()
    return _NC_CACHE[name]


def kernel(L0_attn_x, L0_attn_W_qkv, L0_attn_RF, L0_attn_W_out, L0_attn_b_out,
           L0_ffn_router_w, L0_ffn_router_b, L0_ffn_fc1_w, L0_ffn_fc1_b,
           L0_ffn_fc2_w, L0_ffn_fc2_b, L0_ffn_layer_scale):
    x = np.asarray(L0_attn_x)
    wqkv = np.ascontiguousarray(np.asarray(L0_attn_W_qkv))
    rf = np.asarray(L0_attn_RF)
    wout = np.ascontiguousarray(np.asarray(L0_attn_W_out))
    bout = np.asarray(L0_attn_b_out)
    rw = np.asarray(L0_ffn_router_w)
    rb = np.asarray(L0_ffn_router_b)
    fc1w = np.asarray(L0_ffn_fc1_w)
    fc1b = np.asarray(L0_ffn_fc1_b)
    fc2w = np.asarray(L0_ffn_fc2_w)
    fc2b = np.asarray(L0_ffn_fc2_b)
    ls = np.asarray(L0_ffn_layer_scale)

    rfbd = np.zeros((8, 128, 128), np.float32)
    for pr in range(8):
        rfbd[pr, 0:64, 0:64] = rf[2 * pr]
        rfbd[pr, 64:128, 64:128] = rf[2 * pr + 1]
    rfbd = rfbd.reshape(8 * 128, 128)
    lt = np.triu(np.ones((128, 128), np.float32), 1).astype(bfdt)
    idb = np.eye(128, dtype=np.float32).astype(bfdt)
    idf = np.eye(128, dtype=np.float32)
    boutT = np.ascontiguousarray(bout.reshape(8, 128).T)
    rwt = np.ascontiguousarray(rw.T)
    rbrow = np.ascontiguousarray(rb.reshape(1, 8))

    gpre = np.zeros((16, 16), np.float32)
    for cp in range(8):
        for kp in range(2):
            for k in range(2):
                for c in range(8):
                    if (kp < k) or (kp == k and cp < c):
                        gpre[cp * 2 + kp, k * 8 + c] = 1.0

    nc = _get_nc("FUSED", build_fused)
    ins = []
    for c in range(8):
        xs = np.ascontiguousarray(x[2 * c:2 * c + 2].reshape(TPC, DM).T)
        tok = (c * TPC + np.arange(64)[None, :] * 128 +
               np.arange(128)[:, None]).astype(np.float32)
        basr = np.zeros((1, 16), np.float32)
        for k in range(2):
            for cc in range(8):
                basr[0, k * 8 + cc] = cc * SLOTROWS + (k * 8 + c) * 8192
        ehot = np.zeros((1, 8), np.float32)
        ehot[0, c] = 1.0
        s16o = np.zeros((16, 2), np.float32)
        for cp in range(8):
            for kp in range(2):
                for k in range(2):
                    if (kp < k) or (kp == k and cp < c):
                        s16o[cp * 2 + kp, k] = 1.0
        ins.append({
            "xfm": xs, "wqkv": wqkv, "wout": wout, "boutT": boutT,
            "rfbd": rfbd, "rwt": rwt, "rbrow": rbrow, "tokid": tok,
            "c_lt": lt, "c_idb": idb, "c_idf": idf,
            "w1t": np.ascontiguousarray(fc1w[c].T),
            "w2t": np.ascontiguousarray(fc2w[c].T),
            "b1c": np.ascontiguousarray(fc1b[c].reshape(32, 128).T),
            "lsr": np.ascontiguousarray(ls[c:c + 1]),
            "b2r": np.ascontiguousarray(fc2b[c:c + 1]),
            "c_gpre": gpre, "c_ehot": ehot, "c_base": basr,
            "s16own": s16o,
        })
    res = run_bass_kernel_spmd(nc, ins, core_ids=list(range(8))).results

    y = np.concatenate([np.asarray(res[c]["yout"]) for c in range(8)], axis=0)
    return y.reshape(B_GLOBAL, SEQ, DM)


# version 08
# speedup vs baseline: 1.6449x; 1.4879x over previous
from contextlib import ExitStack

import numpy as np
import ml_dtypes
import concourse.bass as bass
import concourse.mybir as mybir
from concourse.bass_utils import run_bass_kernel_spmd
from concourse.bass import IndirectOffsetOnAxis

F32 = mybir.dt.float32
BF16 = mybir.dt.bfloat16
I32 = mybir.dt.int32
AF = mybir.ActivationFunctionType
ALU = mybir.AluOpType
AX = mybir.AxisListType

B_GLOBAL = 16
SEQ = 4096
DM = 1024
TPC = 8192
NBATCH = 2
N_TOK = B_GLOBAL * SEQ
CAPACITY = int(1.2 * (N_TOK // 8 + 1))
CPAD = 10240
HID = 4096
SLOTROWS = 131328
STAB = 1e-6
LN_EPS = 1e-5
REPS = 1e-9
RG = [list(range(8))]
PAR_A = False
PAR_B = False

bfdt = ml_dtypes.bfloat16


class K:

    def __init__(self, nc, ctx, block):
        self.nc = nc
        self.block = block
        self.sem = {}
        self.cnt = {}
        for name in ("Q", "P", "T", "A", "V", "O", "G", "C"):
            self.sem[name] = ctx.enter_context(nc.semaphore("s" + name))
            self.cnt[name] = 0

    def bar(self, eng, skip=()):
        for name, h in self.sem.items():
            if name in skip:
                continue
            if self.cnt[name] > 0:
                eng.wait_ge(h, self.cnt[name])

    def done(self, inst, q, k=1):
        inst.then_inc(self.sem[q], k)
        self.cnt[q] += k


def build_fused():
    nc = bass.Bass("TRN2", target_bir_lowering=False, debug=False)
    xfm = nc.dram_tensor("xfm", [DM, TPC], F32, kind="ExternalInput")
    wqkv = nc.dram_tensor("wqkv", [DM, 3 * DM], F32, kind="ExternalInput")
    wout = nc.dram_tensor("wout", [DM, DM], F32, kind="ExternalInput")
    boutT = nc.dram_tensor("boutT", [128, 8], F32, kind="ExternalInput")
    rfbd = nc.dram_tensor("rfbd", [8 * 128, 128], F32, kind="ExternalInput")
    rwt = nc.dram_tensor("rwt", [DM, 8], F32, kind="ExternalInput")
    rbrow = nc.dram_tensor("rbrow", [1, 8], F32, kind="ExternalInput")
    tokid = nc.dram_tensor("tokid", [128, 64], F32, kind="ExternalInput")
    c_lt = nc.dram_tensor("c_lt", [128, 128], BF16, kind="ExternalInput")
    c_idb = nc.dram_tensor("c_idb", [128, 128], BF16, kind="ExternalInput")
    c_idf = nc.dram_tensor("c_idf", [128, 128], F32, kind="ExternalInput")
    w1t = nc.dram_tensor("w1t", [DM, HID], F32, kind="ExternalInput")
    w2t = nc.dram_tensor("w2t", [HID, DM], F32, kind="ExternalInput")
    b1c = nc.dram_tensor("b1c", [128, 32], F32, kind="ExternalInput")
    lsr = nc.dram_tensor("lsr", [1, DM], F32, kind="ExternalInput")
    b2r = nc.dram_tensor("b2r", [1, DM], F32, kind="ExternalInput")
    c_gpre = nc.dram_tensor("c_gpre", [16, 16], F32, kind="ExternalInput")
    c_ehot = nc.dram_tensor("c_ehot", [1, 8], F32, kind="ExternalInput")
    c_base = nc.dram_tensor("c_base", [1, 16], F32, kind="ExternalInput")
    s16own = nc.dram_tensor("s16own", [16, 2], F32, kind="ExternalInput")
    xn2_loc = nc.dram_tensor("xn2_loc", [TPC, DM], BF16)
    xn2_all = nc.dram_tensor("xn2_all", [N_TOK, DM], BF16)
    x2_loc = nc.dram_tensor("x2_loc", [TPC, DM], BF16)
    smap_loc = nc.dram_tensor("smap_loc", [SLOTROWS, 2], F32)
    smap_all = nc.dram_tensor("smap_all", [8 * SLOTROWS, 2], F32)
    cnt_loc = nc.dram_tensor("cnt_loc", [1, 16], F32)
    cnt_all = nc.dram_tensor("cnt_all", [8, 16], F32)
    sres_loc = nc.dram_tensor("sres_loc", [CPAD, DM], BF16)
    sres_all = nc.dram_tensor("sres_all", [8 * CPAD, DM], BF16)
    gscr = nc.dram_tensor("gscr", [16, 1], F32)
    yout = nc.dram_tensor("yout", [TPC, DM], F32, kind="ExternalOutput")

    with ExitStack() as ctx:
        e = ctx.enter_context
        idb_s = e(nc.sbuf_tensor("idb_s", [128, 128], BF16))
        ones_c = e(nc.sbuf_tensor("ones_c", [128, 1], BF16))
        ones_r = e(nc.sbuf_tensor("ones_r", [1, 128], BF16))
        onesf_r = e(nc.sbuf_tensor("onesf_r", [1, 128], F32))
        meta_sb = e(nc.sbuf_tensor("meta_sb", [128, 512], F32))
        cnt_sb = e(nc.sbuf_tensor("cnt_sb", [16, 8], F32))
        P0 = e(nc.psum_tensor("P0", [128, 512], F32))
        P1 = e(nc.psum_tensor("P1", [128, 512], F32))
        P2 = e(nc.psum_tensor("P2", [128, 512], F32))
        P3 = e(nc.psum_tensor("P3", [128, 512], F32))

        block = e(nc.Block())
        S = K(nc, ctx, block)

        def gp(f):
            block.gpsimd(f)

        def tn(f):
            block.tensor(f)

        def sc(f):
            block.scalar(f)

        def ve(f):
            block.vector(f)

        def sy(f):
            block.sync(f)

        with ExitStack() as ctx1:
            e1 = ctx1.enter_context
            wq_b = e1(nc.sbuf_tensor("wq_b", [128, 8 * 2048], BF16))
            wo_b = e1(nc.sbuf_tensor("wo_b", [128, 8 * 1024], BF16))
            rf_b = e1(nc.sbuf_tensor("rf_b", [128, 8 * 128], BF16))
            rw_b = e1(nc.sbuf_tensor("rw_b", [128, 64], BF16))
            bout_s = e1(nc.sbuf_tensor("bout_s", [128, 8], F32))
            rb_s = e1(nc.sbuf_tensor("rb_s", [128, 8], F32))
            lt_s = e1(nc.sbuf_tensor("lt_s", [128, 128], BF16))
            idf_s = e1(nc.sbuf_tensor("idf_s", [128, 128], F32))
            epscol = e1(nc.sbuf_tensor("epscol", [128, 1], F32))
            stabcol = e1(nc.sbuf_tensor("stabcol", [128, 1], F32))
            tokid_s = e1(nc.sbuf_tensor("tokid_s", [128, 64], F32))
            zrow2 = e1(nc.sbuf_tensor("zrow2", [1, 2], F32))

            xbf0 = e1(nc.sbuf_tensor("xbf0", [128, 8 * 512], BF16))
            xbf1 = e1(nc.sbuf_tensor("xbf1", [128, 8 * 512], BF16))
            xsq = e1(nc.sbuf_tensor("xsq", [128, 8 * 512], BF16))
            xn1 = e1(nc.sbuf_tensor("xn1", [128, 8 * 512], BF16))

            mt = e1(nc.sbuf_tensor("mt", [1, 512], F32))
            vt = e1(nc.sbuf_tensor("vt", [1, 512], F32))
            rt = e1(nc.sbuf_tensor("rt", [1, 512], F32))
            m2t = e1(nc.sbuf_tensor("m2t", [1, 512], F32))
            v2t = e1(nc.sbuf_tensor("v2t", [1, 512], F32))
            r2t = e1(nc.sbuf_tensor("r2t", [1, 512], F32))
            rm2t = e1(nc.sbuf_tensor("rm2t", [1, 512], F32))
            m_bf = e1(nc.sbuf_tensor("m_bf", [1, 512], BF16))
            r_bf = e1(nc.sbuf_tensor("r_bf", [1, 512], BF16))
            Mb = e1(nc.sbuf_tensor("Mb", [128, 4096], BF16))
            Rb = e1(nc.sbuf_tensor("Rb", [128, 4096], BF16))
            m2col = e1(nc.sbuf_tensor("m2col", [128, 64], F32))
            r2col = e1(nc.sbuf_tensor("r2col", [128, 64], F32))
            rm2col = e1(nc.sbuf_tensor("rm2col", [128, 64], F32))

            kbf2 = e1(nc.sbuf_tensor("kbf2", [128, 2 * 512], BF16))
            vbf2 = e1(nc.sbuf_tensor("vbf2", [128, 2 * 512], BF16))
            vtm2 = e1(nc.sbuf_tensor("vtm2", [128, 2 * 520], BF16))
            phik2 = e1(nc.sbuf_tensor("phik2", [128, 2 * 512], BF16))
            recp2 = e1(nc.sbuf_tensor("recp2", [128, 2 * 512], F32))
            kvacc = e1(nc.sbuf_tensor("kvacc", [128, 8 * 130], F32))
            kvbd = e1(nc.sbuf_tensor("kvbd", [128, 8 * 128], BF16))
            ksr = e1(nc.sbuf_tensor("ksr", [128, 8 * 128], BF16))
            qbf2 = kbf2
            phiq2 = phik2
            phiqn2 = vbf2
            attn_nt = e1(nc.sbuf_tensor("attn_nt", [128, 8 * 512], BF16))
            x2f = e1(nc.sbuf_tensor("x2f", [128, 512], F32))
            x2b2 = e1(nc.sbuf_tensor("x2b2", [128, 2 * 512], BF16))
            x2s2 = e1(nc.sbuf_tensor("x2s2", [128, 2 * 512], BF16))
            x2tmt = e1(nc.sbuf_tensor("x2tmt", [128, 4 * 1024], BF16))
            xn2buf = e1(nc.sbuf_tensor("xn2buf", [128, 4 * 1024], BF16))
            rawl = e1(nc.sbuf_tensor("rawl", [128, 64 * 8], F32))
            csum_b = e1(nc.sbuf_tensor("csum_b", [128, 8], F32))

            lgt = e1(nc.sbuf_tensor("lgt", [128, 64 * 8], F32))
            epat = e1(nc.sbuf_tensor("epat", [128, 64 * 8], F32))
            u_t = e1(nc.sbuf_tensor("u_t", [128, 64 * 8], F32))
            scr8 = e1(nc.sbuf_tensor("scr8", [128, 64 * 8], F32))
            scr8b = e1(nc.sbuf_tensor("scr8b", [128, 64 * 8], F32))
            red1 = e1(nc.sbuf_tensor("red1", [128, 64], F32))
            mx_t = e1(nc.sbuf_tensor("mx_t", [128, 64], F32))
            s_t = e1(nc.sbuf_tensor("s_t", [128, 64], F32))
            u1_t = e1(nc.sbuf_tensor("u1_t", [128, 64], F32))
            u2_t = e1(nc.sbuf_tensor("u2_t", [128, 64], F32))
            i1_t = e1(nc.sbuf_tensor("i1_t", [128, 64], F32))
            i2_t = e1(nc.sbuf_tensor("i2_t", [128, 64], F32))
            w1_t = e1(nc.sbuf_tensor("w1_t", [128, 64], F32))
            w2_t = e1(nc.sbuf_tensor("w2_t", [128, 64], F32))
            va1_t = e1(nc.sbuf_tensor("va1_t", [128, 64], F32))
            va2_t = e1(nc.sbuf_tensor("va2_t", [128, 64], F32))
            oh1 = e1(nc.sbuf_tensor("oh1", [128, 64 * 8], BF16))
            oh2 = e1(nc.sbuf_tensor("oh2", [128, 64 * 8], BF16))
            pos1 = lgt
            pos2 = u_t
            po1 = e1(nc.sbuf_tensor("po1", [128, 64], F32))
            po2 = e1(nc.sbuf_tensor("po2", [128, 64], F32))
            epat2 = e1(nc.sbuf_tensor("epat2", [128, 64 * 8], F32))
            ohM1 = e1(nc.sbuf_tensor("ohM1", [128, 64 * 8], BF16))
            ohM2 = e1(nc.sbuf_tensor("ohM2", [128, 64 * 8], BF16))
            totb = e1(nc.sbuf_tensor("totb", [128, 2 * 512], F32))
            tscab = e1(nc.sbuf_tensor("tscab", [128, 2 * 512], F32))
            onescol_b = e1(nc.sbuf_tensor("onescol_b", [128, 64], F32))
            cntrow = e1(nc.sbuf_tensor("cntrow", [1, 16], F32))
            scv = e1(nc.sbuf_tensor("scv", [128, 2 * 128], F32))
            eaff = e1(nc.sbuf_tensor("eaff", [128, 64 * 8], F32))
            offi = e1(nc.sbuf_tensor("offi", [128, 2 * 64], I32))

            PRa = e1(nc.psum_tensor("PRa", [128, 512], F32))
            PRb = e1(nc.psum_tensor("PRb", [128, 512], F32))
            P4 = e1(nc.psum_tensor("P4", [128, 512], F32))
            PT = e1(nc.psum_tensor("PT", [128, 1024], BF16))

            def _(sync):
                sync.dma_start(out=lt_s[:], in_=c_lt[:, :]).then_inc(S.sem["Q"], 16)
                sync.dma_start(out=idb_s[:], in_=c_idb[:, :]).then_inc(S.sem["Q"], 16)
                sync.dma_start(out=idf_s[:], in_=c_idf[:, :]).then_inc(S.sem["Q"], 16)
                sync.dma_start(out=bout_s[:], in_=boutT[:, :]).then_inc(S.sem["Q"], 16)
                sync.dma_start(out=tokid_s[:], in_=tokid[:, :]).then_inc(S.sem["Q"], 16)
                sync.dma_start(out=rb_s[0:1, :], in_=rbrow[:, :]).then_inc(S.sem["Q"], 16)
                S.cnt["Q"] += 6 * 16
            sy(_)

            def _(g):
                for qi in range(8):
                    g.dma_start(out=wo_b[:, qi * 1024:(qi + 1) * 1024],
                                in_=wout[qi * 128:(qi + 1) * 128, :]).then_inc(S.sem["P"], 16)
                for a in range(8):
                    g.dma_start(out=rf_b[:, a * 128:(a + 1) * 128],
                                in_=rfbd[a * 128:(a + 1) * 128, :]).then_inc(S.sem["P"], 16)
                for ci in range(8):
                    g.dma_start(out=rw_b[:, ci * 8:(ci + 1) * 8],
                                in_=rwt[ci * 128:(ci + 1) * 128, :]).then_inc(S.sem["P"], 16)
                S.cnt["P"] += 24 * 16
                g.iota(epat[:], [[0, 64], [1, 8]], base=0, channel_multiplier=0,
                       allow_small_or_imprecise_dtypes=True).then_inc(S.sem["G"], 1)
                g.iota(epat2[:], [[1, 8], [0, 64]], base=0, channel_multiplier=0,
                       allow_small_or_imprecise_dtypes=True).then_inc(S.sem["G"], 1)
                g.iota(eaff[:], [[0, 64], [1, 8]], base=64, channel_multiplier=0,
                       allow_small_or_imprecise_dtypes=True).then_inc(S.sem["G"], 1)
                S.cnt["G"] += 3
            gp(_)

            def _(v):
                S.bar(v)
                v.memset(ones_c[:], 1.0)
                v.memset(ones_r[:], 1.0)
                v.memset(onesf_r[:], 1.0)
                v.memset(epscol[:], LN_EPS)
                v.memset(stabcol[:], STAB)
                v.memset(onescol_b[:], 1.0)
                v.memset(kvacc[:], 0.0)
                v.memset(zrow2[:], 0.0)
                inst = v.memset(vtm2[:], 1.0)
                S.done(inst, "V")
            ve(_)

            def _(sync):
                S.bar(sync)
                inst = sync.dma_start(out=smap_loc[0:1, :], in_=zrow2[0:1, :])
                inst.then_inc(S.sem["Q"], 16)
                S.cnt["Q"] += 16
            sy(_)

            def _(t):
                S.bar(t)
                inst = t.matmul(P0[:, 0:8], onesf_r[:], rb_s[0:1, :], start=True, stop=True)
                S.done(inst, "T")
            tn(_)

            def _(v):
                S.bar(v)
                inst = v.tensor_copy(rb_s[:, :], P0[:, 0:8])
                S.done(inst, "V")
            ve(_)

            def _(t):
                S.bar(t)
                for ci in range(8):
                    inst = t.matmul(PRa[0:1, 0:8], ones_c[:], rw_b[:, ci * 8:(ci + 1) * 8],
                                    start=(ci == 0), stop=(ci == 7))
                S.done(inst, "T")
            tn(_)

            def _(v):
                S.bar(v)
                inst = v.tensor_scalar_mul(csum_b[0:1, :], PRa[0:1, 0:8], -1.0)
                S.done(inst, "V")
            ve(_)

            def _(t):
                S.bar(t)
                inst = t.matmul(P0[:, 0:8], onesf_r[:], csum_b[0:1, 0:8], start=True, stop=True)
                S.done(inst, "T")
            tn(_)

            def _(v):
                S.bar(v)
                inst = v.tensor_copy(csum_b[:, :], P0[:, 0:8])
                S.done(inst, "V")
            ve(_)

            for b in range(NBATCH):
                def _(g, b=b):
                    S.bar(g)
                    for ci in range(8):
                        inst = g.dma_start(out=wq_b[:, ci * 2048:(ci + 1) * 2048],
                                           in_=wqkv[ci * 128:(ci + 1) * 128, 1024:3072])
                        inst.then_inc(S.sem["P"], 16)
                    for ci in range(8):
                        inst = g.dma_start(out=xbf0[:, ci * 512:(ci + 1) * 512],
                                           in_=xfm[ci * 128:(ci + 1) * 128,
                                                   b * 4096:b * 4096 + 512])
                        inst.then_inc(S.sem["P"], 16)
                    S.cnt["P"] += 16 * 16
                gp(_)

                p_x = {0: S.cnt["P"]}
                t_projA = {}
                v_kvev = {}
                t_phik = {}
                v_relu = {}
                t_kv = {}
                v_kvacc = {}
                t_stats = {}
                a_sq = {}
                v_ln1a = {}
                a_rstd = {}
                v_cast = {}
                t_bc = {}
                v_MbRb = {}
                v_xn1 = {}

                for nt in range(8):
                    xcur = xbf0 if nt % 2 == 0 else xbf1
                    xnxt = xbf1 if nt % 2 == 0 else xbf0

                    if nt + 1 < 8:
                        def _(g, nt=nt, b=b, xnxt=xnxt):
                            if PAR_A: S.bar(g)
                            if nt >= 1:
                                g.wait_ge(S.sem["V"], v_xn1[nt - 1])
                                g.wait_ge(S.sem["T"], t_stats[nt - 1])
                            for ci in range(8):
                                inst = g.dma_start(out=xnxt[:, ci * 512:(ci + 1) * 512],
                                                   in_=xfm[ci * 128:(ci + 1) * 128,
                                                           b * 4096 + (nt + 1) * 512:
                                                           b * 4096 + (nt + 2) * 512])
                                inst.then_inc(S.sem["P"], 16)
                            S.cnt["P"] += 8 * 16
                        gp(_)
                        p_x[nt + 1] = S.cnt["P"]

                    def _(s_, nt=nt, xcur=xcur):
                        if PAR_A: S.bar(s_)
                        if nt == 0:
                            S.bar(s_)
                        else:
                            s_.wait_ge(S.sem["P"], p_x[nt])
                            s_.wait_ge(S.sem["T"], t_stats[nt - 1])
                        for ci in range(8):
                            inst = s_.activation(out=xsq[:, ci * 512:(ci + 1) * 512],
                                                 in_=xcur[:, ci * 512:(ci + 1) * 512],
                                                 func=AF.Square)
                        S.done(inst, "A")
                    sc(_)
                    a_sq[nt] = S.cnt["A"]

                    def _(t, nt=nt, xcur=xcur):
                        if PAR_A: S.bar(t)
                        if nt == 0:
                            S.bar(t)
                        else:
                            t.wait_ge(S.sem["A"], a_sq[nt])
                            t.wait_ge(S.sem["V"], v_MbRb[nt - 1])
                            t.wait_ge(S.sem["V"], v_ln1a[nt - 1])
                        if nt == 0:
                            t.wait_ge(S.sem["A"], a_sq[nt])
                        for ci in range(8):
                            t.matmul(PRa[0:1, :], ones_c[:], xcur[:, ci * 512:(ci + 1) * 512],
                                     start=(ci == 0), stop=(ci == 7))
                        for ci in range(8):
                            inst = t.matmul(PRb[0:1, :], ones_c[:], xsq[:, ci * 512:(ci + 1) * 512],
                                            start=(ci == 0), stop=(ci == 7))
                        S.done(inst, "T")
                    tn(_)
                    t_stats[nt] = S.cnt["T"]

                    def _(v, nt=nt):
                        if PAR_A: S.bar(v)
                        if nt == 0:
                            S.bar(v)
                        v.wait_ge(S.sem["T"], t_stats[nt])
                        v.tensor_scalar_mul(mt[:], PRa[0:1, :], 1.0 / DM)
                        inst = v.tensor_scalar_mul(vt[:], PRb[0:1, :], 1.0 / DM)
                        S.done(inst, "V")
                        v.wait_ge(S.sem["V"], S.cnt["V"])
                        inst = v.tensor_mul(rt[:], mt[:], mt[:])
                        S.done(inst, "V")
                        v.wait_ge(S.sem["V"], S.cnt["V"])
                        inst = v.tensor_sub(vt[:], vt[:], rt[:])
                        S.done(inst, "V")
                    ve(_)
                    v_ln1a[nt] = S.cnt["V"]

                    def _(s_, nt=nt):
                        if PAR_A: S.bar(s_)
                        s_.wait_ge(S.sem["V"], v_ln1a[nt])
                        inst = s_.activation(out=rt[:], in_=vt[:], func=AF.Ln,
                                             bias=epscol[0:1, 0:1])
                        S.done(inst, "A")
                        s_.wait_ge(S.sem["A"], S.cnt["A"])
                        inst = s_.activation(out=rt[:], in_=rt[:], func=AF.Exp, scale=-0.5)
                        S.done(inst, "A")
                    sc(_)
                    a_rstd[nt] = S.cnt["A"]

                    def _(v, nt=nt):
                        if PAR_A: S.bar(v)
                        v.wait_ge(S.sem["A"], a_rstd[nt])
                        v.tensor_copy(m_bf[:], mt[:])
                        inst = v.tensor_copy(r_bf[:], rt[:])
                        S.done(inst, "V")
                    ve(_)
                    v_cast[nt] = S.cnt["V"]

                    def _(t, nt=nt):
                        if PAR_A: S.bar(t)
                        t.wait_ge(S.sem["V"], v_cast[nt])
                        if nt >= 1:
                            t.wait_ge(S.sem["V"], v_MbRb[nt - 1])
                        t.matmul(P4[:, :], ones_r[:], m_bf[0:1, :], start=True, stop=True)
                        inst = t.matmul(PRb[:, :], ones_r[:], r_bf[0:1, :], start=True, stop=True)
                        S.done(inst, "T")
                    tn(_)
                    t_bc[nt] = S.cnt["T"]

                    def _(v, nt=nt):
                        if PAR_A: S.bar(v)
                        v.wait_ge(S.sem["T"], t_bc[nt])
                        v.tensor_copy(Mb[:, nt * 512:(nt + 1) * 512], P4[:, :])
                        inst = v.tensor_copy(Rb[:, nt * 512:(nt + 1) * 512], PRb[:, :])
                        S.done(inst, "V")
                    ve(_)
                    v_MbRb[nt] = S.cnt["V"]

                    def _(v, nt=nt, xcur=xcur):
                        if PAR_A: S.bar(v)
                        if nt >= 1:
                            v.wait_ge(S.sem["T"], t_projA[(nt - 1, 7)])
                        v.wait_ge(S.sem["V"], v_MbRb[nt])
                        x3 = xcur[:].rearrange("p (c t) -> p c t", c=8)
                        n3 = xn1[:].rearrange("p (c t) -> p c t", c=8)
                        mb3 = (Mb[:, nt * 512:(nt + 1) * 512]
                               .rearrange("p (u t) -> p u t", u=1).broadcast_to([128, 8, 512]))
                        rb3 = (Rb[:, nt * 512:(nt + 1) * 512]
                               .rearrange("p (u t) -> p u t", u=1).broadcast_to([128, 8, 512]))
                        v.tensor_tensor(n3, x3, mb3, ALU.subtract)
                        inst = v.tensor_tensor(n3, n3, rb3, ALU.mult)
                        S.done(inst, "V")
                    ve(_)
                    v_xn1[nt] = S.cnt["V"]

                    for pr in range(10):
                        if pr < 8:
                            def _(t, nt=nt, pr=pr):
                                if PAR_A: S.bar(t)
                                if pr == 0:
                                    t.wait_ge(S.sem["V"], v_xn1[nt])
                                prev = (nt, pr - 1) if pr >= 1 else (nt - 1, 7)
                                if prev in v_kvev:
                                    t.wait_ge(S.sem["V"], v_kvev[prev])
                                for ci in range(8):
                                    t.matmul(P0[:, :], wq_b[:, ci * 2048 + pr * 128:
                                                            ci * 2048 + (pr + 1) * 128],
                                             xn1[:, ci * 512:(ci + 1) * 512],
                                             start=(ci == 0), stop=(ci == 7))
                                for ci in range(8):
                                    inst = t.matmul(P1[:, :], wq_b[:, ci * 2048 + 1024 + pr * 128:
                                                                   ci * 2048 + 1024 + (pr + 1) * 128],
                                                    xn1[:, ci * 512:(ci + 1) * 512],
                                                    start=(ci == 0), stop=(ci == 7))
                                S.done(inst, "T")
                            tn(_)
                            t_projA[(nt, pr)] = S.cnt["T"]

                            def _(v, nt=nt, pr=pr):
                                if PAR_A: S.bar(v)
                                v.wait_ge(S.sem["T"], t_projA[(nt, pr)])
                                prev2 = (nt, pr - 2) if pr >= 2 else (nt - 1, pr + 6)
                                if prev2 in t_phik:
                                    v.wait_ge(S.sem["T"], t_phik[prev2])
                                sl = slice((pr % 2) * 512, (pr % 2) * 512 + 512)
                                v.tensor_copy(kbf2[:, sl], P0[:, :])
                                inst = v.tensor_copy(vbf2[:, sl], P1[:, :])
                                S.done(inst, "V")
                            ve(_)
                            v_kvev[(nt, pr)] = S.cnt["V"]

                        if 1 <= pr <= 8:
                            k = pr - 1

                            def _(t, nt=nt, k=k):
                                if PAR_A: S.bar(t)
                                t.wait_ge(S.sem["V"], v_kvev[(nt, k)])
                                prevk = (nt, k - 1) if k >= 1 else (nt - 1, 7)
                                if prevk in v_relu:
                                    t.wait_ge(S.sem["V"], v_relu[prevk])
                                s0 = (k % 2) * 512
                                for j in range(4):
                                    t.matmul(P2[:, j * 128:(j + 1) * 128],
                                             kbf2[:, s0 + j * 128:s0 + (j + 1) * 128],
                                             rf_b[:, k * 128:(k + 1) * 128],
                                             start=(j == 0), stop=(j == 3))
                                for j in range(4):
                                    inst = t.transpose(PT[:, j * 128:(j + 1) * 128],
                                                       vbf2[:, s0 + j * 128:s0 + (j + 1) * 128],
                                                       idb_s[:])
                                S.done(inst, "T")
                            tn(_)
                            t_phik[(nt, k)] = S.cnt["T"]

                            def _(v, nt=nt, k=k):
                                if PAR_A: S.bar(v)
                                v.wait_ge(S.sem["T"], t_phik[(nt, k)])
                                prevk2 = (nt, k - 2) if k >= 2 else (nt - 1, k + 6)
                                if prevk2 in t_kv:
                                    v.wait_ge(S.sem["T"], t_kv[prevk2])
                                s0 = (k % 2) * 512
                                vb = (k % 2) * 520
                                v.tensor_relu(phik2[:, s0:s0 + 512], P2[:, :])
                                for j in range(4):
                                    v.tensor_copy(vtm2[:, vb + j * 130:vb + j * 130 + 64],
                                                  PT[:, j * 128:j * 128 + 64])
                                    inst = v.tensor_copy(vtm2[:, vb + j * 130 + 65:vb + j * 130 + 129],
                                                         PT[:, j * 128 + 64:(j + 1) * 128])
                                S.done(inst, "V")
                            ve(_)
                            v_relu[(nt, k)] = S.cnt["V"]

                        if 2 <= pr <= 9:
                            m = pr - 2

                            def _(t, nt=nt, m=m):
                                if PAR_A: S.bar(t)
                                t.wait_ge(S.sem["V"], v_relu[(nt, m)])
                                prevm = (nt, m - 1) if m >= 1 else (nt - 1, 7)
                                if prevm in v_kvacc:
                                    t.wait_ge(S.sem["V"], v_kvacc[prevm])
                                s0 = (m % 2) * 512
                                vb = (m % 2) * 520
                                first = True
                                for j in range(4):
                                    t.matmul(P3[0:64, 0:65], phik2[:, s0 + j * 128:s0 + j * 128 + 64],
                                             vtm2[:, vb + j * 130:vb + j * 130 + 65],
                                             start=first, stop=False)
                                    first = False
                                    inst = t.matmul(P3[64:128, 65:130],
                                                    phik2[:, s0 + j * 128 + 64:s0 + (j + 1) * 128],
                                                    vtm2[:, vb + j * 130 + 65:vb + j * 130 + 130],
                                                    start=False, stop=(j == 3))
                                S.done(inst, "T")
                            tn(_)
                            t_kv[(nt, m)] = S.cnt["T"]

                            def _(v, nt=nt, m=m):
                                if PAR_A: S.bar(v)
                                v.wait_ge(S.sem["T"], t_kv[(nt, m)])
                                sl = slice(m * 130, (m + 1) * 130)
                                inst = v.tensor_add(kvacc[:, sl], kvacc[:, sl], P3[:, 0:130])
                                S.done(inst, "V")
                            ve(_)
                            v_kvacc[(nt, m)] = S.cnt["V"]

                def _(v, b=b):
                    S.bar(v)
                    v.memset(kvbd[:], 0.0)
                    v.memset(ksr[:], 0.0)
                    for pr in range(8):
                        base = pr * 130
                        v.tensor_copy(kvbd[0:64, pr * 128:pr * 128 + 64],
                                      kvacc[0:64, base:base + 64])
                        v.tensor_copy(kvbd[64:128, pr * 128 + 64:(pr + 1) * 128],
                                      kvacc[64:128, base + 65:base + 129])
                        v.tensor_copy(ksr[0:64, pr * 128:pr * 128 + 64],
                                      kvacc[0:64, base + 64:base + 65].broadcast_to([64, 64]))
                        inst = v.tensor_copy(ksr[64:128, pr * 128 + 64:(pr + 1) * 128],
                                             kvacc[64:128, base + 129:base + 130]
                                             .broadcast_to([64, 64]))
                    v.memset(kvacc[:], 0.0)
                    S.done(inst, "V")
                ve(_)

                def _(g, b=b):
                    S.bar(g)
                    for ci in range(8):
                        inst = g.dma_start(out=wq_b[:, ci * 2048:ci * 2048 + 1024],
                                           in_=wqkv[ci * 128:(ci + 1) * 128, 0:1024])
                        inst.then_inc(S.sem["P"], 16)
                    for ci in range(8):
                        inst = g.dma_start(out=xbf0[:, ci * 512:(ci + 1) * 512],
                                           in_=xfm[ci * 128:(ci + 1) * 128,
                                                   b * 4096:b * 4096 + 512])
                        inst.then_inc(S.sem["P"], 16)
                    S.cnt["P"] += 16 * 16
                gp(_)

                p_xB = {0: S.cnt["P"]}
                t_projB = {}
                v_qbf = {}
                t_phiq = {}
                v_reluB = {}
                t_ksr = {}
                v_recip = {}
                t_kvB = {}
                v_attn = {}
                v_xn1B = {}
                t_wout = {}
                v_x2b = {}
                a_sq2 = {}
                t_stats2 = {}
                v_x2tmt = {}
                v_ln2a = {}
                a_rstd2 = {}
                v_rm2 = {}
                t_tr2 = {}
                v_cols = {}
                v_xn2m = {}
                o_st = {}

                for nt in range(8):
                    xcur = xbf0 if nt % 2 == 0 else xbf1
                    xnxt = xbf1 if nt % 2 == 0 else xbf0

                    if nt + 1 < 8:
                        def _(g, nt=nt, b=b, xnxt=xnxt):
                            if PAR_B: S.bar(g)
                            if nt >= 1:
                                g.wait_ge(S.sem["V"], v_x2b[(nt - 1, 7)])
                            for ci in range(8):
                                inst = g.dma_start(out=xnxt[:, ci * 512:(ci + 1) * 512],
                                                   in_=xfm[ci * 128:(ci + 1) * 128,
                                                           b * 4096 + (nt + 1) * 512:
                                                           b * 4096 + (nt + 2) * 512])
                                inst.then_inc(S.sem["P"], 16)
                            S.cnt["P"] += 8 * 16
                        gp(_)
                        p_xB[nt + 1] = S.cnt["P"]

                    def _(v, nt=nt, xcur=xcur):
                        if PAR_B: S.bar(v)
                        if nt == 0:
                            S.bar(v)
                        else:
                            v.wait_ge(S.sem["P"], p_xB[nt])
                            v.wait_ge(S.sem["T"], t_projB[(nt - 1, 7)])
                        x3 = xcur[:].rearrange("p (c t) -> p c t", c=8)
                        n3 = xn1[:].rearrange("p (c t) -> p c t", c=8)
                        mb3 = (Mb[:, nt * 512:(nt + 1) * 512]
                               .rearrange("p (u t) -> p u t", u=1).broadcast_to([128, 8, 512]))
                        rb3 = (Rb[:, nt * 512:(nt + 1) * 512]
                               .rearrange("p (u t) -> p u t", u=1).broadcast_to([128, 8, 512]))
                        v.tensor_tensor(n3, x3, mb3, ALU.subtract)
                        inst = v.tensor_tensor(n3, n3, rb3, ALU.mult)
                        S.done(inst, "V")
                    ve(_)
                    v_xn1B[nt] = S.cnt["V"]

                    for pr in range(11):
                        if pr < 8:
                            def _(t, nt=nt, pr=pr):
                                if PAR_B: S.bar(t)
                                if nt == 0 and pr == 0:
                                    S.bar(t)
                                if pr == 0:
                                    t.wait_ge(S.sem["V"], v_xn1B[nt])
                                prev = (nt, pr - 1) if pr >= 1 else (nt - 1, 7)
                                if prev in v_qbf:
                                    t.wait_ge(S.sem["V"], v_qbf[prev])
                                for ci in range(8):
                                    inst = t.matmul(P0[:, :], wq_b[:, ci * 2048 + pr * 128:
                                                                  ci * 2048 + (pr + 1) * 128],
                                                    xn1[:, ci * 512:(ci + 1) * 512],
                                                    start=(ci == 0), stop=(ci == 7))
                                S.done(inst, "T")
                            tn(_)
                            t_projB[(nt, pr)] = S.cnt["T"]

                            def _(v, nt=nt, pr=pr):
                                if PAR_B: S.bar(v)
                                v.wait_ge(S.sem["T"], t_projB[(nt, pr)])
                                prev2 = (nt, pr - 2) if pr >= 2 else (nt - 1, pr + 6)
                                if prev2 in t_phiq:
                                    v.wait_ge(S.sem["T"], t_phiq[prev2])
                                sl = slice((pr % 2) * 512, (pr % 2) * 512 + 512)
                                inst = v.tensor_copy(qbf2[:, sl], P0[:, :])
                                S.done(inst, "V")
                            ve(_)
                            v_qbf[(nt, pr)] = S.cnt["V"]

                        if 1 <= pr <= 8:
                            k = pr - 1

                            def _(t, nt=nt, k=k):
                                if PAR_B: S.bar(t)
                                t.wait_ge(S.sem["V"], v_qbf[(nt, k)])
                                prevk = (nt, k - 1) if k >= 1 else (nt - 1, 7)
                                if prevk in v_reluB:
                                    t.wait_ge(S.sem["V"], v_reluB[prevk])
                                s0 = (k % 2) * 512
                                inst = t.matmul(P1[:, :], rf_b[:, k * 128:(k + 1) * 128],
                                                qbf2[:, s0:s0 + 512], start=True, stop=True)
                                S.done(inst, "T")
                            tn(_)
                            t_phiq[(nt, k)] = S.cnt["T"]

                            def _(v, nt=nt, k=k):
                                if PAR_B: S.bar(v)
                                v.wait_ge(S.sem["T"], t_phiq[(nt, k)])
                                prevk2 = (nt, k - 2) if k >= 2 else (nt - 1, k + 6)
                                if prevk2 in t_ksr:
                                    v.wait_ge(S.sem["T"], t_ksr[prevk2])
                                s0 = (k % 2) * 512
                                inst = v.tensor_relu(phiq2[:, s0:s0 + 512], P1[:, :])
                                S.done(inst, "V")
                            ve(_)
                            v_reluB[(nt, k)] = S.cnt["V"]

                        if 2 <= pr <= 9:
                            m = pr - 2

                            def _(t, nt=nt, m=m, b=b):
                                if PAR_B: S.bar(t)
                                t.wait_ge(S.sem["V"], v_reluB[(nt, m)])
                                if m >= 1:
                                    t.wait_ge(S.sem["V"], v_recip[(nt, m - 1)])
                                elif nt >= 1:
                                    t.wait_ge(S.sem["V"], v_ln2a[nt - 1])
                                s0 = (m % 2) * 512
                                inst = t.matmul(P2[:, :], ksr[:, m * 128:(m + 1) * 128],
                                                phiq2[:, s0:s0 + 512], start=True, stop=True)
                                S.done(inst, "T")
                            tn(_)
                            t_ksr[(nt, m)] = S.cnt["T"]

                            def _(v, nt=nt, m=m):
                                if PAR_B: S.bar(v)
                                v.wait_ge(S.sem["T"], t_ksr[(nt, m)])
                                prevm2 = (nt, m - 2) if m >= 2 else (nt - 1, m + 6)
                                if prevm2 in t_kvB:
                                    v.wait_ge(S.sem["T"], t_kvB[prevm2])
                                s0 = (m % 2) * 512
                                inst = v.tensor_scalar_add(recp2[:, s0:s0 + 512], P2[:, :], STAB)
                                S.done(inst, "V")
                                v.wait_ge(S.sem["V"], S.cnt["V"])
                                inst = v.reciprocal(recp2[:, s0:s0 + 512], recp2[:, s0:s0 + 512])
                                S.done(inst, "V")
                                v.wait_ge(S.sem["V"], S.cnt["V"])
                                inst = v.tensor_mul(phiqn2[:, s0:s0 + 512], phiq2[:, s0:s0 + 512],
                                                    recp2[:, s0:s0 + 512])
                                S.done(inst, "V")
                            ve(_)
                            v_recip[(nt, m)] = S.cnt["V"]

                        if 3 <= pr <= 10:
                            m3 = pr - 3

                            def _(t, nt=nt, m3=m3):
                                if PAR_B: S.bar(t)
                                t.wait_ge(S.sem["V"], v_recip[(nt, m3)])
                                prevm3 = (nt, m3 - 1) if m3 >= 1 else (nt - 1, 7)
                                if prevm3 in v_attn:
                                    t.wait_ge(S.sem["V"], v_attn[prevm3])
                                s0 = (m3 % 2) * 512
                                inst = t.matmul(P3[:, :], kvbd[:, m3 * 128:(m3 + 1) * 128],
                                                phiqn2[:, s0:s0 + 512], start=True, stop=True)
                                S.done(inst, "T")
                            tn(_)
                            t_kvB[(nt, m3)] = S.cnt["T"]

                            def _(v, nt=nt, m3=m3):
                                if PAR_B: S.bar(v)
                                v.wait_ge(S.sem["T"], t_kvB[(nt, m3)])
                                inst = v.tensor_copy(attn_nt[:, m3 * 512:(m3 + 1) * 512], P3[:, :])
                                S.done(inst, "V")
                            ve(_)
                            v_attn[(nt, m3)] = S.cnt["V"]

                    for oi in range(9):
                        if oi < 8:
                            def _(t, nt=nt, oi=oi):
                                if PAR_B: S.bar(t)
                                if oi == 0:
                                    t.wait_ge(S.sem["V"], v_attn[(nt, 7)])
                                    if nt >= 1:
                                        t.wait_ge(S.sem["V"], v_cols[nt - 1])
                                else:
                                    t.wait_ge(S.sem["V"], v_x2b[(nt, oi - 1)])
                                for qi in range(8):
                                    inst = t.matmul(P4[:, :], wo_b[:, qi * 1024 + oi * 128:
                                                                  qi * 1024 + (oi + 1) * 128],
                                                    attn_nt[:, qi * 512:(qi + 1) * 512],
                                                    start=(qi == 0), stop=(qi == 7))
                                S.done(inst, "T")
                            tn(_)
                            t_wout[(nt, oi)] = S.cnt["T"]

                            def _(v, nt=nt, oi=oi, xcur=xcur):
                                if PAR_B: S.bar(v)
                                v.wait_ge(S.sem["T"], t_wout[(nt, oi)])
                                prev2 = (nt, oi - 2) if oi >= 2 else (nt - 1, oi + 6)
                                if prev2 in t_stats2:
                                    v.wait_ge(S.sem["T"], t_stats2[prev2])
                                s0 = (oi % 2) * 512
                                v.tensor_add(x2f[:], P4[:, :], xcur[:, oi * 512:(oi + 1) * 512])
                                inst = v.tensor_scalar(out=x2b2[:, s0:s0 + 512], in0=x2f[:],
                                                       scalar1=bout_s[:, oi:oi + 1],
                                                       scalar2=None, op0=ALU.add)
                                S.done(inst, "V")
                            ve(_)
                            v_x2b[(nt, oi)] = S.cnt["V"]

                            def _(s_, nt=nt, oi=oi):
                                if PAR_B: S.bar(s_)
                                if nt == 0 and oi == 0:
                                    S.bar(s_)
                                s_.wait_ge(S.sem["V"], v_x2b[(nt, oi)])
                                prev2 = (nt, oi - 2) if oi >= 2 else (nt - 1, oi + 6)
                                if prev2 in t_stats2:
                                    s_.wait_ge(S.sem["T"], t_stats2[prev2])
                                s0 = (oi % 2) * 512
                                inst = s_.activation(out=x2s2[:, s0:s0 + 512],
                                                     in_=x2b2[:, s0:s0 + 512], func=AF.Square)
                                S.done(inst, "A")
                            sc(_)
                            a_sq2[(nt, oi)] = S.cnt["A"]

                        if 1 <= oi <= 8:
                            w = oi - 1

                            def _(t, nt=nt, w=w):
                                if PAR_B: S.bar(t)
                                t.wait_ge(S.sem["A"], a_sq2[(nt, w)])
                                if w == 0:
                                    if nt >= 1:
                                        t.wait_ge(S.sem["V"], v_ln2a[nt - 1])
                                        t.wait_ge(S.sem["V"], v_x2tmt[(nt - 1, 7)])
                                else:
                                    t.wait_ge(S.sem["V"], v_x2tmt[(nt, w - 1)])
                                s0 = (w % 2) * 512
                                t.matmul(PRa[0:1, :], ones_c[:], x2b2[:, s0:s0 + 512],
                                         start=(w == 0), stop=(w == 7))
                                t.matmul(PRb[0:1, :], ones_c[:], x2s2[:, s0:s0 + 512],
                                         start=(w == 0), stop=(w == 7))
                                for j in range(4):
                                    t.matmul(P2[:, j * 8:(j + 1) * 8],
                                             x2b2[:, s0 + j * 128:s0 + (j + 1) * 128],
                                             rw_b[:, w * 8:(w + 1) * 8],
                                             start=(w == 0 and j == 0), stop=(w == 7 and j == 3))
                                for j in range(4):
                                    inst = t.transpose(PT[:, j * 128:(j + 1) * 128],
                                                       x2b2[:, s0 + j * 128:s0 + (j + 1) * 128],
                                                       idb_s[:])
                                S.done(inst, "T")
                            tn(_)
                            t_stats2[(nt, w)] = S.cnt["T"]

                            def _(v, nt=nt, w=w):
                                if PAR_B: S.bar(v)
                                v.wait_ge(S.sem["T"], t_stats2[(nt, w)])
                                if w == 0 and nt >= 1:
                                    v.wait_ge(S.sem["O"], o_st[nt - 1])
                                for j in range(4):
                                    inst = v.tensor_copy(
                                        x2tmt[:, j * 1024 + w * 128:j * 1024 + (w + 1) * 128],
                                        PT[:, j * 128:(j + 1) * 128])
                                S.done(inst, "V")
                            ve(_)
                            v_x2tmt[(nt, w)] = S.cnt["V"]

                    def _(v, nt=nt, b=b):
                        if PAR_B: S.bar(v)
                        v.wait_ge(S.sem["T"], t_stats2[(nt, 7)])
                        for j in range(4):
                            v.tensor_copy(rawl[:, (b * 32 + nt * 4 + j) * 8:
                                               (b * 32 + nt * 4 + j + 1) * 8],
                                          P2[:, j * 8:(j + 1) * 8])
                        v.tensor_scalar_mul(m2t[:], PRa[0:1, :], 1.0 / DM)
                        inst = v.tensor_scalar_mul(v2t[:], PRb[0:1, :], 1.0 / DM)
                        S.done(inst, "V")
                        v.wait_ge(S.sem["V"], S.cnt["V"])
                        inst = v.tensor_mul(rm2t[:], m2t[:], m2t[:])
                        S.done(inst, "V")
                        v.wait_ge(S.sem["V"], S.cnt["V"])
                        inst = v.tensor_sub(v2t[:], v2t[:], rm2t[:])
                        S.done(inst, "V")
                    ve(_)
                    v_ln2a[nt] = S.cnt["V"]

                    def _(s_, nt=nt):
                        if PAR_B: S.bar(s_)
                        s_.wait_ge(S.sem["V"], v_ln2a[nt])
                        inst = s_.activation(out=r2t[:], in_=v2t[:], func=AF.Ln,
                                             bias=epscol[0:1, 0:1])
                        S.done(inst, "A")
                        s_.wait_ge(S.sem["A"], S.cnt["A"])
                        inst = s_.activation(out=r2t[:], in_=r2t[:], func=AF.Exp, scale=-0.5)
                        S.done(inst, "A")
                    sc(_)
                    a_rstd2[nt] = S.cnt["A"]

                    def _(v, nt=nt):
                        if PAR_B: S.bar(v)
                        v.wait_ge(S.sem["A"], a_rstd2[nt])
                        inst = v.tensor_mul(rm2t[:], r2t[:], m2t[:])
                        S.done(inst, "V")
                    ve(_)
                    v_rm2[nt] = S.cnt["V"]

                    def _(t, nt=nt):
                        if PAR_B: S.bar(t)
                        t.wait_ge(S.sem["V"], v_rm2[nt])
                        t.wait_ge(S.sem["V"], v_x2b[(nt, 7)])
                        for ch in range(4):
                            t.transpose(P4[:, ch:ch + 1], m2t[0:1, ch * 128:(ch + 1) * 128],
                                        idf_s[0:1, 0:1])
                        for ch in range(4):
                            t.transpose(P4[:, 4 + ch:5 + ch], r2t[0:1, ch * 128:(ch + 1) * 128],
                                        idf_s[0:1, 0:1])
                        for ch in range(4):
                            inst = t.transpose(P4[:, 8 + ch:9 + ch],
                                               rm2t[0:1, ch * 128:(ch + 1) * 128],
                                               idf_s[0:1, 0:1])
                        S.done(inst, "T")
                    tn(_)
                    t_tr2[nt] = S.cnt["T"]

                    def _(v, nt=nt, b=b):
                        if PAR_B: S.bar(v)
                        v.wait_ge(S.sem["T"], t_tr2[nt])
                        g0 = b * 32 + nt * 4
                        v.tensor_copy(m2col[:, g0:g0 + 4], P4[:, 0:4])
                        v.tensor_copy(r2col[:, g0:g0 + 4], P4[:, 4:8])
                        inst = v.tensor_copy(rm2col[:, g0:g0 + 4], P4[:, 8:12])
                        S.done(inst, "V")
                    ve(_)
                    v_cols[nt] = S.cnt["V"]

                    def _(v, nt=nt, b=b):
                        if PAR_B: S.bar(v)
                        for ch in range(4):
                            gch = b * 32 + nt * 4 + ch
                            inst = v.tensor_scalar(out=xn2buf[:, ch * 1024:(ch + 1) * 1024],
                                                   in0=x2tmt[:, ch * 1024:(ch + 1) * 1024],
                                                   scalar1=m2col[:, gch:gch + 1],
                                                   scalar2=r2col[:, gch:gch + 1],
                                                   op0=ALU.subtract, op1=ALU.mult)
                        S.done(inst, "V")
                    ve(_)
                    v_xn2m[nt] = S.cnt["V"]

                    def _(sync, nt=nt, b=b):
                        if PAR_B: S.bar(sync)
                        sync.wait_ge(S.sem["V"], v_xn2m[nt])
                        for j in range(4):
                            r0 = b * 4096 + nt * 512 + j * 128
                            inst = sync.dma_start(out=x2_loc[r0:r0 + 128, :],
                                                  in_=x2tmt[:, j * 1024:(j + 1) * 1024])
                            inst.then_inc(S.sem["O"], 16)
                        for j in range(4):
                            r0 = b * 4096 + nt * 512 + j * 128
                            inst = sync.dma_start(out=xn2_loc[r0:r0 + 128, :],
                                                  in_=xn2buf[:, j * 1024:(j + 1) * 1024])
                            inst.then_inc(S.sem["O"], 16)
                        S.cnt["O"] += 8 * 16
                    sy(_)
                    o_st[nt] = S.cnt["O"]

                def _(v, b=b):
                    S.bar(v)
                    sl = slice(b * 32 * 8, (b + 1) * 32 * 8)
                    c3 = (csum_b[:].rearrange("p (u e) -> p u e", u=1)
                          .broadcast_to([128, 32, 8]))
                    rm3 = (rm2col[:, b * 32:(b + 1) * 32]
                           .rearrange("p (c u) -> p c u", u=1).broadcast_to([128, 32, 8]))
                    r23 = (r2col[:, b * 32:(b + 1) * 32]
                           .rearrange("p (c u) -> p c u", u=1).broadcast_to([128, 32, 8]))
                    u3 = u_t[:, sl].rearrange("p (c e) -> p c e", e=8)
                    l3 = lgt[:, sl].rearrange("p (c e) -> p c e", e=8)
                    raw3 = rawl[:, sl].rearrange("p (c e) -> p c e", e=8)
                    v.tensor_tensor(u3, c3, rm3, ALU.mult)
                    inst = v.tensor_tensor(l3, raw3, r23, ALU.mult)
                    S.done(inst, "V")
                ve(_)

                def _(v, b=b):
                    S.bar(v)
                    sl = slice(b * 32 * 8, (b + 1) * 32 * 8)
                    inst = v.tensor_add(lgt[:, sl], lgt[:, sl], u_t[:, sl])
                    S.done(inst, "V")
                ve(_)

                def _(v, b=b):
                    S.bar(v)
                    sl = slice(b * 32 * 8, (b + 1) * 32 * 8)
                    l3 = lgt[:, sl].rearrange("p (c e) -> p c e", e=8)
                    rb3 = (rb_s[:].rearrange("p (u e) -> p u e", u=1)
                           .broadcast_to([128, 32, 8]))
                    inst = v.tensor_tensor(l3, l3, rb3, ALU.add)
                    S.done(inst, "V")
                ve(_)

            def _(v):
                S.bar(v)
                l3 = lgt[:].rearrange("p (c e) -> p c e", e=8)
                inst = v.tensor_reduce(mx_t[:], l3, AX.X, ALU.max)
                S.done(inst, "V")
            ve(_)

            def _(v):
                S.bar(v)
                u3 = u_t[:].rearrange("p (c e) -> p c e", e=8)
                l3 = lgt[:].rearrange("p (c e) -> p c e", e=8)

                def bc(t64):
                    return t64[:].rearrange("p (c u) -> p c u", u=1).broadcast_to([128, 64, 8])

                inst = v.tensor_tensor(u3, l3, bc(mx_t), ALU.subtract)
                S.done(inst, "V")
            ve(_)

            def _(s):
                S.bar(s)
                inst = s.activation(out=u_t[:], in_=u_t[:], func=AF.Exp)
                S.done(inst, "A")
            sc(_)

            def _(v):
                S.bar(v)
                u3 = u_t[:].rearrange("p (c e) -> p c e", e=8)

                v.tensor_reduce(s_t[:], u3, AX.X, ALU.add)
                inst = v.tensor_reduce(u1_t[:], u3, AX.X, ALU.max)
                S.done(inst, "V")
            ve(_)

            def bc(t64):
                return t64[:].rearrange("p (c u) -> p c u", u=1).broadcast_to([128, 64, 8])

            def bcM(t64):
                return t64[:].rearrange("p (u c) -> p u c", u=1).broadcast_to([128, 8, 64])

            U3 = u_t[:].rearrange("p (c e) -> p c e", e=8)
            E3 = epat[:].rearrange("p (c e) -> p c e", e=8)
            S3 = scr8[:].rearrange("p (c e) -> p c e", e=8)
            S3B = scr8b[:].rearrange("p (c e) -> p c e", e=8)
            L3V = lgt[:].rearrange("p (c e) -> p c e", e=8)
            EM = epat2[:].rearrange("p (e c) -> p e c", e=8)

            def _(v):
                S.bar(v)
                inst = v.tensor_tensor(S3, U3, bc(u1_t), ALU.is_ge)
                S.done(inst, "V")
            ve(_)

            def _(v):
                S.bar(v)
                inst = v.scalar_tensor_tensor(scr8b[:], scr8[:], -64.0, eaff[:],
                                              ALU.mult, ALU.add)
                S.done(inst, "V")
            ve(_)

            def _(v):
                S.bar(v)
                inst = v.tensor_reduce(i1_t[:], S3B, AX.X, ALU.min)
                S.done(inst, "V")
            ve(_)

            def _(v):
                S.bar(v)
                inst = v.tensor_tensor(S3, E3, bc(i1_t), ALU.not_equal)
                S.done(inst, "V")
            ve(_)

            def _(v):
                S.bar(v)
                inst = v.tensor_tensor(S3B, U3, S3, ALU.mult)
                S.done(inst, "V")
            ve(_)

            def _(v):
                S.bar(v)
                inst = v.tensor_reduce(u2_t[:], S3B, AX.X, ALU.max)
                S.done(inst, "V")
            ve(_)

            def _(v):
                S.bar(v)
                inst = v.tensor_tensor(L3V, U3, bc(u2_t), ALU.is_ge)
                S.done(inst, "V")
            ve(_)

            def _(v):
                S.bar(v)
                inst = v.tensor_mul(lgt[:], lgt[:], scr8[:])
                S.done(inst, "V")
            ve(_)

            def _(v):
                S.bar(v)
                inst = v.scalar_tensor_tensor(scr8b[:], lgt[:], -64.0, eaff[:],
                                              ALU.mult, ALU.add)
                S.done(inst, "V")
            ve(_)

            def _(v):
                S.bar(v)
                inst = v.tensor_reduce(i2_t[:], S3B, AX.X, ALU.min)
                S.done(inst, "V")
            ve(_)

            def _(v):
                S.bar(v)
                inst = v.tensor_add(red1[:], u1_t[:], u2_t[:])
                S.done(inst, "V")
            ve(_)

            def _(v):
                S.bar(v)
                inst = v.scalar_tensor_tensor(red1[:], s_t[:], REPS, red1[:],
                                              ALU.mult, ALU.add)
                S.done(inst, "V")
            ve(_)

            def _(s):
                S.bar(s)
                inst = s.activation(out=red1[:], in_=red1[:], func=AF.Ln)
                S.done(inst, "A")
            sc(_)

            def _(s):
                S.bar(s)
                inst = s.activation(out=red1[:], in_=red1[:], func=AF.Exp, scale=-1.0)
                S.done(inst, "A")
            sc(_)

            def _(v):
                S.bar(v)
                v.tensor_mul(w1_t[:], u1_t[:], red1[:])
                inst = v.tensor_mul(w2_t[:], u2_t[:], red1[:])
                S.done(inst, "V")
            ve(_)

            def _(v):
                S.bar(v)
                v.tensor_single_scalar(va1_t[:], w1_t[:], REPS, ALU.is_gt)
                inst = v.tensor_single_scalar(va2_t[:], w2_t[:], REPS, ALU.is_gt)
                S.done(inst, "V")
            ve(_)

            def _(v):
                S.bar(v)
                v.tensor_tensor(S3, E3, bc(i1_t), ALU.is_equal)
                v.tensor_tensor(S3B, E3, bc(i2_t), ALU.is_equal)
                v.tensor_tensor(lgt[:].rearrange("p (e c) -> p e c", e=8), EM,
                                bcM(i1_t), ALU.is_equal)
                inst = v.tensor_tensor(u_t[:].rearrange("p (e c) -> p e c", e=8), EM,
                                       bcM(i2_t), ALU.is_equal)
                S.done(inst, "V")
            ve(_)

            def _(v):
                S.bar(v)
                v.tensor_tensor(oh1[:].rearrange("p (c e) -> p c e", e=8), S3,
                                bc(va1_t), ALU.mult)
                v.tensor_tensor(oh2[:].rearrange("p (c e) -> p c e", e=8), S3B,
                                bc(va2_t), ALU.mult)
                v.tensor_tensor(ohM1[:].rearrange("p (e c) -> p e c", e=8),
                                lgt[:].rearrange("p (e c) -> p e c", e=8),
                                bcM(va1_t), ALU.mult)
                inst = v.tensor_tensor(ohM2[:].rearrange("p (e c) -> p e c", e=8),
                                       u_t[:].rearrange("p (e c) -> p e c", e=8),
                                       bcM(va2_t), ALU.mult)
                S.done(inst, "V")
            ve(_)

            def _(t):
                S.bar(t)
                t.matmul(P0[:, :], lt_s[:], oh1[:], start=True, stop=True)
                t.matmul(P1[:, :], lt_s[:], oh2[:], start=True, stop=True)
                t.matmul(P2[0:1, :], ones_c[:], ohM1[:], start=True, stop=True)
                inst = t.matmul(P3[0:1, :], ones_c[:], ohM2[:], start=True, stop=True)
                S.done(inst, "T")
            tn(_)

            def _(v):
                S.bar(v)
                v.tensor_copy(totb[0:1, 0:512], P2[0:1, :])
                inst = v.tensor_copy(totb[0:1, 512:1024], P3[0:1, :])
                S.done(inst, "V")
            ve(_)

            def _(v):
                S.bar(v)
                for k in range(2):
                    tv = totb[0:1, k * 512:(k + 1) * 512].rearrange("o (e c) -> o e c", e=8)
                    inst = v.tensor_reduce(cntrow[0:1, k * 8:(k + 1) * 8], tv, AX.X, ALU.add)
                S.done(inst, "V")
            ve(_)

            def _(t):
                S.bar(t)
                t.matmul(P2[:, :], onesf_r[:], totb[0:1, 0:512], start=True, stop=True)
                inst = t.matmul(P3[:, :], onesf_r[:], totb[0:1, 512:1024], start=True, stop=True)
                S.done(inst, "T")
            tn(_)

            def _(v):
                S.bar(v)
                v.tensor_copy(totb[:, 0:512], P2[:, :])
                inst = v.tensor_copy(totb[:, 512:1024], P3[:, :])
                S.done(inst, "V")
            ve(_)

            def _(v):
                S.bar(v)
                for k in range(2):
                    for ee in range(8):
                        sl = slice(k * 512 + ee * 64, k * 512 + (ee + 1) * 64)
                        inst = v.tensor_tensor_scan(tscab[:, sl], onescol_b[:], totb[:, sl],
                                                    0.0, ALU.mult, ALU.add)
                S.done(inst, "V")
            ve(_)

            def _(v):
                S.bar(v)
                inst = v.tensor_sub(tscab[:], tscab[:], totb[:])
                S.done(inst, "V")
            ve(_)

            for k in range(2):
                def _(v, k=k):
                    S.bar(v)
                    dst = pos1 if k == 0 else pos2
                    psrc = P0 if k == 0 else P1
                    exv = tscab[:, k * 512:(k + 1) * 512].rearrange("p (e c) -> p c e", e=8)
                    inst = v.tensor_tensor(dst[:].rearrange("p (c e) -> p c e", e=8),
                                           psrc[:, :].rearrange("p (c e) -> p c e", e=8),
                                           exv, ALU.add)
                    S.done(inst, "V")
                ve(_)

            def _(v):
                S.bar(v)
                v.tensor_tensor(S3, pos1[:].rearrange("p (c e) -> p c e", e=8),
                                oh1[:].rearrange("p (c e) -> p c e", e=8), ALU.mult)
                inst = v.tensor_tensor(S3B, pos2[:].rearrange("p (c e) -> p c e", e=8),
                                       oh2[:].rearrange("p (c e) -> p c e", e=8), ALU.mult)
                S.done(inst, "V")
            ve(_)

            def _(v):
                S.bar(v)
                v.tensor_reduce(po1[:], S3, AX.X, ALU.add)
                inst = v.tensor_reduce(po2[:], S3B, AX.X, ALU.add)
                S.done(inst, "V")
            ve(_)

            def _(v):
                S.bar(v)
                m3 = meta_sb[:].rearrange("p (c q) -> p c q", q=8)
                for qix, src in enumerate([i1_t, i2_t, po1, po2, va1_t, va2_t, w1_t, w2_t]):
                    v.tensor_copy(m3[:, :, qix:qix + 1],
                                  src[:].rearrange("p (c u) -> p c u", u=1))
                v.tensor_scalar(out=red1[:], in0=i1_t[:], scalar1=8192.0,
                                scalar2=0.0, op0=ALU.mult, op1=ALU.add)
                v.tensor_scalar(out=mx_t[:], in0=i2_t[:], scalar1=8192.0,
                                scalar2=65536.0, op0=ALU.mult, op1=ALU.add)
                v.tensor_scalar(out=s_t[:], in0=va1_t[:], scalar1=-131200.0,
                                scalar2=131200.0, op0=ALU.mult, op1=ALU.add)
                inst = v.tensor_scalar(out=u1_t[:], in0=va2_t[:], scalar1=-131200.0,
                                       scalar2=131200.0, op0=ALU.mult, op1=ALU.add)
                S.done(inst, "V")
            ve(_)

            def _(v):
                S.bar(v)
                v.tensor_add(red1[:], red1[:], po1[:])
                inst = v.tensor_add(mx_t[:], mx_t[:], po2[:])
                S.done(inst, "V")
            ve(_)

            def _(v):
                S.bar(v)
                v.tensor_mul(red1[:], red1[:], va1_t[:])
                inst = v.tensor_mul(mx_t[:], mx_t[:], va2_t[:])
                S.done(inst, "V")
            ve(_)

            def _(v):
                S.bar(v)
                v.tensor_add(red1[:], red1[:], s_t[:])
                inst = v.tensor_add(mx_t[:], mx_t[:], u1_t[:])
                S.done(inst, "V")
            ve(_)

            def _(v):
                S.bar(v)
                v.tensor_copy(offi[:, 0:64], red1[:])
                v.tensor_copy(offi[:, 64:128], mx_t[:])
                s20 = scv[:, 0:128].rearrange("p (c u) -> p c u", u=2)
                s21 = scv[:, 128:256].rearrange("p (c u) -> p c u", u=2)
                v.tensor_copy(s20[:, :, 0:1], tokid_s[:].rearrange("p (c u) -> p c u", u=1))
                v.tensor_copy(s21[:, :, 0:1], tokid_s[:].rearrange("p (c u) -> p c u", u=1))
                v.tensor_copy(s20[:, :, 1:2], w1_t[:].rearrange("p (c u) -> p c u", u=1))
                inst = v.tensor_copy(s21[:, :, 1:2], w2_t[:].rearrange("p (c u) -> p c u", u=1))
                S.done(inst, "V")
            ve(_)

            def _(sync):
                S.bar(sync)
                inst = sync.dma_start(out=cnt_loc[:, :], in_=cntrow[:])
                inst.then_inc(S.sem["O"], 16)
                S.cnt["O"] += 16
            sy(_)

            def _(g):
                S.bar(g)
                for k in range(2):
                    for ch in range(64):
                        inst = g.indirect_dma_start(
                            smap_loc[:, :],
                            IndirectOffsetOnAxis(offi[:, k * 64 + ch:k * 64 + ch + 1], 0),
                            scv[:, k * 128 + 2 * ch:k * 128 + 2 * ch + 2], None)
                        inst.then_inc(S.sem["O"], 16)
                S.cnt["O"] += 128 * 16
            gp(_)

            def _(g):
                S.bar(g)
                g.collective_compute(
                    "AllGather", ALU.bypass, replica_groups=RG,
                    ins=[cnt_loc.ap().opt()], outs=[cnt_all.ap().opt()],
                ).then_inc(S.sem["C"])
                g.collective_compute(
                    "AllGather", ALU.bypass, replica_groups=RG,
                    ins=[smap_loc.ap().opt()], outs=[smap_all.ap().opt()],
                ).then_inc(S.sem["C"])
                g.collective_compute(
                    "AllGather", ALU.bypass, replica_groups=RG,
                    ins=[xn2_loc.ap().opt()], outs=[xn2_all.ap().opt()],
                ).then_inc(S.sem["C"])
                S.cnt["C"] += 3
            gp(_)

        with ExitStack() as ctx2:
            e2 = ctx2.enter_context
            w1_b = e2(nc.sbuf_tensor("w1_b", [128, 8 * 4096], BF16))
            w2_b = e2(nc.sbuf_tensor("w2_b", [128, 32 * 1024], BF16))
            b1_s = e2(nc.sbuf_tensor("b1_s", [128, 32], F32))
            b2lsb = e2(nc.sbuf_tensor("b2lsb", [1, 1024], BF16))
            onesf16 = e2(nc.sbuf_tensor("onesf16", [1, 16], F32))
            onesc16 = e2(nc.sbuf_tensor("onesc16", [16, 1], F32))
            tvrow = e2(nc.sbuf_tensor("tvrow", [1, 1], F32))
            gpre_sb = e2(nc.sbuf_tensor("gpre_sb", [16, 16], F32))
            ehot_sb = e2(nc.sbuf_tensor("ehot_sb", [16, 8], F32))
            base_sb = e2(nc.sbuf_tensor("base_sb", [1, 16], F32))
            gch_sb = e2(nc.sbuf_tensor("gch_sb", [16, 1], F32))
            ccol_sb = e2(nc.sbuf_tensor("ccol_sb", [16, 1], F32))
            grow_sb = e2(nc.sbuf_tensor("grow_sb", [1, 16], F32))
            gb_sb = e2(nc.sbuf_tensor("gb_sb", [128, 16], F32))
            bb_sb = e2(nc.sbuf_tensor("bb_sb", [128, 16], F32))
            jpat = e2(nc.sbuf_tensor("jpat", [128, 16], F32))
            tv_sb = e2(nc.sbuf_tensor("tv_sb", [128, 1], F32))
            slots = e2(nc.sbuf_tensor("slots", [128, 80], F32))
            bsel = e2(nc.sbuf_tensor("bsel", [128, 40 * 16], F32))
            gsel = e2(nc.sbuf_tensor("gsel", [128, 80], F32))
            srcr = e2(nc.sbuf_tensor("srcr", [128, 80], F32))
            wmask = e2(nc.sbuf_tensor("wmask", [128, 80], F32))
            srci = e2(nc.sbuf_tensor("srci", [128, 80], I32))
            tokw = e2(nc.sbuf_tensor("tokw", [128, 80 * 2], F32))
            tokf = e2(nc.sbuf_tensor("tokf", [128, 80], F32))
            wf = e2(nc.sbuf_tensor("wf", [128, 80], F32))
            toki = e2(nc.sbuf_tensor("toki", [128, 80], I32))
            ctx2b = ExitStack()
            e2b = ctx2b.enter_context
            lsb = e2b(nc.sbuf_tensor("lsb", [128, 1024], F32))
            lsr_s = e2b(nc.sbuf_tensor("lsr_s", [1, 1024], F32))
            b2_s = e2b(nc.sbuf_tensor("b2_s", [1, 1024], F32))

            PS = e2(nc.psum_tensor("PS", [16, 16], F32))
            PS3 = e2(nc.psum_tensor("PS3", [1, 16], F32))
            PT2a = e2(nc.psum_tensor("PT2a", [128, 1024], BF16))
            PT2b = e2(nc.psum_tensor("PT2b", [128, 1024], BF16))

            def _(sync):
                S.bar(sync)
                sync.dma_start(out=cnt_sb[:],
                               in_=cnt_all.rearrange("c (k e) -> (c k) e", e=8)[:, :]
                               ).then_inc(S.sem["Q"], 16)
                sync.dma_start(out=gpre_sb[:], in_=c_gpre[:, :]).then_inc(S.sem["Q"], 16)
                sync.dma_start(out=ehot_sb[0:1, :], in_=c_ehot[:, :]).then_inc(S.sem["Q"], 16)
                sync.dma_start(out=base_sb[:], in_=c_base[:, :]).then_inc(S.sem["Q"], 16)
                sync.dma_start(out=b1_s[:], in_=b1c[:, :]).then_inc(S.sem["Q"], 16)
                sync.dma_start(out=lsr_s[:], in_=lsr[:, :]).then_inc(S.sem["Q"], 16)
                sync.dma_start(out=b2_s[:], in_=b2r[:, :]).then_inc(S.sem["Q"], 16)
                S.cnt["Q"] += 7 * 16
            sy(_)

            def _(g):
                S.bar(g)
                for ci in range(8):
                    g.dma_start(out=w1_b[:, ci * 4096:(ci + 1) * 4096],
                                in_=w1t[ci * 128:(ci + 1) * 128, :]).then_inc(S.sem["P"], 16)
                for hi in range(32):
                    g.dma_start(out=w2_b[:, hi * 1024:(hi + 1) * 1024],
                                in_=w2t[hi * 128:(hi + 1) * 128, :]).then_inc(S.sem["P"], 16)
                S.cnt["P"] += 40 * 16
                g.iota(slots[:], [[128, 80]], base=0, channel_multiplier=1,
                       allow_small_or_imprecise_dtypes=True).then_inc(S.sem["G"], 1)
                g.iota(jpat[:], [[0, 1], [1, 16]], base=0, channel_multiplier=0,
                       allow_small_or_imprecise_dtypes=True).then_inc(S.sem["G"], 1)
                S.cnt["G"] += 2
            gp(_)

            def _(v):
                S.bar(v)
                v.memset(onesf16[:], 1.0)
                v.memset(onesc16[:], 1.0)
                inst = v.tensor_mul(b2_s[:], b2_s[:], lsr_s[:])
                S.done(inst, "V")
            ve(_)

            def _(v):
                S.bar(v)
                inst = v.tensor_copy(b2lsb[:], b2_s[:])
                S.done(inst, "V")
            ve(_)

            def _(t):
                S.bar(t)
                t.matmul(P0[:, :], onesf_r[:], lsr_s[0:1, 0:512], start=True, stop=True)
                t.matmul(P1[:, :], onesf_r[:], lsr_s[0:1, 512:1024], start=True, stop=True)
                inst = t.matmul(PS[0:16, 0:8], onesf16[:], ehot_sb[0:1, :], start=True, stop=True)
                S.done(inst, "T")
            tn(_)

            def _(v):
                S.bar(v)
                v.tensor_copy(lsb[:, 0:512], P0[:, :])
                v.tensor_copy(lsb[:, 512:1024], P1[:, :])
                inst = v.tensor_copy(ehot_sb[:, :], PS[0:16, 0:8])
                S.done(inst, "V")
            ve(_)

            def _(v):
                S.bar(v)
                for hi in range(32):
                    inst = v.tensor_mul(w2_b[:, hi * 1024:(hi + 1) * 1024],
                                        w2_b[:, hi * 1024:(hi + 1) * 1024], lsb[:])
                S.done(inst, "V")
            ve(_)

            def _(t):
                S.bar(t)
                inst = t.matmul(PS[:, 0:8], gpre_sb[:], cnt_sb[:], start=True, stop=True)
                S.done(inst, "T")
            tn(_)

            def _(v):
                S.bar(v)
                v.tensor_tensor(bsel[0:16, 0:8], PS[:, 0:8], ehot_sb[:, :], ALU.mult)
                inst = v.tensor_tensor(bsel[0:16, 8:16], cnt_sb[:], ehot_sb[:, :], ALU.mult)
                S.done(inst, "V")
            ve(_)

            def _(v):
                S.bar(v)
                v.tensor_reduce(gch_sb[:], bsel[0:16, 0:8].rearrange("p (u e) -> p u e", u=1),
                                AX.X, ALU.add)
                inst = v.tensor_reduce(ccol_sb[:], bsel[0:16, 8:16].rearrange("p (u e) -> p u e", u=1),
                                       AX.X, ALU.add)
                S.done(inst, "V")
            ve(_)

            def _(t):
                S.bar(t)
                inst = t.matmul(PS3[0:1, 0:1], onesc16[:], ccol_sb[:], start=True, stop=True)
                S.done(inst, "T")
            tn(_)

            def _(v):
                S.bar(v)
                inst = v.tensor_copy(tvrow[:], PS3[0:1, 0:1])
                S.done(inst, "V")
            ve(_)

            def _(t):
                S.bar(t)
                inst = t.matmul(P0[:, 0:1], onesf_r[:], tvrow[0:1, :], start=True, stop=True)
                S.done(inst, "T")
            tn(_)

            def _(v):
                S.bar(v)
                inst = v.tensor_copy(tv_sb[:], P0[:, 0:1])
                S.done(inst, "V")
            ve(_)

            def _(sync):
                S.bar(sync)
                sync.dma_start(out=gscr[:, :], in_=gch_sb[:]).then_inc(S.sem["Q"], 16)
                S.cnt["Q"] += 16
            sy(_)

            def _(sync):
                S.bar(sync)
                inst = sync.dma_start(out=grow_sb[:], in_=gscr.rearrange("r u -> u r")[:, :],
                                      allow_slow_non_contiguous=True)
                inst.then_inc(S.sem["Q"], 16)
                S.cnt["Q"] += 16
            sy(_)

            def _(t):
                S.bar(t)
                t.matmul(P0[:, 0:16], onesf_r[:], grow_sb[0:1, :], start=True, stop=True)
                inst = t.matmul(P1[:, 0:16], onesf_r[:], base_sb[0:1, :], start=True, stop=True)
                S.done(inst, "T")
            tn(_)

            def _(v):
                S.bar(v)
                v.tensor_copy(gb_sb[:], P0[:, 0:16])
                inst = v.tensor_copy(bb_sb[:], P1[:, 0:16])
                S.done(inst, "V")
            ve(_)

            for hh in range(2):
                cs = slice(hh * 40, (hh + 1) * 40)
                b3 = bsel[:].rearrange("p (c j) -> p c j", j=16)
                sl3 = slots[:, cs].rearrange("p (c u) -> p c u", u=1).broadcast_to([128, 40, 16])
                g3 = gb_sb[:].rearrange("p (u j) -> p u j", u=1).broadcast_to([128, 40, 16])
                j3 = jpat[:].rearrange("p (u j) -> p u j", u=1).broadcast_to([128, 40, 16])
                i3 = srcr[:, cs].rearrange("p (c u) -> p c u", u=1).broadcast_to([128, 40, 16])
                bb3 = bb_sb[:].rearrange("p (u j) -> p u j", u=1).broadcast_to([128, 40, 16])

                def _(v, b3=b3, sl3=sl3, g3=g3):
                    S.bar(v)
                    inst = v.tensor_tensor(b3, sl3, g3, ALU.is_ge)
                    S.done(inst, "V")
                ve(_)

                def _(v, b3=b3, cs=cs):
                    S.bar(v)
                    inst = v.tensor_reduce(srcr[:, cs], b3, AX.X, ALU.add)
                    S.done(inst, "V")
                ve(_)

                def _(v, b3=b3, g3=g3, cs=cs):
                    S.bar(v)
                    v.tensor_scalar_add(srcr[:, cs], srcr[:, cs], -1.0)
                    inst = v.tensor_tensor(b3, b3, g3, ALU.mult)
                    S.done(inst, "V")
                ve(_)

                def _(v, b3=b3, j3=j3, i3=i3, cs=cs):
                    S.bar(v)
                    v.tensor_reduce(gsel[:, cs], b3, AX.X, ALU.max)
                    inst = v.tensor_tensor(b3, j3, i3, ALU.is_equal)
                    S.done(inst, "V")
                ve(_)

                def _(v, b3=b3, bb3=bb3):
                    S.bar(v)
                    inst = v.tensor_tensor(b3, b3, bb3, ALU.mult)
                    S.done(inst, "V")
                ve(_)

                def _(v, b3=b3, cs=cs):
                    S.bar(v)
                    inst = v.tensor_reduce(wmask[:, cs], b3, AX.X, ALU.add)
                    S.done(inst, "V")
                ve(_)

            def _(v):
                S.bar(v)
                inst = v.tensor_sub(gsel[:], slots[:], gsel[:])
                S.done(inst, "V")
            ve(_)

            def _(v):
                S.bar(v)
                inst = v.tensor_add(srcr[:], wmask[:], gsel[:])
                S.done(inst, "V")
            ve(_)

            def _(v):
                S.bar(v)
                v.tensor_scalar(out=wmask[:], in0=slots[:], scalar1=tv_sb[:, 0:1],
                                scalar2=None, op0=ALU.is_lt)
                inst = v.tensor_single_scalar(tokf[:], slots[:], float(CAPACITY), ALU.is_lt)
                S.done(inst, "V")
            ve(_)

            def _(v):
                S.bar(v)
                inst = v.tensor_mul(wmask[:], wmask[:], tokf[:])
                S.done(inst, "V")
            ve(_)

            def _(v):
                S.bar(v)
                inst = v.tensor_mul(srcr[:], srcr[:], wmask[:])
                S.done(inst, "V")
            ve(_)

            def _(v):
                S.bar(v)
                inst = v.tensor_copy(srci[:], srcr[:])
                S.done(inst, "V")
            ve(_)

            def _(g):
                S.bar(g)
                for ch in range(80):
                    inst = g.indirect_dma_start(
                        tokw[:, 2 * ch:2 * ch + 2], None,
                        smap_all[:, :], IndirectOffsetOnAxis(srci[:, ch:ch + 1], 0))
                    inst.then_inc(S.sem["P"], 16)
                S.cnt["P"] += 80 * 16
            gp(_)

            def _(v):
                S.bar(v)
                t3 = tokw[:].rearrange("p (c u) -> p c u", u=2)
                w3 = wmask[:].rearrange("p (c u) -> p c u", u=1)
                v.tensor_tensor(tokf[:].rearrange("p (c u) -> p c u", u=1), t3[:, :, 0:1], w3, ALU.mult)
                inst = v.tensor_tensor(wf[:].rearrange("p (c u) -> p c u", u=1), t3[:, :, 1:2], w3, ALU.mult)
                S.done(inst, "V")
            ve(_)

            def _(v):
                S.bar(v)
                inst = v.tensor_copy(toki[:], tokf[:])
                S.done(inst, "V")
            ve(_)

            ctx2b.close()
            h_b = e2(nc.sbuf_tensor("h_b", [128, 32 * 512], BF16))
            toks0 = e2(nc.sbuf_tensor("toks0", [128, 4096], BF16))
            toks1 = e2(nc.sbuf_tensor("toks1", [128, 4096], BF16))
            tfm = e2(nc.sbuf_tensor("tfm", [128, 8 * 512], BF16))
            sres = e2(nc.sbuf_tensor("sres", [128, 4 * 1024], BF16))

            NGRP = 20
            T0 = S.cnt["T"]
            A0 = S.cnt["A"]
            V0 = S.cnt["V"]
            O0 = S.cnt["O"]

            def t_round_m(g, r):
                return T0 + g * 44 + r + 1

            def t_fc1_m(g, hi):
                return T0 + g * 44 + 4 + hi + 1

            def t_fc2_m(g, s):
                return T0 + g * 44 + 36 + s + 1

            def a_gelu_m(g, hi):
                return A0 + g * 32 + hi + 1

            def v_evac_m(g, r):
                return V0 + g * 12 + r + 1

            def v_epi_m(g, s):
                return V0 + g * 12 + 4 + s + 1

            def _(g):
                S.bar(g)
                for j in range(4):
                    inst = g.indirect_dma_start(
                        toks0[:, j * 1024:(j + 1) * 1024], None,
                        xn2_all[:, :],
                        IndirectOffsetOnAxis(toki[:, j:j + 1], 0))
                    inst.then_inc(S.sem["P"], 16)
                S.cnt["P"] += 4 * 16
            gp(_)
            P0c = S.cnt["P"]

            def p_gath_m(g):
                return P0c + g * 64

            def o_store_m(g):
                return O0 + (g + 1) * 64

            for grp in range(NGRP):
                tok_cur = toks0 if grp % 2 == 0 else toks1
                tok_nxt = toks1 if grp % 2 == 0 else toks0

                if grp + 1 < NGRP:
                    def _(g, grp=grp, tok_nxt=tok_nxt):
                        if grp >= 1:
                            g.wait_ge(S.sem["V"], v_epi_m(grp - 1, 7))
                        for j in range(4):
                            inst = g.indirect_dma_start(
                                tok_nxt[:, j * 1024:(j + 1) * 1024], None,
                                xn2_all[:, :],
                                IndirectOffsetOnAxis(
                                    toki[:, (grp + 1) * 4 + j:(grp + 1) * 4 + j + 1], 0))
                            inst.then_inc(S.sem["P"], 16)
                        S.cnt["P"] += 4 * 16
                    gp(_)

                def _(t, grp=grp, tok_cur=tok_cur):
                    t.wait_ge(S.sem["P"], p_gath_m(grp))
                    for r in range(4):
                        gi = grp * 4 + r
                        if gi >= 2:
                            g2, r2 = divmod(gi - 2, 4)
                            t.wait_ge(S.sem["V"], v_evac_m(g2, r2))
                        pt = PT2a if gi % 2 == 0 else PT2b
                        for ci in range(8):
                            inst = t.transpose(
                                pt[:, ci * 128:(ci + 1) * 128],
                                tok_cur[:, r * 1024 + ci * 128:r * 1024 + (ci + 1) * 128],
                                idb_s[:])
                        S.done(inst, "T")
                        assert S.cnt["T"] == t_round_m(grp, r)
                tn(_)

                def _(v, grp=grp):
                    for r in range(4):
                        gi = grp * 4 + r
                        v.wait_ge(S.sem["T"], t_round_m(grp, r))
                        pt = PT2a if gi % 2 == 0 else PT2b
                        for ci in range(8):
                            inst = v.tensor_copy(
                                tfm[:, ci * 512 + r * 128:ci * 512 + (r + 1) * 128],
                                pt[:, ci * 128:(ci + 1) * 128])
                        S.done(inst, "V")
                        assert S.cnt["V"] == v_evac_m(grp, r)
                ve(_)

                def _(t, grp=grp):
                    t.wait_ge(S.sem["V"], v_evac_m(grp, 3))
                    for hi in range(32):
                        b = grp * 32 + hi
                        if b >= 2:
                            g2, h2 = divmod(b - 2, 32)
                            t.wait_ge(S.sem["A"], a_gelu_m(g2, h2))
                        p = P0 if hi % 2 == 0 else P1
                        for ci in range(8):
                            inst = t.matmul(p[:, :], w1_b[:, ci * 4096 + hi * 128:
                                                          ci * 4096 + (hi + 1) * 128],
                                            tfm[:, ci * 512:(ci + 1) * 512],
                                            start=(ci == 0), stop=(ci == 7))
                        S.done(inst, "T")
                        assert S.cnt["T"] == t_fc1_m(grp, hi)
                    t.wait_ge(S.sem["A"], a_gelu_m(grp, 31))
                    for s in range(8):
                        j, hf = divmod(s, 2)
                        c = grp * 8 + s
                        if c >= 2:
                            g2, s2 = divmod(c - 2, 8)
                            t.wait_ge(S.sem["V"], v_epi_m(g2, s2))
                        p = P2 if s % 2 == 0 else P3
                        for hi in range(32):
                            t.matmul(p[:, :],
                                     h_b[:, hi * 512 + j * 128:hi * 512 + (j + 1) * 128],
                                     w2_b[:, hi * 1024 + hf * 512:hi * 1024 + (hf + 1) * 512],
                                     start=(hi == 0), stop=False)
                        inst = t.matmul(p[:, :], ones_r[:],
                                        b2lsb[0:1, hf * 512:(hf + 1) * 512],
                                        start=False, stop=True)
                        S.done(inst, "T")
                        assert S.cnt["T"] == t_fc2_m(grp, s)
                tn(_)

                def _(s_, grp=grp):
                    for hi in range(32):
                        s_.wait_ge(S.sem["T"], t_fc1_m(grp, hi))
                        p = P0 if hi % 2 == 0 else P1
                        inst = s_.activation(out=h_b[:, hi * 512:(hi + 1) * 512], in_=p[:, :],
                                             func=AF.Gelu, bias=b1_s[:, hi:hi + 1])
                        S.done(inst, "A")
                        assert S.cnt["A"] == a_gelu_m(grp, hi)
                sc(_)

                def _(v, grp=grp, tok_cur=tok_cur):
                    for s in range(8):
                        j, hf = divmod(s, 2)
                        v.wait_ge(S.sem["T"], t_fc2_m(grp, s))
                        if s == 0 and grp >= 1:
                            v.wait_ge(S.sem["O"], o_store_m(grp - 1))
                        p = P2 if s % 2 == 0 else P3
                        sl = slice(j * 1024 + hf * 512, j * 1024 + (hf + 1) * 512)
                        v.tensor_add(bsel[:, 0:512], p[:, :], tok_cur[:, sl])
                        inst = v.tensor_scalar(out=sres[:, sl], in0=bsel[:, 0:512],
                                               scalar1=wf[:, grp * 4 + j:grp * 4 + j + 1],
                                               scalar2=None, op0=ALU.mult)
                        S.done(inst, "V")
                        assert S.cnt["V"] == v_epi_m(grp, s)
                ve(_)

                def _(sync, grp=grp):
                    sync.wait_ge(S.sem["V"], v_epi_m(grp, 7))
                    for j in range(4):
                        r0 = grp * 512 + j * 128
                        inst = sync.dma_start(out=sres_loc[r0:r0 + 128, :],
                                              in_=sres[:, j * 1024:(j + 1) * 1024])
                        inst.then_inc(S.sem["O"], 16)
                    S.cnt["O"] += 4 * 16
                    assert S.cnt["O"] == o_store_m(grp)
                sy(_)

            def _(g):
                S.bar(g)
                g.collective_compute(
                    "AllGather", ALU.bypass, replica_groups=RG,
                    ins=[sres_loc.ap().opt()], outs=[sres_all.ap().opt()],
                ).then_inc(S.sem["C"])
                S.cnt["C"] += 1
            gp(_)

        with ExitStack() as ctx3:
            e3x = ctx3.enter_context
            s16_sb = e3x(nc.sbuf_tensor("s16_sb", [16, 2], F32))
            g1row = e3x(nc.sbuf_tensor("g1row", [1, 8], F32))
            g2row = e3x(nc.sbuf_tensor("g2row", [1, 8], F32))
            gb1 = e3x(nc.sbuf_tensor("gb1", [128, 8], F32))
            gb2 = e3x(nc.sbuf_tensor("gb2", [128, 8], F32))
            epat3 = e3x(nc.sbuf_tensor("epat3", [128, 512], F32))
            scr3 = e3x(nc.sbuf_tensor("scr3", [128, 512], F32))
            scrb3 = e3x(nc.sbuf_tensor("scrb3", [128, 512], F32))
            keep3 = e3x(nc.sbuf_tensor("keep3", [128, 64], F32))
            off1 = e3x(nc.sbuf_tensor("off1", [128, 64], F32))
            off2 = e3x(nc.sbuf_tensor("off2", [128, 64], F32))
            oi1 = e3x(nc.sbuf_tensor("oi1", [128, 64], I32))
            oi2 = e3x(nc.sbuf_tensor("oi2", [128, 64], I32))
            gt = e3x(nc.sbuf_tensor("gt", [128, 1024], BF16))
            x2tile = e3x(nc.sbuf_tensor("x2tile", [128, 1024], BF16))
            ytile = e3x(nc.sbuf_tensor("ytile", [128, 1024], F32))
            PB = e3x(nc.psum_tensor("PB", [128, 16], F32))
            PSb = e3x(nc.psum_tensor("PSb", [1, 16], F32))

            def _(sync):
                S.bar(sync)
                sync.dma_start(out=s16_sb[:], in_=s16own[:, :]).then_inc(S.sem["Q"], 16)
                S.cnt["Q"] += 16
            sy(_)

            def _(g):
                S.bar(g)
                inst = g.iota(epat3[:], [[0, 64], [1, 8]], base=0, channel_multiplier=0,
                              allow_small_or_imprecise_dtypes=True)
                inst.then_inc(S.sem["G"], 1)
                S.cnt["G"] += 1
            gp(_)

            def _(t):
                S.bar(t)
                inst = t.matmul(PSb[0:1, 0:8], s16_sb[:, 0:1], cnt_sb[:], start=True, stop=True)
                S.done(inst, "T")
            tn(_)

            def _(v):
                S.bar(v)
                inst = v.tensor_copy(g1row[:], PSb[0:1, 0:8])
                S.done(inst, "V")
            ve(_)

            def _(t):
                S.bar(t)
                inst = t.matmul(PSb[0:1, 0:8], s16_sb[:, 1:2], cnt_sb[:], start=True, stop=True)
                S.done(inst, "T")
            tn(_)

            def _(v):
                S.bar(v)
                inst = v.tensor_copy(g2row[:], PSb[0:1, 0:8])
                S.done(inst, "V")
            ve(_)

            def _(t):
                S.bar(t)
                t.matmul(PB[:, 0:8], onesf_r[:], g1row[0:1, :], start=True, stop=False)
                inst = t.matmul(PB[:, 8:16], onesf_r[:], g2row[0:1, :], start=False, stop=True)
                S.done(inst, "T")
            tn(_)

            def _(v):
                S.bar(v)
                v.tensor_copy(gb1[:], PB[:, 0:8])
                inst = v.tensor_copy(gb2[:], PB[:, 8:16])
                S.done(inst, "V")
            ve(_)

            for k in range(2):
                m3 = meta_sb[:].rearrange("p (c q) -> p c q", q=8)
                e3 = epat3[:].rearrange("p (c e) -> p c e", e=8)
                s3 = scr3[:].rearrange("p (c e) -> p c e", e=8)
                s3b = scrb3[:].rearrange("p (c e) -> p c e", e=8)
                idx = m3[:, :, k:k + 1]
                po = m3[:, :, 2 + k:3 + k]
                va = m3[:, :, 4 + k:5 + k]
                gb = gb1 if k == 0 else gb2
                off = off1 if k == 0 else off2
                oi = oi1 if k == 0 else oi2
                o3 = off[:].rearrange("p (c u) -> p c u", u=1)
                k3 = keep3[:].rearrange("p (c u) -> p c u", u=1)

                def _(v, s3=s3, e3=e3, idx=idx):
                    S.bar(v)
                    inst = v.tensor_tensor(s3, e3, idx.broadcast_to([128, 64, 8]),
                                           ALU.is_equal)
                    S.done(inst, "V")
                ve(_)

                def _(v, s3=s3, s3b=s3b, gb=gb):
                    S.bar(v)
                    inst = v.tensor_tensor(s3b, s3,
                                           gb[:].rearrange("p (u e) -> p u e", u=1)
                                           .broadcast_to([128, 64, 8]), ALU.mult)
                    S.done(inst, "V")
                ve(_)

                def _(v, off=off, s3b=s3b):
                    S.bar(v)
                    inst = v.tensor_reduce(off[:], s3b, AX.X, ALU.add)
                    S.done(inst, "V")
                ve(_)

                def _(v, o3=o3, po=po, idx=idx):
                    S.bar(v)
                    v.tensor_tensor(o3, o3, po, ALU.add)
                    inst = v.tensor_scalar(
                        out=scrb3[:, 0:64].rearrange("p (c u) -> p c u", u=1), in0=idx,
                        scalar1=float(CPAD), scalar2=None, op0=ALU.mult)
                    S.done(inst, "V")
                ve(_)

                def _(v, off=off):
                    S.bar(v)
                    v.tensor_single_scalar(keep3[:], off[:], float(CAPACITY), ALU.is_lt)
                    inst = v.tensor_add(off[:], off[:], scrb3[:, 0:64])
                    S.done(inst, "V")
                ve(_)

                def _(v, k3=k3, va=va):
                    S.bar(v)
                    inst = v.tensor_tensor(k3, k3, va, ALU.mult)
                    S.done(inst, "V")
                ve(_)

                def _(v, off=off):
                    S.bar(v)
                    v.tensor_mul(off[:], off[:], keep3[:])
                    inst = v.tensor_scalar(out=scrb3[:, 64:128], in0=keep3[:],
                                           scalar1=-float(CPAD - 1), scalar2=float(CPAD - 1),
                                           op0=ALU.mult, op1=ALU.add)
                    S.done(inst, "V")
                ve(_)

                def _(v, off=off):
                    S.bar(v)
                    inst = v.tensor_add(off[:], off[:], scrb3[:, 64:128])
                    S.done(inst, "V")
                ve(_)

                def _(v, oi=oi, off=off):
                    S.bar(v)
                    inst = v.tensor_copy(oi[:], off[:])
                    S.done(inst, "V")
                ve(_)

            for ch in range(64):
                def _(g, ch=ch):
                    S.bar(g)
                    g.indirect_dma_start(
                        gt[:, :], None,
                        sres_all[:, :], IndirectOffsetOnAxis(oi1[:, ch:ch + 1], 0)
                    ).then_inc(S.sem["P"], 16)
                    S.cnt["P"] += 16
                    g.wait_ge(S.sem["P"], S.cnt["P"])
                    inst = g.indirect_dma_start(
                        gt[:, :], None,
                        sres_all[:, :], IndirectOffsetOnAxis(oi2[:, ch:ch + 1], 0),
                        compute_op=ALU.add)
                    inst.then_inc(S.sem["P"], 16)
                    S.cnt["P"] += 16
                gp(_)

                def _(sync, ch=ch):
                    S.bar(sync, skip=("P",))
                    inst = sync.dma_start(out=x2tile[:], in_=x2_loc[ch * 128:(ch + 1) * 128, :])
                    inst.then_inc(S.sem["Q"], 16)
                    S.cnt["Q"] += 16
                sy(_)

                def _(v, ch=ch):
                    S.bar(v)
                    inst = v.tensor_add(ytile[:], gt[:], x2tile[:])
                    S.done(inst, "V")
                ve(_)

                def _(sync, ch=ch):
                    S.bar(sync)
                    inst = sync.dma_start(out=yout[ch * 128:(ch + 1) * 128, :], in_=ytile[:])
                    inst.then_inc(S.sem["O"], 16)
                    S.cnt["O"] += 16
                sy(_)

            def _(sync):
                S.bar(sync)
                sync.nop()
            sy(_)
    return nc



_NC_CACHE = {}


def _get_nc(name, builder):
    if name not in _NC_CACHE:
        _NC_CACHE[name] = builder()
    return _NC_CACHE[name]


def kernel(L0_attn_x, L0_attn_W_qkv, L0_attn_RF, L0_attn_W_out, L0_attn_b_out,
           L0_ffn_router_w, L0_ffn_router_b, L0_ffn_fc1_w, L0_ffn_fc1_b,
           L0_ffn_fc2_w, L0_ffn_fc2_b, L0_ffn_layer_scale):
    x = np.asarray(L0_attn_x)
    wqkv = np.ascontiguousarray(np.asarray(L0_attn_W_qkv))
    rf = np.asarray(L0_attn_RF)
    wout = np.ascontiguousarray(np.asarray(L0_attn_W_out))
    bout = np.asarray(L0_attn_b_out)
    rw = np.asarray(L0_ffn_router_w)
    rb = np.asarray(L0_ffn_router_b)
    fc1w = np.asarray(L0_ffn_fc1_w)
    fc1b = np.asarray(L0_ffn_fc1_b)
    fc2w = np.asarray(L0_ffn_fc2_w)
    fc2b = np.asarray(L0_ffn_fc2_b)
    ls = np.asarray(L0_ffn_layer_scale)

    rfbd = np.zeros((8, 128, 128), np.float32)
    for pr in range(8):
        rfbd[pr, 0:64, 0:64] = rf[2 * pr]
        rfbd[pr, 64:128, 64:128] = rf[2 * pr + 1]
    rfbd = rfbd.reshape(8 * 128, 128)
    lt = np.triu(np.ones((128, 128), np.float32), 1).astype(bfdt)
    idb = np.eye(128, dtype=np.float32).astype(bfdt)
    idf = np.eye(128, dtype=np.float32)
    boutT = np.ascontiguousarray(bout.reshape(8, 128).T)
    rwt = np.ascontiguousarray(rw.T)
    rbrow = np.ascontiguousarray(rb.reshape(1, 8))

    gpre = np.zeros((16, 16), np.float32)
    for cp in range(8):
        for kp in range(2):
            for k in range(2):
                for c in range(8):
                    if (kp < k) or (kp == k and cp < c):
                        gpre[cp * 2 + kp, k * 8 + c] = 1.0

    nc = _get_nc("FUSED", build_fused)
    ins = []
    for c in range(8):
        xs = np.ascontiguousarray(x[2 * c:2 * c + 2].reshape(TPC, DM).T)
        tok = (c * TPC + np.arange(64)[None, :] * 128 +
               np.arange(128)[:, None]).astype(np.float32)
        basr = np.zeros((1, 16), np.float32)
        for k in range(2):
            for cc in range(8):
                basr[0, k * 8 + cc] = cc * SLOTROWS + (k * 8 + c) * 8192
        ehot = np.zeros((1, 8), np.float32)
        ehot[0, c] = 1.0
        s16o = np.zeros((16, 2), np.float32)
        for cp in range(8):
            for kp in range(2):
                for k in range(2):
                    if (kp < k) or (kp == k and cp < c):
                        s16o[cp * 2 + kp, k] = 1.0
        ins.append({
            "xfm": xs, "wqkv": wqkv, "wout": wout, "boutT": boutT,
            "rfbd": rfbd, "rwt": rwt, "rbrow": rbrow, "tokid": tok,
            "c_lt": lt, "c_idb": idb, "c_idf": idf,
            "w1t": np.ascontiguousarray(fc1w[c].T),
            "w2t": np.ascontiguousarray(fc2w[c].T),
            "b1c": np.ascontiguousarray(fc1b[c].reshape(32, 128).T),
            "lsr": np.ascontiguousarray(ls[c:c + 1]),
            "b2r": np.ascontiguousarray(fc2b[c:c + 1]),
            "c_gpre": gpre, "c_ehot": ehot, "c_base": basr,
            "s16own": s16o,
        })
    res = run_bass_kernel_spmd(nc, ins, core_ids=list(range(8))).results

    y = np.concatenate([np.asarray(res[c]["yout"]) for c in range(8)], axis=0)
    return y.reshape(B_GLOBAL, SEQ, DM)


# version 09
# speedup vs baseline: 1.7040x; 1.0359x over previous
from contextlib import ExitStack

import numpy as np
import ml_dtypes
import concourse.bass as bass
import concourse.mybir as mybir
from concourse.bass_utils import run_bass_kernel_spmd
from concourse.bass import IndirectOffsetOnAxis

F32 = mybir.dt.float32
BF16 = mybir.dt.bfloat16
I32 = mybir.dt.int32
AF = mybir.ActivationFunctionType
ALU = mybir.AluOpType
AX = mybir.AxisListType

B_GLOBAL = 16
SEQ = 4096
DM = 1024
TPC = 8192
NBATCH = 2
N_TOK = B_GLOBAL * SEQ
CAPACITY = int(1.2 * (N_TOK // 8 + 1))
CPAD = 10240
HID = 4096
SLOTROWS = 131328
STAB = 1e-6
LN_EPS = 1e-5
REPS = 1e-9
RG = [list(range(8))]
PAR_A = False
PAR_B = False

bfdt = ml_dtypes.bfloat16


class K:

    def __init__(self, nc, ctx, block):
        self.nc = nc
        self.block = block
        self.sem = {}
        self.cnt = {}
        for name in ("Q", "P", "T", "A", "V", "O", "G", "C"):
            self.sem[name] = ctx.enter_context(nc.semaphore("s" + name))
            self.cnt[name] = 0

    def bar(self, eng, skip=()):
        for name, h in self.sem.items():
            if name in skip:
                continue
            if self.cnt[name] > 0:
                eng.wait_ge(h, self.cnt[name])

    def done(self, inst, q, k=1):
        inst.then_inc(self.sem[q], k)
        self.cnt[q] += k


def build_fused():
    nc = bass.Bass("TRN2", target_bir_lowering=False, debug=False)
    xfm = nc.dram_tensor("xfm", [DM, TPC], F32, kind="ExternalInput")
    wqkv = nc.dram_tensor("wqkv", [DM, 3 * DM], F32, kind="ExternalInput")
    wout = nc.dram_tensor("wout", [DM, DM], F32, kind="ExternalInput")
    boutT = nc.dram_tensor("boutT", [128, 8], F32, kind="ExternalInput")
    rfbd = nc.dram_tensor("rfbd", [8 * 128, 128], F32, kind="ExternalInput")
    rwt = nc.dram_tensor("rwt", [DM, 8], F32, kind="ExternalInput")
    rbrow = nc.dram_tensor("rbrow", [1, 8], F32, kind="ExternalInput")
    tokid = nc.dram_tensor("tokid", [128, 64], F32, kind="ExternalInput")
    c_lt = nc.dram_tensor("c_lt", [128, 128], BF16, kind="ExternalInput")
    c_idb = nc.dram_tensor("c_idb", [128, 128], BF16, kind="ExternalInput")
    c_idf = nc.dram_tensor("c_idf", [128, 128], F32, kind="ExternalInput")
    w1t = nc.dram_tensor("w1t", [DM, HID], F32, kind="ExternalInput")
    w2t = nc.dram_tensor("w2t", [HID, DM], F32, kind="ExternalInput")
    b1c = nc.dram_tensor("b1c", [128, 32], F32, kind="ExternalInput")
    lsr = nc.dram_tensor("lsr", [1, DM], F32, kind="ExternalInput")
    b2r = nc.dram_tensor("b2r", [1, DM], F32, kind="ExternalInput")
    c_gpre = nc.dram_tensor("c_gpre", [16, 16], F32, kind="ExternalInput")
    c_ehot = nc.dram_tensor("c_ehot", [1, 8], F32, kind="ExternalInput")
    c_base = nc.dram_tensor("c_base", [1, 16], F32, kind="ExternalInput")
    s16own = nc.dram_tensor("s16own", [16, 2], F32, kind="ExternalInput")
    xn2_loc = nc.dram_tensor("xn2_loc", [TPC, DM], BF16)
    xn2_all = nc.dram_tensor("xn2_all", [N_TOK, DM], BF16)
    x2_loc = nc.dram_tensor("x2_loc", [TPC, DM], BF16)
    smap_loc = nc.dram_tensor("smap_loc", [SLOTROWS, 2], F32)
    smap_all = nc.dram_tensor("smap_all", [8 * SLOTROWS, 2], F32)
    cnt_loc = nc.dram_tensor("cnt_loc", [1, 16], F32)
    cnt_all = nc.dram_tensor("cnt_all", [8, 16], F32)
    sres_loc = nc.dram_tensor("sres_loc", [CPAD, DM], BF16)
    sres_all = nc.dram_tensor("sres_all", [8 * CPAD, DM], BF16)
    gscr = nc.dram_tensor("gscr", [16, 1], F32)
    yout = nc.dram_tensor("yout", [TPC, DM], F32, kind="ExternalOutput")

    with ExitStack() as ctx:
        e = ctx.enter_context
        idb_s = e(nc.sbuf_tensor("idb_s", [128, 128], BF16))
        ones_c = e(nc.sbuf_tensor("ones_c", [128, 1], BF16))
        ones_r = e(nc.sbuf_tensor("ones_r", [1, 128], BF16))
        onesf_r = e(nc.sbuf_tensor("onesf_r", [1, 128], F32))
        meta_sb = e(nc.sbuf_tensor("meta_sb", [128, 512], F32))
        cnt_sb = e(nc.sbuf_tensor("cnt_sb", [16, 8], F32))
        P0 = e(nc.psum_tensor("P0", [128, 512], F32))
        P1 = e(nc.psum_tensor("P1", [128, 512], F32))
        P2 = e(nc.psum_tensor("P2", [128, 512], F32))
        P3 = e(nc.psum_tensor("P3", [128, 512], F32))

        block = e(nc.Block())
        S = K(nc, ctx, block)

        def gp(f):
            block.gpsimd(f)

        def tn(f):
            block.tensor(f)

        def sc(f):
            block.scalar(f)

        def ve(f):
            block.vector(f)

        def sy(f):
            block.sync(f)

        with ExitStack() as ctx1:
            e1 = ctx1.enter_context
            wq_b = e1(nc.sbuf_tensor("wq_b", [128, 8 * 2048], BF16))
            wo_b = e1(nc.sbuf_tensor("wo_b", [128, 8 * 1024], BF16))
            rf_b = e1(nc.sbuf_tensor("rf_b", [128, 8 * 128], BF16))
            rw_b = e1(nc.sbuf_tensor("rw_b", [128, 64], BF16))
            bout_s = e1(nc.sbuf_tensor("bout_s", [128, 8], F32))
            rb_s = e1(nc.sbuf_tensor("rb_s", [128, 8], F32))
            lt_s = e1(nc.sbuf_tensor("lt_s", [128, 128], BF16))
            idf_s = e1(nc.sbuf_tensor("idf_s", [128, 128], F32))
            epscol = e1(nc.sbuf_tensor("epscol", [128, 1], F32))
            stabcol = e1(nc.sbuf_tensor("stabcol", [128, 1], F32))
            tokid_s = e1(nc.sbuf_tensor("tokid_s", [128, 64], F32))
            zrow2 = e1(nc.sbuf_tensor("zrow2", [1, 2], F32))

            xbf0 = e1(nc.sbuf_tensor("xbf0", [128, 8 * 512], BF16))
            xbf1 = e1(nc.sbuf_tensor("xbf1", [128, 8 * 512], BF16))
            xsq = e1(nc.sbuf_tensor("xsq", [128, 8 * 512], BF16))
            xn1 = e1(nc.sbuf_tensor("xn1", [128, 8 * 512], BF16))

            mt = e1(nc.sbuf_tensor("mt", [1, 512], F32))
            vt = e1(nc.sbuf_tensor("vt", [1, 512], F32))
            rt = e1(nc.sbuf_tensor("rt", [1, 512], F32))
            m2t = e1(nc.sbuf_tensor("m2t", [1, 512], F32))
            v2t = e1(nc.sbuf_tensor("v2t", [1, 512], F32))
            r2t = e1(nc.sbuf_tensor("r2t", [1, 512], F32))
            rm2t = e1(nc.sbuf_tensor("rm2t", [1, 512], F32))
            m_bf = e1(nc.sbuf_tensor("m_bf", [1, 512], BF16))
            r_bf = e1(nc.sbuf_tensor("r_bf", [1, 512], BF16))
            Mb = e1(nc.sbuf_tensor("Mb", [128, 4096], BF16))
            Rb = e1(nc.sbuf_tensor("Rb", [128, 4096], BF16))
            m2col = e1(nc.sbuf_tensor("m2col", [128, 64], F32))
            r2col = e1(nc.sbuf_tensor("r2col", [128, 64], F32))
            rm2col = e1(nc.sbuf_tensor("rm2col", [128, 64], F32))

            kbf2 = e1(nc.sbuf_tensor("kbf2", [128, 2 * 512], BF16))
            vbf2 = e1(nc.sbuf_tensor("vbf2", [128, 2 * 512], BF16))
            vtm2 = e1(nc.sbuf_tensor("vtm2", [128, 2 * 520], BF16))
            phik2 = e1(nc.sbuf_tensor("phik2", [128, 2 * 512], BF16))
            recp2 = e1(nc.sbuf_tensor("recp2", [128, 2 * 512], F32))
            kvacc = e1(nc.sbuf_tensor("kvacc", [128, 8 * 130], F32))
            kvbd = e1(nc.sbuf_tensor("kvbd", [128, 8 * 128], BF16))
            ksr = e1(nc.sbuf_tensor("ksr", [128, 8 * 128], BF16))
            qbf2 = kbf2
            phiq2 = phik2
            phiqn2 = vbf2
            attn_nt = e1(nc.sbuf_tensor("attn_nt", [128, 8 * 512], BF16))
            x2f = e1(nc.sbuf_tensor("x2f", [128, 512], F32))
            x2b2 = e1(nc.sbuf_tensor("x2b2", [128, 2 * 512], BF16))
            x2s2 = e1(nc.sbuf_tensor("x2s2", [128, 2 * 512], BF16))
            x2tmt = e1(nc.sbuf_tensor("x2tmt", [128, 4 * 1024], BF16))
            xn2buf = e1(nc.sbuf_tensor("xn2buf", [128, 4 * 1024], BF16))
            rawl = e1(nc.sbuf_tensor("rawl", [128, 64 * 8], F32))
            csum_b = e1(nc.sbuf_tensor("csum_b", [128, 8], F32))

            lgt = e1(nc.sbuf_tensor("lgt", [128, 64 * 8], F32))
            epat = e1(nc.sbuf_tensor("epat", [128, 64 * 8], F32))
            u_t = e1(nc.sbuf_tensor("u_t", [128, 64 * 8], F32))
            scr8 = e1(nc.sbuf_tensor("scr8", [128, 64 * 8], F32))
            scr8b = e1(nc.sbuf_tensor("scr8b", [128, 64 * 8], F32))
            red1 = e1(nc.sbuf_tensor("red1", [128, 64], F32))
            mx_t = e1(nc.sbuf_tensor("mx_t", [128, 64], F32))
            s_t = e1(nc.sbuf_tensor("s_t", [128, 64], F32))
            u1_t = e1(nc.sbuf_tensor("u1_t", [128, 64], F32))
            u2_t = e1(nc.sbuf_tensor("u2_t", [128, 64], F32))
            i1_t = e1(nc.sbuf_tensor("i1_t", [128, 64], F32))
            i2_t = e1(nc.sbuf_tensor("i2_t", [128, 64], F32))
            w1_t = e1(nc.sbuf_tensor("w1_t", [128, 64], F32))
            w2_t = e1(nc.sbuf_tensor("w2_t", [128, 64], F32))
            va1_t = e1(nc.sbuf_tensor("va1_t", [128, 64], F32))
            va2_t = e1(nc.sbuf_tensor("va2_t", [128, 64], F32))
            oh1 = e1(nc.sbuf_tensor("oh1", [128, 64 * 8], BF16))
            oh2 = e1(nc.sbuf_tensor("oh2", [128, 64 * 8], BF16))
            pos1 = lgt
            pos2 = u_t
            po1 = e1(nc.sbuf_tensor("po1", [128, 64], F32))
            po2 = e1(nc.sbuf_tensor("po2", [128, 64], F32))
            epat2 = e1(nc.sbuf_tensor("epat2", [128, 64 * 8], F32))
            ohM1 = e1(nc.sbuf_tensor("ohM1", [128, 64 * 8], BF16))
            ohM2 = e1(nc.sbuf_tensor("ohM2", [128, 64 * 8], BF16))
            totb = e1(nc.sbuf_tensor("totb", [128, 2 * 512], F32))
            tscab = e1(nc.sbuf_tensor("tscab", [128, 2 * 512], F32))
            onescol_b = e1(nc.sbuf_tensor("onescol_b", [128, 64], F32))
            cntrow = e1(nc.sbuf_tensor("cntrow", [1, 16], F32))
            scv = e1(nc.sbuf_tensor("scv", [128, 2 * 128], F32))
            eaff = e1(nc.sbuf_tensor("eaff", [128, 64 * 8], F32))
            offi = e1(nc.sbuf_tensor("offi", [128, 2 * 64], I32))

            PRa = e1(nc.psum_tensor("PRa", [128, 512], F32))
            PRb = e1(nc.psum_tensor("PRb", [128, 512], F32))
            P4 = e1(nc.psum_tensor("P4", [128, 512], F32))
            PT = e1(nc.psum_tensor("PT", [128, 1024], BF16))

            def _(sync):
                sync.dma_start(out=lt_s[:], in_=c_lt[:, :]).then_inc(S.sem["Q"], 16)
                sync.dma_start(out=idb_s[:], in_=c_idb[:, :]).then_inc(S.sem["Q"], 16)
                sync.dma_start(out=idf_s[:], in_=c_idf[:, :]).then_inc(S.sem["Q"], 16)
                sync.dma_start(out=bout_s[:], in_=boutT[:, :]).then_inc(S.sem["Q"], 16)
                sync.dma_start(out=tokid_s[:], in_=tokid[:, :]).then_inc(S.sem["Q"], 16)
                sync.dma_start(out=rb_s[0:1, :], in_=rbrow[:, :]).then_inc(S.sem["Q"], 16)
                S.cnt["Q"] += 6 * 16
            sy(_)

            def _(g):
                for qi in range(8):
                    g.dma_start(out=wo_b[:, qi * 1024:(qi + 1) * 1024],
                                in_=wout[qi * 128:(qi + 1) * 128, :]).then_inc(S.sem["P"], 16)
                for a in range(8):
                    g.dma_start(out=rf_b[:, a * 128:(a + 1) * 128],
                                in_=rfbd[a * 128:(a + 1) * 128, :]).then_inc(S.sem["P"], 16)
                for ci in range(8):
                    g.dma_start(out=rw_b[:, ci * 8:(ci + 1) * 8],
                                in_=rwt[ci * 128:(ci + 1) * 128, :]).then_inc(S.sem["P"], 16)
                S.cnt["P"] += 24 * 16
                g.iota(epat[:], [[0, 64], [1, 8]], base=0, channel_multiplier=0,
                       allow_small_or_imprecise_dtypes=True).then_inc(S.sem["G"], 1)
                g.iota(epat2[:], [[1, 8], [0, 64]], base=0, channel_multiplier=0,
                       allow_small_or_imprecise_dtypes=True).then_inc(S.sem["G"], 1)
                g.iota(eaff[:], [[0, 64], [1, 8]], base=64, channel_multiplier=0,
                       allow_small_or_imprecise_dtypes=True).then_inc(S.sem["G"], 1)
                S.cnt["G"] += 3
            gp(_)

            def _(v):
                S.bar(v)
                v.memset(ones_c[:], 1.0)
                v.memset(ones_r[:], 1.0)
                v.memset(onesf_r[:], 1.0)
                v.memset(epscol[:], LN_EPS)
                v.memset(stabcol[:], STAB)
                v.memset(onescol_b[:], 1.0)
                v.memset(kvacc[:], 0.0)
                v.memset(zrow2[:], 0.0)
                inst = v.memset(vtm2[:], 1.0)
                S.done(inst, "V")
            ve(_)

            def _(sync):
                S.bar(sync)
                inst = sync.dma_start(out=smap_loc[0:1, :], in_=zrow2[0:1, :])
                inst.then_inc(S.sem["Q"], 16)
                S.cnt["Q"] += 16
            sy(_)

            def _(t):
                S.bar(t)
                inst = t.matmul(P0[:, 0:8], onesf_r[:], rb_s[0:1, :], start=True, stop=True)
                S.done(inst, "T")
            tn(_)

            def _(v):
                S.bar(v)
                inst = v.tensor_copy(rb_s[:, :], P0[:, 0:8])
                S.done(inst, "V")
            ve(_)

            def _(t):
                S.bar(t)
                for ci in range(8):
                    inst = t.matmul(PRa[0:1, 0:8], ones_c[:], rw_b[:, ci * 8:(ci + 1) * 8],
                                    start=(ci == 0), stop=(ci == 7))
                S.done(inst, "T")
            tn(_)

            def _(v):
                S.bar(v)
                inst = v.tensor_scalar_mul(csum_b[0:1, :], PRa[0:1, 0:8], -1.0)
                S.done(inst, "V")
            ve(_)

            def _(t):
                S.bar(t)
                inst = t.matmul(P0[:, 0:8], onesf_r[:], csum_b[0:1, 0:8], start=True, stop=True)
                S.done(inst, "T")
            tn(_)

            def _(v):
                S.bar(v)
                inst = v.tensor_copy(csum_b[:, :], P0[:, 0:8])
                S.done(inst, "V")
            ve(_)

            for b in range(NBATCH):
                def _(g, b=b):
                    S.bar(g)
                    for ci in range(8):
                        inst = g.dma_start(out=wq_b[:, ci * 2048:(ci + 1) * 2048],
                                           in_=wqkv[ci * 128:(ci + 1) * 128, 1024:3072])
                        inst.then_inc(S.sem["P"], 16)
                    for ci in range(8):
                        inst = g.dma_start(out=xbf0[:, ci * 512:(ci + 1) * 512],
                                           in_=xfm[ci * 128:(ci + 1) * 128,
                                                   b * 4096:b * 4096 + 512])
                        inst.then_inc(S.sem["P"], 16)
                    S.cnt["P"] += 16 * 16
                gp(_)

                p_x = {0: S.cnt["P"]}
                t_projA = {}
                v_kvev = {}
                t_phik = {}
                v_relu = {}
                t_kv = {}
                v_kvacc = {}
                t_stats = {}
                a_sq = {}
                v_ln1a = {}
                a_rstd = {}
                v_cast = {}
                t_bc = {}
                v_MbRb = {}
                v_xn1 = {}

                for nt in range(8):
                    xcur = xbf0 if nt % 2 == 0 else xbf1
                    xnxt = xbf1 if nt % 2 == 0 else xbf0

                    if nt + 1 < 8:
                        def _(g, nt=nt, b=b, xnxt=xnxt):
                            if PAR_A: S.bar(g)
                            if nt >= 1:
                                g.wait_ge(S.sem["V"], v_xn1[nt - 1])
                                g.wait_ge(S.sem["T"], t_stats[nt - 1])
                            for ci in range(8):
                                inst = g.dma_start(out=xnxt[:, ci * 512:(ci + 1) * 512],
                                                   in_=xfm[ci * 128:(ci + 1) * 128,
                                                           b * 4096 + (nt + 1) * 512:
                                                           b * 4096 + (nt + 2) * 512])
                                inst.then_inc(S.sem["P"], 16)
                            S.cnt["P"] += 8 * 16
                        gp(_)
                        p_x[nt + 1] = S.cnt["P"]

                    def _(s_, nt=nt, xcur=xcur):
                        if PAR_A: S.bar(s_)
                        if nt == 0:
                            S.bar(s_)
                        else:
                            s_.wait_ge(S.sem["P"], p_x[nt])
                            s_.wait_ge(S.sem["T"], t_stats[nt - 1])
                        for ci in range(8):
                            inst = s_.activation(out=xsq[:, ci * 512:(ci + 1) * 512],
                                                 in_=xcur[:, ci * 512:(ci + 1) * 512],
                                                 func=AF.Square)
                        S.done(inst, "A")
                    sc(_)
                    a_sq[nt] = S.cnt["A"]

                    def _(t, nt=nt, xcur=xcur):
                        if PAR_A: S.bar(t)
                        if nt == 0:
                            S.bar(t)
                        else:
                            t.wait_ge(S.sem["A"], a_sq[nt])
                            t.wait_ge(S.sem["V"], v_MbRb[nt - 1])
                            t.wait_ge(S.sem["V"], v_ln1a[nt - 1])
                        if nt == 0:
                            t.wait_ge(S.sem["A"], a_sq[nt])
                        for ci in range(8):
                            t.matmul(PRa[0:1, :], ones_c[:], xcur[:, ci * 512:(ci + 1) * 512],
                                     start=(ci == 0), stop=(ci == 7))
                        for ci in range(8):
                            inst = t.matmul(PRb[0:1, :], ones_c[:], xsq[:, ci * 512:(ci + 1) * 512],
                                            start=(ci == 0), stop=(ci == 7))
                        S.done(inst, "T")
                    tn(_)
                    t_stats[nt] = S.cnt["T"]

                    def _(v, nt=nt):
                        if PAR_A: S.bar(v)
                        if nt == 0:
                            S.bar(v)
                        v.wait_ge(S.sem["T"], t_stats[nt])
                        v.tensor_scalar_mul(mt[:], PRa[0:1, :], 1.0 / DM)
                        inst = v.tensor_scalar_mul(vt[:], PRb[0:1, :], 1.0 / DM)
                        S.done(inst, "V")
                        v.wait_ge(S.sem["V"], S.cnt["V"])
                        inst = v.tensor_mul(rt[:], mt[:], mt[:])
                        S.done(inst, "V")
                        v.wait_ge(S.sem["V"], S.cnt["V"])
                        inst = v.tensor_sub(vt[:], vt[:], rt[:])
                        S.done(inst, "V")
                    ve(_)
                    v_ln1a[nt] = S.cnt["V"]

                    def _(s_, nt=nt):
                        if PAR_A: S.bar(s_)
                        s_.wait_ge(S.sem["V"], v_ln1a[nt])
                        inst = s_.activation(out=rt[:], in_=vt[:], func=AF.Ln,
                                             bias=epscol[0:1, 0:1])
                        S.done(inst, "A")
                        s_.wait_ge(S.sem["A"], S.cnt["A"])
                        inst = s_.activation(out=rt[:], in_=rt[:], func=AF.Exp, scale=-0.5)
                        S.done(inst, "A")
                    sc(_)
                    a_rstd[nt] = S.cnt["A"]

                    def _(v, nt=nt):
                        if PAR_A: S.bar(v)
                        v.wait_ge(S.sem["A"], a_rstd[nt])
                        v.tensor_copy(m_bf[:], mt[:])
                        inst = v.tensor_copy(r_bf[:], rt[:])
                        S.done(inst, "V")
                    ve(_)
                    v_cast[nt] = S.cnt["V"]

                    def _(t, nt=nt):
                        if PAR_A: S.bar(t)
                        t.wait_ge(S.sem["V"], v_cast[nt])
                        if nt >= 1:
                            t.wait_ge(S.sem["V"], v_MbRb[nt - 1])
                        t.matmul(P4[:, :], ones_r[:], m_bf[0:1, :], start=True, stop=True)
                        inst = t.matmul(PRb[:, :], ones_r[:], r_bf[0:1, :], start=True, stop=True)
                        S.done(inst, "T")
                    tn(_)
                    t_bc[nt] = S.cnt["T"]

                    def _(v, nt=nt):
                        if PAR_A: S.bar(v)
                        v.wait_ge(S.sem["T"], t_bc[nt])
                        v.tensor_copy(Mb[:, nt * 512:(nt + 1) * 512], P4[:, :])
                        inst = v.tensor_copy(Rb[:, nt * 512:(nt + 1) * 512], PRb[:, :])
                        S.done(inst, "V")
                    ve(_)
                    v_MbRb[nt] = S.cnt["V"]

                    def _(v, nt=nt, xcur=xcur):
                        if PAR_A: S.bar(v)
                        if nt >= 1:
                            v.wait_ge(S.sem["T"], t_projA[(nt - 1, 7)])
                        v.wait_ge(S.sem["V"], v_MbRb[nt])
                        x3 = xcur[:].rearrange("p (c t) -> p c t", c=8)
                        n3 = xn1[:].rearrange("p (c t) -> p c t", c=8)
                        mb3 = (Mb[:, nt * 512:(nt + 1) * 512]
                               .rearrange("p (u t) -> p u t", u=1).broadcast_to([128, 8, 512]))
                        rb3 = (Rb[:, nt * 512:(nt + 1) * 512]
                               .rearrange("p (u t) -> p u t", u=1).broadcast_to([128, 8, 512]))
                        v.tensor_tensor(n3, x3, mb3, ALU.subtract)
                        inst = v.tensor_tensor(n3, n3, rb3, ALU.mult)
                        S.done(inst, "V")
                    ve(_)
                    v_xn1[nt] = S.cnt["V"]

                    for pr in range(10):
                        if pr < 8:
                            def _(t, nt=nt, pr=pr):
                                if PAR_A: S.bar(t)
                                if pr == 0:
                                    t.wait_ge(S.sem["V"], v_xn1[nt])
                                prev = (nt, pr - 1) if pr >= 1 else (nt - 1, 7)
                                if prev in v_kvev:
                                    t.wait_ge(S.sem["V"], v_kvev[prev])
                                for ci in range(8):
                                    t.matmul(P0[:, :], wq_b[:, ci * 2048 + pr * 128:
                                                            ci * 2048 + (pr + 1) * 128],
                                             xn1[:, ci * 512:(ci + 1) * 512],
                                             start=(ci == 0), stop=(ci == 7))
                                for ci in range(8):
                                    inst = t.matmul(P1[:, :], wq_b[:, ci * 2048 + 1024 + pr * 128:
                                                                   ci * 2048 + 1024 + (pr + 1) * 128],
                                                    xn1[:, ci * 512:(ci + 1) * 512],
                                                    start=(ci == 0), stop=(ci == 7))
                                S.done(inst, "T")
                            tn(_)
                            t_projA[(nt, pr)] = S.cnt["T"]

                            def _(v, nt=nt, pr=pr):
                                if PAR_A: S.bar(v)
                                v.wait_ge(S.sem["T"], t_projA[(nt, pr)])
                                prev2 = (nt, pr - 2) if pr >= 2 else (nt - 1, pr + 6)
                                if prev2 in t_phik:
                                    v.wait_ge(S.sem["T"], t_phik[prev2])
                                sl = slice((pr % 2) * 512, (pr % 2) * 512 + 512)
                                v.tensor_copy(kbf2[:, sl], P0[:, :])
                                inst = v.tensor_copy(vbf2[:, sl], P1[:, :])
                                S.done(inst, "V")
                            ve(_)
                            v_kvev[(nt, pr)] = S.cnt["V"]

                        if 1 <= pr <= 8:
                            k = pr - 1

                            def _(t, nt=nt, k=k):
                                if PAR_A: S.bar(t)
                                t.wait_ge(S.sem["V"], v_kvev[(nt, k)])
                                prevk = (nt, k - 1) if k >= 1 else (nt - 1, 7)
                                if prevk in v_relu:
                                    t.wait_ge(S.sem["V"], v_relu[prevk])
                                s0 = (k % 2) * 512
                                for j in range(4):
                                    t.matmul(P2[:, j * 128:(j + 1) * 128],
                                             kbf2[:, s0 + j * 128:s0 + (j + 1) * 128],
                                             rf_b[:, k * 128:(k + 1) * 128],
                                             start=(j == 0), stop=(j == 3))
                                for j in range(4):
                                    inst = t.transpose(PT[:, j * 128:(j + 1) * 128],
                                                       vbf2[:, s0 + j * 128:s0 + (j + 1) * 128],
                                                       idb_s[:])
                                S.done(inst, "T")
                            tn(_)
                            t_phik[(nt, k)] = S.cnt["T"]

                            def _(v, nt=nt, k=k):
                                if PAR_A: S.bar(v)
                                v.wait_ge(S.sem["T"], t_phik[(nt, k)])
                                prevk2 = (nt, k - 2) if k >= 2 else (nt - 1, k + 6)
                                if prevk2 in t_kv:
                                    v.wait_ge(S.sem["T"], t_kv[prevk2])
                                s0 = (k % 2) * 512
                                vb = (k % 2) * 520
                                v.tensor_relu(phik2[:, s0:s0 + 512], P2[:, :])
                                for j in range(4):
                                    v.tensor_copy(vtm2[:, vb + j * 130:vb + j * 130 + 64],
                                                  PT[:, j * 128:j * 128 + 64])
                                    inst = v.tensor_copy(vtm2[:, vb + j * 130 + 65:vb + j * 130 + 129],
                                                         PT[:, j * 128 + 64:(j + 1) * 128])
                                S.done(inst, "V")
                            ve(_)
                            v_relu[(nt, k)] = S.cnt["V"]

                        if 2 <= pr <= 9:
                            m = pr - 2

                            def _(t, nt=nt, m=m):
                                if PAR_A: S.bar(t)
                                t.wait_ge(S.sem["V"], v_relu[(nt, m)])
                                prevm = (nt, m - 1) if m >= 1 else (nt - 1, 7)
                                if prevm in v_kvacc:
                                    t.wait_ge(S.sem["V"], v_kvacc[prevm])
                                s0 = (m % 2) * 512
                                vb = (m % 2) * 520
                                first = True
                                for j in range(4):
                                    t.matmul(P3[0:64, 0:65], phik2[:, s0 + j * 128:s0 + j * 128 + 64],
                                             vtm2[:, vb + j * 130:vb + j * 130 + 65],
                                             start=first, stop=False)
                                    first = False
                                    inst = t.matmul(P3[64:128, 65:130],
                                                    phik2[:, s0 + j * 128 + 64:s0 + (j + 1) * 128],
                                                    vtm2[:, vb + j * 130 + 65:vb + j * 130 + 130],
                                                    start=False, stop=(j == 3))
                                S.done(inst, "T")
                            tn(_)
                            t_kv[(nt, m)] = S.cnt["T"]

                            def _(v, nt=nt, m=m):
                                if PAR_A: S.bar(v)
                                v.wait_ge(S.sem["T"], t_kv[(nt, m)])
                                sl = slice(m * 130, (m + 1) * 130)
                                inst = v.tensor_add(kvacc[:, sl], kvacc[:, sl], P3[:, 0:130])
                                S.done(inst, "V")
                            ve(_)
                            v_kvacc[(nt, m)] = S.cnt["V"]

                def _(v, b=b):
                    S.bar(v)
                    v.memset(kvbd[:], 0.0)
                    v.memset(ksr[:], 0.0)
                    for pr in range(8):
                        base = pr * 130
                        v.tensor_copy(kvbd[0:64, pr * 128:pr * 128 + 64],
                                      kvacc[0:64, base:base + 64])
                        v.tensor_copy(kvbd[64:128, pr * 128 + 64:(pr + 1) * 128],
                                      kvacc[64:128, base + 65:base + 129])
                        v.tensor_copy(ksr[0:64, pr * 128:pr * 128 + 64],
                                      kvacc[0:64, base + 64:base + 65].broadcast_to([64, 64]))
                        inst = v.tensor_copy(ksr[64:128, pr * 128 + 64:(pr + 1) * 128],
                                             kvacc[64:128, base + 129:base + 130]
                                             .broadcast_to([64, 64]))
                    v.memset(kvacc[:], 0.0)
                    S.done(inst, "V")
                ve(_)

                def _(g, b=b):
                    S.bar(g)
                    for ci in range(8):
                        inst = g.dma_start(out=wq_b[:, ci * 2048:ci * 2048 + 1024],
                                           in_=wqkv[ci * 128:(ci + 1) * 128, 0:1024])
                        inst.then_inc(S.sem["P"], 16)
                    for ci in range(8):
                        inst = g.dma_start(out=xbf0[:, ci * 512:(ci + 1) * 512],
                                           in_=xfm[ci * 128:(ci + 1) * 128,
                                                   b * 4096:b * 4096 + 512])
                        inst.then_inc(S.sem["P"], 16)
                    S.cnt["P"] += 16 * 16
                gp(_)

                p_xB = {0: S.cnt["P"]}
                t_projB = {}
                v_qbf = {}
                t_phiq = {}
                v_reluB = {}
                t_ksr = {}
                v_recip = {}
                t_kvB = {}
                v_attn = {}
                v_xn1B = {}
                t_wout = {}
                v_x2b = {}
                a_sq2 = {}
                t_stats2 = {}
                v_x2tmt = {}
                v_ln2a = {}
                a_rstd2 = {}
                v_rm2 = {}
                t_tr2 = {}
                v_cols = {}
                v_xn2m = {}
                o_st = {}

                for nt in range(8):
                    xcur = xbf0 if nt % 2 == 0 else xbf1
                    xnxt = xbf1 if nt % 2 == 0 else xbf0

                    if nt + 1 < 8:
                        def _(g, nt=nt, b=b, xnxt=xnxt):
                            if PAR_B: S.bar(g)
                            if nt >= 1:
                                g.wait_ge(S.sem["V"], v_x2b[(nt - 1, 7)])
                            for ci in range(8):
                                inst = g.dma_start(out=xnxt[:, ci * 512:(ci + 1) * 512],
                                                   in_=xfm[ci * 128:(ci + 1) * 128,
                                                           b * 4096 + (nt + 1) * 512:
                                                           b * 4096 + (nt + 2) * 512])
                                inst.then_inc(S.sem["P"], 16)
                            S.cnt["P"] += 8 * 16
                        gp(_)
                        p_xB[nt + 1] = S.cnt["P"]

                    def _(v, nt=nt, xcur=xcur):
                        if PAR_B: S.bar(v)
                        if nt == 0:
                            S.bar(v)
                        else:
                            v.wait_ge(S.sem["P"], p_xB[nt])
                            v.wait_ge(S.sem["T"], t_projB[(nt - 1, 7)])
                        x3 = xcur[:].rearrange("p (c t) -> p c t", c=8)
                        n3 = xn1[:].rearrange("p (c t) -> p c t", c=8)
                        mb3 = (Mb[:, nt * 512:(nt + 1) * 512]
                               .rearrange("p (u t) -> p u t", u=1).broadcast_to([128, 8, 512]))
                        rb3 = (Rb[:, nt * 512:(nt + 1) * 512]
                               .rearrange("p (u t) -> p u t", u=1).broadcast_to([128, 8, 512]))
                        v.tensor_tensor(n3, x3, mb3, ALU.subtract)
                        inst = v.tensor_tensor(n3, n3, rb3, ALU.mult)
                        S.done(inst, "V")
                    ve(_)
                    v_xn1B[nt] = S.cnt["V"]

                    for pr in range(11):
                        if pr < 8:
                            def _(t, nt=nt, pr=pr):
                                if PAR_B: S.bar(t)
                                if nt == 0 and pr == 0:
                                    S.bar(t)
                                if pr == 0:
                                    t.wait_ge(S.sem["V"], v_xn1B[nt])
                                prev = (nt, pr - 1) if pr >= 1 else (nt - 1, 7)
                                if prev in v_qbf:
                                    t.wait_ge(S.sem["V"], v_qbf[prev])
                                for ci in range(8):
                                    inst = t.matmul(P0[:, :], wq_b[:, ci * 2048 + pr * 128:
                                                                  ci * 2048 + (pr + 1) * 128],
                                                    xn1[:, ci * 512:(ci + 1) * 512],
                                                    start=(ci == 0), stop=(ci == 7))
                                S.done(inst, "T")
                            tn(_)
                            t_projB[(nt, pr)] = S.cnt["T"]

                            def _(v, nt=nt, pr=pr):
                                if PAR_B: S.bar(v)
                                v.wait_ge(S.sem["T"], t_projB[(nt, pr)])
                                prev2 = (nt, pr - 2) if pr >= 2 else (nt - 1, pr + 6)
                                if prev2 in t_phiq:
                                    v.wait_ge(S.sem["T"], t_phiq[prev2])
                                sl = slice((pr % 2) * 512, (pr % 2) * 512 + 512)
                                inst = v.tensor_copy(qbf2[:, sl], P0[:, :])
                                S.done(inst, "V")
                            ve(_)
                            v_qbf[(nt, pr)] = S.cnt["V"]

                        if 1 <= pr <= 8:
                            k = pr - 1

                            def _(t, nt=nt, k=k):
                                if PAR_B: S.bar(t)
                                t.wait_ge(S.sem["V"], v_qbf[(nt, k)])
                                prevk = (nt, k - 1) if k >= 1 else (nt - 1, 7)
                                if prevk in v_reluB:
                                    t.wait_ge(S.sem["V"], v_reluB[prevk])
                                s0 = (k % 2) * 512
                                inst = t.matmul(P1[:, :], rf_b[:, k * 128:(k + 1) * 128],
                                                qbf2[:, s0:s0 + 512], start=True, stop=True)
                                S.done(inst, "T")
                            tn(_)
                            t_phiq[(nt, k)] = S.cnt["T"]

                            def _(v, nt=nt, k=k):
                                if PAR_B: S.bar(v)
                                v.wait_ge(S.sem["T"], t_phiq[(nt, k)])
                                prevk2 = (nt, k - 2) if k >= 2 else (nt - 1, k + 6)
                                if prevk2 in t_ksr:
                                    v.wait_ge(S.sem["T"], t_ksr[prevk2])
                                s0 = (k % 2) * 512
                                inst = v.tensor_relu(phiq2[:, s0:s0 + 512], P1[:, :])
                                S.done(inst, "V")
                            ve(_)
                            v_reluB[(nt, k)] = S.cnt["V"]

                        if 2 <= pr <= 9:
                            m = pr - 2

                            def _(t, nt=nt, m=m, b=b):
                                if PAR_B: S.bar(t)
                                t.wait_ge(S.sem["V"], v_reluB[(nt, m)])
                                if m >= 1:
                                    t.wait_ge(S.sem["V"], v_recip[(nt, m - 1)])
                                elif nt >= 1:
                                    t.wait_ge(S.sem["V"], v_ln2a[nt - 1])
                                s0 = (m % 2) * 512
                                inst = t.matmul(P2[:, :], ksr[:, m * 128:(m + 1) * 128],
                                                phiq2[:, s0:s0 + 512], start=True, stop=True)
                                S.done(inst, "T")
                            tn(_)
                            t_ksr[(nt, m)] = S.cnt["T"]

                            def _(v, nt=nt, m=m):
                                if PAR_B: S.bar(v)
                                v.wait_ge(S.sem["T"], t_ksr[(nt, m)])
                                prevm2 = (nt, m - 2) if m >= 2 else (nt - 1, m + 6)
                                if prevm2 in t_kvB:
                                    v.wait_ge(S.sem["T"], t_kvB[prevm2])
                                s0 = (m % 2) * 512
                                inst = v.tensor_scalar_add(recp2[:, s0:s0 + 512], P2[:, :], STAB)
                                S.done(inst, "V")
                                v.wait_ge(S.sem["V"], S.cnt["V"])
                                inst = v.reciprocal(recp2[:, s0:s0 + 512], recp2[:, s0:s0 + 512])
                                S.done(inst, "V")
                                v.wait_ge(S.sem["V"], S.cnt["V"])
                                inst = v.tensor_mul(phiqn2[:, s0:s0 + 512], phiq2[:, s0:s0 + 512],
                                                    recp2[:, s0:s0 + 512])
                                S.done(inst, "V")
                            ve(_)
                            v_recip[(nt, m)] = S.cnt["V"]

                        if 3 <= pr <= 10:
                            m3 = pr - 3

                            def _(t, nt=nt, m3=m3):
                                if PAR_B: S.bar(t)
                                t.wait_ge(S.sem["V"], v_recip[(nt, m3)])
                                prevm3 = (nt, m3 - 1) if m3 >= 1 else (nt - 1, 7)
                                if prevm3 in v_attn:
                                    t.wait_ge(S.sem["V"], v_attn[prevm3])
                                s0 = (m3 % 2) * 512
                                inst = t.matmul(P3[:, :], kvbd[:, m3 * 128:(m3 + 1) * 128],
                                                phiqn2[:, s0:s0 + 512], start=True, stop=True)
                                S.done(inst, "T")
                            tn(_)
                            t_kvB[(nt, m3)] = S.cnt["T"]

                            def _(v, nt=nt, m3=m3):
                                if PAR_B: S.bar(v)
                                v.wait_ge(S.sem["T"], t_kvB[(nt, m3)])
                                inst = v.tensor_copy(attn_nt[:, m3 * 512:(m3 + 1) * 512], P3[:, :])
                                S.done(inst, "V")
                            ve(_)
                            v_attn[(nt, m3)] = S.cnt["V"]

                    for oi in range(9):
                        if oi < 8:
                            def _(t, nt=nt, oi=oi):
                                if PAR_B: S.bar(t)
                                if oi == 0:
                                    t.wait_ge(S.sem["V"], v_attn[(nt, 7)])
                                    if nt >= 1:
                                        t.wait_ge(S.sem["V"], v_cols[nt - 1])
                                else:
                                    t.wait_ge(S.sem["V"], v_x2b[(nt, oi - 1)])
                                for qi in range(8):
                                    inst = t.matmul(P4[:, :], wo_b[:, qi * 1024 + oi * 128:
                                                                  qi * 1024 + (oi + 1) * 128],
                                                    attn_nt[:, qi * 512:(qi + 1) * 512],
                                                    start=(qi == 0), stop=(qi == 7))
                                S.done(inst, "T")
                            tn(_)
                            t_wout[(nt, oi)] = S.cnt["T"]

                            def _(v, nt=nt, oi=oi, xcur=xcur):
                                if PAR_B: S.bar(v)
                                v.wait_ge(S.sem["T"], t_wout[(nt, oi)])
                                prev2 = (nt, oi - 2) if oi >= 2 else (nt - 1, oi + 6)
                                if prev2 in t_stats2:
                                    v.wait_ge(S.sem["T"], t_stats2[prev2])
                                s0 = (oi % 2) * 512
                                v.tensor_add(x2f[:], P4[:, :], xcur[:, oi * 512:(oi + 1) * 512])
                                inst = v.tensor_scalar(out=x2b2[:, s0:s0 + 512], in0=x2f[:],
                                                       scalar1=bout_s[:, oi:oi + 1],
                                                       scalar2=None, op0=ALU.add)
                                S.done(inst, "V")
                            ve(_)
                            v_x2b[(nt, oi)] = S.cnt["V"]

                            def _(s_, nt=nt, oi=oi):
                                if PAR_B: S.bar(s_)
                                if nt == 0 and oi == 0:
                                    S.bar(s_)
                                s_.wait_ge(S.sem["V"], v_x2b[(nt, oi)])
                                prev2 = (nt, oi - 2) if oi >= 2 else (nt - 1, oi + 6)
                                if prev2 in t_stats2:
                                    s_.wait_ge(S.sem["T"], t_stats2[prev2])
                                s0 = (oi % 2) * 512
                                inst = s_.activation(out=x2s2[:, s0:s0 + 512],
                                                     in_=x2b2[:, s0:s0 + 512], func=AF.Square)
                                S.done(inst, "A")
                            sc(_)
                            a_sq2[(nt, oi)] = S.cnt["A"]

                        if 1 <= oi <= 8:
                            w = oi - 1

                            def _(t, nt=nt, w=w):
                                if PAR_B: S.bar(t)
                                t.wait_ge(S.sem["A"], a_sq2[(nt, w)])
                                if w == 0:
                                    if nt >= 1:
                                        t.wait_ge(S.sem["V"], v_ln2a[nt - 1])
                                        t.wait_ge(S.sem["V"], v_x2tmt[(nt - 1, 7)])
                                else:
                                    t.wait_ge(S.sem["V"], v_x2tmt[(nt, w - 1)])
                                s0 = (w % 2) * 512
                                t.matmul(PRa[0:1, :], ones_c[:], x2b2[:, s0:s0 + 512],
                                         start=(w == 0), stop=(w == 7))
                                t.matmul(PRb[0:1, :], ones_c[:], x2s2[:, s0:s0 + 512],
                                         start=(w == 0), stop=(w == 7))
                                for j in range(4):
                                    t.matmul(P2[:, j * 8:(j + 1) * 8],
                                             x2b2[:, s0 + j * 128:s0 + (j + 1) * 128],
                                             rw_b[:, w * 8:(w + 1) * 8],
                                             start=(w == 0 and j == 0), stop=(w == 7 and j == 3))
                                for j in range(4):
                                    inst = t.transpose(PT[:, j * 128:(j + 1) * 128],
                                                       x2b2[:, s0 + j * 128:s0 + (j + 1) * 128],
                                                       idb_s[:])
                                S.done(inst, "T")
                            tn(_)
                            t_stats2[(nt, w)] = S.cnt["T"]

                            def _(v, nt=nt, w=w):
                                if PAR_B: S.bar(v)
                                v.wait_ge(S.sem["T"], t_stats2[(nt, w)])
                                if w == 0 and nt >= 1:
                                    v.wait_ge(S.sem["O"], o_st[nt - 1])
                                for j in range(4):
                                    inst = v.tensor_copy(
                                        x2tmt[:, j * 1024 + w * 128:j * 1024 + (w + 1) * 128],
                                        PT[:, j * 128:(j + 1) * 128])
                                S.done(inst, "V")
                            ve(_)
                            v_x2tmt[(nt, w)] = S.cnt["V"]

                    def _(v, nt=nt, b=b):
                        if PAR_B: S.bar(v)
                        v.wait_ge(S.sem["T"], t_stats2[(nt, 7)])
                        for j in range(4):
                            v.tensor_copy(rawl[:, (b * 32 + nt * 4 + j) * 8:
                                               (b * 32 + nt * 4 + j + 1) * 8],
                                          P2[:, j * 8:(j + 1) * 8])
                        v.tensor_scalar_mul(m2t[:], PRa[0:1, :], 1.0 / DM)
                        inst = v.tensor_scalar_mul(v2t[:], PRb[0:1, :], 1.0 / DM)
                        S.done(inst, "V")
                        v.wait_ge(S.sem["V"], S.cnt["V"])
                        inst = v.tensor_mul(rm2t[:], m2t[:], m2t[:])
                        S.done(inst, "V")
                        v.wait_ge(S.sem["V"], S.cnt["V"])
                        inst = v.tensor_sub(v2t[:], v2t[:], rm2t[:])
                        S.done(inst, "V")
                    ve(_)
                    v_ln2a[nt] = S.cnt["V"]

                    def _(s_, nt=nt):
                        if PAR_B: S.bar(s_)
                        s_.wait_ge(S.sem["V"], v_ln2a[nt])
                        inst = s_.activation(out=r2t[:], in_=v2t[:], func=AF.Ln,
                                             bias=epscol[0:1, 0:1])
                        S.done(inst, "A")
                        s_.wait_ge(S.sem["A"], S.cnt["A"])
                        inst = s_.activation(out=r2t[:], in_=r2t[:], func=AF.Exp, scale=-0.5)
                        S.done(inst, "A")
                    sc(_)
                    a_rstd2[nt] = S.cnt["A"]

                    def _(v, nt=nt):
                        if PAR_B: S.bar(v)
                        v.wait_ge(S.sem["A"], a_rstd2[nt])
                        inst = v.tensor_mul(rm2t[:], r2t[:], m2t[:])
                        S.done(inst, "V")
                    ve(_)
                    v_rm2[nt] = S.cnt["V"]

                    def _(t, nt=nt):
                        if PAR_B: S.bar(t)
                        t.wait_ge(S.sem["V"], v_rm2[nt])
                        t.wait_ge(S.sem["V"], v_x2b[(nt, 7)])
                        for ch in range(4):
                            t.transpose(P4[:, ch:ch + 1], m2t[0:1, ch * 128:(ch + 1) * 128],
                                        idf_s[0:1, 0:1])
                        for ch in range(4):
                            t.transpose(P4[:, 4 + ch:5 + ch], r2t[0:1, ch * 128:(ch + 1) * 128],
                                        idf_s[0:1, 0:1])
                        for ch in range(4):
                            inst = t.transpose(P4[:, 8 + ch:9 + ch],
                                               rm2t[0:1, ch * 128:(ch + 1) * 128],
                                               idf_s[0:1, 0:1])
                        S.done(inst, "T")
                    tn(_)
                    t_tr2[nt] = S.cnt["T"]

                    def _(v, nt=nt, b=b):
                        if PAR_B: S.bar(v)
                        v.wait_ge(S.sem["T"], t_tr2[nt])
                        g0 = b * 32 + nt * 4
                        v.tensor_copy(m2col[:, g0:g0 + 4], P4[:, 0:4])
                        v.tensor_copy(r2col[:, g0:g0 + 4], P4[:, 4:8])
                        inst = v.tensor_copy(rm2col[:, g0:g0 + 4], P4[:, 8:12])
                        S.done(inst, "V")
                    ve(_)
                    v_cols[nt] = S.cnt["V"]

                    def _(v, nt=nt, b=b):
                        if PAR_B: S.bar(v)
                        for ch in range(4):
                            gch = b * 32 + nt * 4 + ch
                            inst = v.tensor_scalar(out=xn2buf[:, ch * 1024:(ch + 1) * 1024],
                                                   in0=x2tmt[:, ch * 1024:(ch + 1) * 1024],
                                                   scalar1=m2col[:, gch:gch + 1],
                                                   scalar2=r2col[:, gch:gch + 1],
                                                   op0=ALU.subtract, op1=ALU.mult)
                        S.done(inst, "V")
                    ve(_)
                    v_xn2m[nt] = S.cnt["V"]

                    def _(sync, nt=nt, b=b):
                        if PAR_B: S.bar(sync)
                        sync.wait_ge(S.sem["V"], v_xn2m[nt])
                        for j in range(4):
                            r0 = b * 4096 + nt * 512 + j * 128
                            inst = sync.dma_start(out=x2_loc[r0:r0 + 128, :],
                                                  in_=x2tmt[:, j * 1024:(j + 1) * 1024])
                            inst.then_inc(S.sem["O"], 16)
                        for j in range(4):
                            r0 = b * 4096 + nt * 512 + j * 128
                            inst = sync.dma_start(out=xn2_loc[r0:r0 + 128, :],
                                                  in_=xn2buf[:, j * 1024:(j + 1) * 1024])
                            inst.then_inc(S.sem["O"], 16)
                        S.cnt["O"] += 8 * 16
                    sy(_)
                    o_st[nt] = S.cnt["O"]

                    if b == 1 and nt == 3:
                        def _(g):
                            g.wait_ge(S.sem["O"], o_st[3])
                            g.collective_compute(
                                "AllGather", ALU.bypass, replica_groups=RG,
                                ins=[xn2_loc[4096:6144, :]],
                                outs=[xn2_all[32768:49152, :]],
                            ).then_inc(S.sem["C"])
                            S.cnt["C"] += 1
                        gp(_)
                    if b == 1 and nt == 7:
                        def _(g):
                            g.wait_ge(S.sem["O"], o_st[7])
                            g.collective_compute(
                                "AllGather", ALU.bypass, replica_groups=RG,
                                ins=[xn2_loc[6144:8192, :]],
                                outs=[xn2_all[49152:65536, :]],
                            ).then_inc(S.sem["C"])
                            S.cnt["C"] += 1
                        gp(_)

                def _(v, b=b):
                    S.bar(v)
                    sl = slice(b * 32 * 8, (b + 1) * 32 * 8)
                    c3 = (csum_b[:].rearrange("p (u e) -> p u e", u=1)
                          .broadcast_to([128, 32, 8]))
                    rm3 = (rm2col[:, b * 32:(b + 1) * 32]
                           .rearrange("p (c u) -> p c u", u=1).broadcast_to([128, 32, 8]))
                    r23 = (r2col[:, b * 32:(b + 1) * 32]
                           .rearrange("p (c u) -> p c u", u=1).broadcast_to([128, 32, 8]))
                    u3 = u_t[:, sl].rearrange("p (c e) -> p c e", e=8)
                    l3 = lgt[:, sl].rearrange("p (c e) -> p c e", e=8)
                    raw3 = rawl[:, sl].rearrange("p (c e) -> p c e", e=8)
                    v.tensor_tensor(u3, c3, rm3, ALU.mult)
                    inst = v.tensor_tensor(l3, raw3, r23, ALU.mult)
                    S.done(inst, "V")
                ve(_)

                def _(v, b=b):
                    S.bar(v)
                    sl = slice(b * 32 * 8, (b + 1) * 32 * 8)
                    inst = v.tensor_add(lgt[:, sl], lgt[:, sl], u_t[:, sl])
                    S.done(inst, "V")
                ve(_)

                def _(v, b=b):
                    S.bar(v)
                    sl = slice(b * 32 * 8, (b + 1) * 32 * 8)
                    l3 = lgt[:, sl].rearrange("p (c e) -> p c e", e=8)
                    rb3 = (rb_s[:].rearrange("p (u e) -> p u e", u=1)
                           .broadcast_to([128, 32, 8]))
                    inst = v.tensor_tensor(l3, l3, rb3, ALU.add)
                    S.done(inst, "V")
                ve(_)

                if b == 0:
                    def _(g):
                        g.wait_ge(S.sem["O"], o_st[7])
                        g.collective_compute(
                            "AllGather", ALU.bypass, replica_groups=RG,
                            ins=[xn2_loc[0:4096, :]],
                            outs=[xn2_all[0:32768, :]],
                        ).then_inc(S.sem["C"])
                        S.cnt["C"] += 1
                    gp(_)

            def _(v):
                S.bar(v)
                l3 = lgt[:].rearrange("p (c e) -> p c e", e=8)
                inst = v.tensor_reduce(mx_t[:], l3, AX.X, ALU.max)
                S.done(inst, "V")
            ve(_)

            def _(v):
                S.bar(v)
                u3 = u_t[:].rearrange("p (c e) -> p c e", e=8)
                l3 = lgt[:].rearrange("p (c e) -> p c e", e=8)

                def bc(t64):
                    return t64[:].rearrange("p (c u) -> p c u", u=1).broadcast_to([128, 64, 8])

                inst = v.tensor_tensor(u3, l3, bc(mx_t), ALU.subtract)
                S.done(inst, "V")
            ve(_)

            def _(s):
                S.bar(s)
                inst = s.activation(out=u_t[:], in_=u_t[:], func=AF.Exp)
                S.done(inst, "A")
            sc(_)

            def _(v):
                S.bar(v)
                u3 = u_t[:].rearrange("p (c e) -> p c e", e=8)

                v.tensor_reduce(s_t[:], u3, AX.X, ALU.add)
                inst = v.tensor_reduce(u1_t[:], u3, AX.X, ALU.max)
                S.done(inst, "V")
            ve(_)

            def bc(t64):
                return t64[:].rearrange("p (c u) -> p c u", u=1).broadcast_to([128, 64, 8])

            def bcM(t64):
                return t64[:].rearrange("p (u c) -> p u c", u=1).broadcast_to([128, 8, 64])

            U3 = u_t[:].rearrange("p (c e) -> p c e", e=8)
            E3 = epat[:].rearrange("p (c e) -> p c e", e=8)
            S3 = scr8[:].rearrange("p (c e) -> p c e", e=8)
            S3B = scr8b[:].rearrange("p (c e) -> p c e", e=8)
            L3V = lgt[:].rearrange("p (c e) -> p c e", e=8)
            EM = epat2[:].rearrange("p (e c) -> p e c", e=8)

            def _(v):
                S.bar(v)
                inst = v.tensor_tensor(S3, U3, bc(u1_t), ALU.is_ge)
                S.done(inst, "V")
            ve(_)

            def _(v):
                S.bar(v)
                inst = v.scalar_tensor_tensor(scr8b[:], scr8[:], -64.0, eaff[:],
                                              ALU.mult, ALU.add)
                S.done(inst, "V")
            ve(_)

            def _(v):
                S.bar(v)
                inst = v.tensor_reduce(i1_t[:], S3B, AX.X, ALU.min)
                S.done(inst, "V")
            ve(_)

            def _(v):
                S.bar(v)
                inst = v.tensor_tensor(S3, E3, bc(i1_t), ALU.not_equal)
                S.done(inst, "V")
            ve(_)

            def _(v):
                S.bar(v)
                inst = v.tensor_tensor(S3B, U3, S3, ALU.mult)
                S.done(inst, "V")
            ve(_)

            def _(v):
                S.bar(v)
                inst = v.tensor_reduce(u2_t[:], S3B, AX.X, ALU.max)
                S.done(inst, "V")
            ve(_)

            def _(v):
                S.bar(v)
                inst = v.tensor_tensor(L3V, U3, bc(u2_t), ALU.is_ge)
                S.done(inst, "V")
            ve(_)

            def _(v):
                S.bar(v)
                inst = v.tensor_mul(lgt[:], lgt[:], scr8[:])
                S.done(inst, "V")
            ve(_)

            def _(v):
                S.bar(v)
                inst = v.scalar_tensor_tensor(scr8b[:], lgt[:], -64.0, eaff[:],
                                              ALU.mult, ALU.add)
                S.done(inst, "V")
            ve(_)

            def _(v):
                S.bar(v)
                inst = v.tensor_reduce(i2_t[:], S3B, AX.X, ALU.min)
                S.done(inst, "V")
            ve(_)

            def _(v):
                S.bar(v)
                inst = v.tensor_add(red1[:], u1_t[:], u2_t[:])
                S.done(inst, "V")
            ve(_)

            def _(v):
                S.bar(v)
                inst = v.scalar_tensor_tensor(red1[:], s_t[:], REPS, red1[:],
                                              ALU.mult, ALU.add)
                S.done(inst, "V")
            ve(_)

            def _(s):
                S.bar(s)
                inst = s.activation(out=red1[:], in_=red1[:], func=AF.Ln)
                S.done(inst, "A")
            sc(_)

            def _(s):
                S.bar(s)
                inst = s.activation(out=red1[:], in_=red1[:], func=AF.Exp, scale=-1.0)
                S.done(inst, "A")
            sc(_)

            def _(v):
                S.bar(v)
                v.tensor_mul(w1_t[:], u1_t[:], red1[:])
                inst = v.tensor_mul(w2_t[:], u2_t[:], red1[:])
                S.done(inst, "V")
            ve(_)

            def _(v):
                S.bar(v)
                v.tensor_single_scalar(va1_t[:], w1_t[:], REPS, ALU.is_gt)
                inst = v.tensor_single_scalar(va2_t[:], w2_t[:], REPS, ALU.is_gt)
                S.done(inst, "V")
            ve(_)

            def _(v):
                S.bar(v)
                v.tensor_tensor(S3, E3, bc(i1_t), ALU.is_equal)
                v.tensor_tensor(S3B, E3, bc(i2_t), ALU.is_equal)
                v.tensor_tensor(lgt[:].rearrange("p (e c) -> p e c", e=8), EM,
                                bcM(i1_t), ALU.is_equal)
                inst = v.tensor_tensor(u_t[:].rearrange("p (e c) -> p e c", e=8), EM,
                                       bcM(i2_t), ALU.is_equal)
                S.done(inst, "V")
            ve(_)

            def _(v):
                S.bar(v)
                v.tensor_tensor(oh1[:].rearrange("p (c e) -> p c e", e=8), S3,
                                bc(va1_t), ALU.mult)
                v.tensor_tensor(oh2[:].rearrange("p (c e) -> p c e", e=8), S3B,
                                bc(va2_t), ALU.mult)
                v.tensor_tensor(ohM1[:].rearrange("p (e c) -> p e c", e=8),
                                lgt[:].rearrange("p (e c) -> p e c", e=8),
                                bcM(va1_t), ALU.mult)
                inst = v.tensor_tensor(ohM2[:].rearrange("p (e c) -> p e c", e=8),
                                       u_t[:].rearrange("p (e c) -> p e c", e=8),
                                       bcM(va2_t), ALU.mult)
                S.done(inst, "V")
            ve(_)

            def _(t):
                S.bar(t)
                t.matmul(P0[:, :], lt_s[:], oh1[:], start=True, stop=True)
                t.matmul(P1[:, :], lt_s[:], oh2[:], start=True, stop=True)
                t.matmul(P2[0:1, :], ones_c[:], ohM1[:], start=True, stop=True)
                inst = t.matmul(P3[0:1, :], ones_c[:], ohM2[:], start=True, stop=True)
                S.done(inst, "T")
            tn(_)

            def _(v):
                S.bar(v)
                v.tensor_copy(totb[0:1, 0:512], P2[0:1, :])
                inst = v.tensor_copy(totb[0:1, 512:1024], P3[0:1, :])
                S.done(inst, "V")
            ve(_)

            def _(v):
                S.bar(v)
                for k in range(2):
                    tv = totb[0:1, k * 512:(k + 1) * 512].rearrange("o (e c) -> o e c", e=8)
                    inst = v.tensor_reduce(cntrow[0:1, k * 8:(k + 1) * 8], tv, AX.X, ALU.add)
                S.done(inst, "V")
            ve(_)

            def _(t):
                S.bar(t)
                t.matmul(P2[:, :], onesf_r[:], totb[0:1, 0:512], start=True, stop=True)
                inst = t.matmul(P3[:, :], onesf_r[:], totb[0:1, 512:1024], start=True, stop=True)
                S.done(inst, "T")
            tn(_)

            def _(v):
                S.bar(v)
                v.tensor_copy(totb[:, 0:512], P2[:, :])
                inst = v.tensor_copy(totb[:, 512:1024], P3[:, :])
                S.done(inst, "V")
            ve(_)

            def _(v):
                S.bar(v)
                for k in range(2):
                    for ee in range(8):
                        sl = slice(k * 512 + ee * 64, k * 512 + (ee + 1) * 64)
                        inst = v.tensor_tensor_scan(tscab[:, sl], onescol_b[:], totb[:, sl],
                                                    0.0, ALU.mult, ALU.add)
                S.done(inst, "V")
            ve(_)

            def _(v):
                S.bar(v)
                inst = v.tensor_sub(tscab[:], tscab[:], totb[:])
                S.done(inst, "V")
            ve(_)

            for k in range(2):
                def _(v, k=k):
                    S.bar(v)
                    dst = pos1 if k == 0 else pos2
                    psrc = P0 if k == 0 else P1
                    exv = tscab[:, k * 512:(k + 1) * 512].rearrange("p (e c) -> p c e", e=8)
                    inst = v.tensor_tensor(dst[:].rearrange("p (c e) -> p c e", e=8),
                                           psrc[:, :].rearrange("p (c e) -> p c e", e=8),
                                           exv, ALU.add)
                    S.done(inst, "V")
                ve(_)

            def _(v):
                S.bar(v)
                v.tensor_tensor(S3, pos1[:].rearrange("p (c e) -> p c e", e=8),
                                oh1[:].rearrange("p (c e) -> p c e", e=8), ALU.mult)
                inst = v.tensor_tensor(S3B, pos2[:].rearrange("p (c e) -> p c e", e=8),
                                       oh2[:].rearrange("p (c e) -> p c e", e=8), ALU.mult)
                S.done(inst, "V")
            ve(_)

            def _(v):
                S.bar(v)
                v.tensor_reduce(po1[:], S3, AX.X, ALU.add)
                inst = v.tensor_reduce(po2[:], S3B, AX.X, ALU.add)
                S.done(inst, "V")
            ve(_)

            def _(v):
                S.bar(v)
                m3 = meta_sb[:].rearrange("p (c q) -> p c q", q=8)
                for qix, src in enumerate([i1_t, i2_t, po1, po2, va1_t, va2_t, w1_t, w2_t]):
                    v.tensor_copy(m3[:, :, qix:qix + 1],
                                  src[:].rearrange("p (c u) -> p c u", u=1))
                v.tensor_scalar(out=red1[:], in0=i1_t[:], scalar1=8192.0,
                                scalar2=0.0, op0=ALU.mult, op1=ALU.add)
                v.tensor_scalar(out=mx_t[:], in0=i2_t[:], scalar1=8192.0,
                                scalar2=65536.0, op0=ALU.mult, op1=ALU.add)
                v.tensor_scalar(out=s_t[:], in0=va1_t[:], scalar1=-131200.0,
                                scalar2=131200.0, op0=ALU.mult, op1=ALU.add)
                inst = v.tensor_scalar(out=u1_t[:], in0=va2_t[:], scalar1=-131200.0,
                                       scalar2=131200.0, op0=ALU.mult, op1=ALU.add)
                S.done(inst, "V")
            ve(_)

            def _(v):
                S.bar(v)
                v.tensor_add(red1[:], red1[:], po1[:])
                inst = v.tensor_add(mx_t[:], mx_t[:], po2[:])
                S.done(inst, "V")
            ve(_)

            def _(v):
                S.bar(v)
                v.tensor_mul(red1[:], red1[:], va1_t[:])
                inst = v.tensor_mul(mx_t[:], mx_t[:], va2_t[:])
                S.done(inst, "V")
            ve(_)

            def _(v):
                S.bar(v)
                v.tensor_add(red1[:], red1[:], s_t[:])
                inst = v.tensor_add(mx_t[:], mx_t[:], u1_t[:])
                S.done(inst, "V")
            ve(_)

            def _(v):
                S.bar(v)
                v.tensor_copy(offi[:, 0:64], red1[:])
                v.tensor_copy(offi[:, 64:128], mx_t[:])
                s20 = scv[:, 0:128].rearrange("p (c u) -> p c u", u=2)
                s21 = scv[:, 128:256].rearrange("p (c u) -> p c u", u=2)
                v.tensor_copy(s20[:, :, 0:1], tokid_s[:].rearrange("p (c u) -> p c u", u=1))
                v.tensor_copy(s21[:, :, 0:1], tokid_s[:].rearrange("p (c u) -> p c u", u=1))
                v.tensor_copy(s20[:, :, 1:2], w1_t[:].rearrange("p (c u) -> p c u", u=1))
                inst = v.tensor_copy(s21[:, :, 1:2], w2_t[:].rearrange("p (c u) -> p c u", u=1))
                S.done(inst, "V")
            ve(_)

            def _(sync):
                S.bar(sync)
                inst = sync.dma_start(out=cnt_loc[:, :], in_=cntrow[:])
                inst.then_inc(S.sem["O"], 16)
                S.cnt["O"] += 16
            sy(_)

            def _(g):
                S.bar(g)
                for k in range(2):
                    for ch in range(64):
                        inst = g.indirect_dma_start(
                            smap_loc[:, :],
                            IndirectOffsetOnAxis(offi[:, k * 64 + ch:k * 64 + ch + 1], 0),
                            scv[:, k * 128 + 2 * ch:k * 128 + 2 * ch + 2], None)
                        inst.then_inc(S.sem["O"], 16)
                S.cnt["O"] += 128 * 16
            gp(_)

            def _(g):
                S.bar(g)
                g.collective_compute(
                    "AllGather", ALU.bypass, replica_groups=RG,
                    ins=[cnt_loc.ap().opt()], outs=[cnt_all.ap().opt()],
                ).then_inc(S.sem["C"])
                g.collective_compute(
                    "AllGather", ALU.bypass, replica_groups=RG,
                    ins=[smap_loc.ap().opt()], outs=[smap_all.ap().opt()],
                ).then_inc(S.sem["C"])
                S.cnt["C"] += 2
            gp(_)

        with ExitStack() as ctx2:
            e2 = ctx2.enter_context
            w1_b = e2(nc.sbuf_tensor("w1_b", [128, 8 * 4096], BF16))
            w2_b = e2(nc.sbuf_tensor("w2_b", [128, 32 * 1024], BF16))
            b1_s = e2(nc.sbuf_tensor("b1_s", [128, 32], F32))
            b2lsb = e2(nc.sbuf_tensor("b2lsb", [1, 1024], BF16))
            onesf16 = e2(nc.sbuf_tensor("onesf16", [1, 16], F32))
            onesc16 = e2(nc.sbuf_tensor("onesc16", [16, 1], F32))
            tvrow = e2(nc.sbuf_tensor("tvrow", [1, 1], F32))
            gpre_sb = e2(nc.sbuf_tensor("gpre_sb", [16, 16], F32))
            ehot_sb = e2(nc.sbuf_tensor("ehot_sb", [16, 8], F32))
            base_sb = e2(nc.sbuf_tensor("base_sb", [1, 16], F32))
            gch_sb = e2(nc.sbuf_tensor("gch_sb", [16, 1], F32))
            ccol_sb = e2(nc.sbuf_tensor("ccol_sb", [16, 1], F32))
            grow_sb = e2(nc.sbuf_tensor("grow_sb", [1, 16], F32))
            gb_sb = e2(nc.sbuf_tensor("gb_sb", [128, 16], F32))
            bb_sb = e2(nc.sbuf_tensor("bb_sb", [128, 16], F32))
            jpat = e2(nc.sbuf_tensor("jpat", [128, 16], F32))
            tv_sb = e2(nc.sbuf_tensor("tv_sb", [128, 1], F32))
            slots = e2(nc.sbuf_tensor("slots", [128, 80], F32))
            bsel = e2(nc.sbuf_tensor("bsel", [128, 40 * 16], F32))
            gsel = e2(nc.sbuf_tensor("gsel", [128, 80], F32))
            srcr = e2(nc.sbuf_tensor("srcr", [128, 80], F32))
            wmask = e2(nc.sbuf_tensor("wmask", [128, 80], F32))
            srci = e2(nc.sbuf_tensor("srci", [128, 80], I32))
            tokw = e2(nc.sbuf_tensor("tokw", [128, 80 * 2], F32))
            tokf = e2(nc.sbuf_tensor("tokf", [128, 80], F32))
            wf = e2(nc.sbuf_tensor("wf", [128, 80], F32))
            toki = e2(nc.sbuf_tensor("toki", [128, 80], I32))
            ctx2b = ExitStack()
            e2b = ctx2b.enter_context
            lsb = e2b(nc.sbuf_tensor("lsb", [128, 1024], F32))
            lsr_s = e2b(nc.sbuf_tensor("lsr_s", [1, 1024], F32))
            b2_s = e2b(nc.sbuf_tensor("b2_s", [1, 1024], F32))

            PS = e2(nc.psum_tensor("PS", [16, 16], F32))
            PS3 = e2(nc.psum_tensor("PS3", [1, 16], F32))
            PT2a = e2(nc.psum_tensor("PT2a", [128, 1024], BF16))
            PT2b = e2(nc.psum_tensor("PT2b", [128, 1024], BF16))

            def _(sync):
                S.bar(sync)
                sync.dma_start(out=cnt_sb[:],
                               in_=cnt_all.rearrange("c (k e) -> (c k) e", e=8)[:, :]
                               ).then_inc(S.sem["Q"], 16)
                sync.dma_start(out=gpre_sb[:], in_=c_gpre[:, :]).then_inc(S.sem["Q"], 16)
                sync.dma_start(out=ehot_sb[0:1, :], in_=c_ehot[:, :]).then_inc(S.sem["Q"], 16)
                sync.dma_start(out=base_sb[:], in_=c_base[:, :]).then_inc(S.sem["Q"], 16)
                sync.dma_start(out=b1_s[:], in_=b1c[:, :]).then_inc(S.sem["Q"], 16)
                sync.dma_start(out=lsr_s[:], in_=lsr[:, :]).then_inc(S.sem["Q"], 16)
                sync.dma_start(out=b2_s[:], in_=b2r[:, :]).then_inc(S.sem["Q"], 16)
                S.cnt["Q"] += 7 * 16
            sy(_)

            def _(g):
                S.bar(g, skip=("C",))
                for ci in range(8):
                    g.dma_start(out=w1_b[:, ci * 4096:(ci + 1) * 4096],
                                in_=w1t[ci * 128:(ci + 1) * 128, :]).then_inc(S.sem["P"], 16)
                for hi in range(32):
                    g.dma_start(out=w2_b[:, hi * 1024:(hi + 1) * 1024],
                                in_=w2t[hi * 128:(hi + 1) * 128, :]).then_inc(S.sem["P"], 16)
                S.cnt["P"] += 40 * 16
                g.iota(slots[:], [[128, 80]], base=0, channel_multiplier=1,
                       allow_small_or_imprecise_dtypes=True).then_inc(S.sem["G"], 1)
                g.iota(jpat[:], [[0, 1], [1, 16]], base=0, channel_multiplier=0,
                       allow_small_or_imprecise_dtypes=True).then_inc(S.sem["G"], 1)
                S.cnt["G"] += 2
            gp(_)

            def _(v):
                S.bar(v)
                v.memset(onesf16[:], 1.0)
                v.memset(onesc16[:], 1.0)
                inst = v.tensor_mul(b2_s[:], b2_s[:], lsr_s[:])
                S.done(inst, "V")
            ve(_)

            def _(v):
                S.bar(v)
                inst = v.tensor_copy(b2lsb[:], b2_s[:])
                S.done(inst, "V")
            ve(_)

            def _(t):
                S.bar(t)
                t.matmul(P0[:, :], onesf_r[:], lsr_s[0:1, 0:512], start=True, stop=True)
                t.matmul(P1[:, :], onesf_r[:], lsr_s[0:1, 512:1024], start=True, stop=True)
                inst = t.matmul(PS[0:16, 0:8], onesf16[:], ehot_sb[0:1, :], start=True, stop=True)
                S.done(inst, "T")
            tn(_)

            def _(v):
                S.bar(v)
                v.tensor_copy(lsb[:, 0:512], P0[:, :])
                v.tensor_copy(lsb[:, 512:1024], P1[:, :])
                inst = v.tensor_copy(ehot_sb[:, :], PS[0:16, 0:8])
                S.done(inst, "V")
            ve(_)

            def _(v):
                S.bar(v)
                for hi in range(32):
                    inst = v.tensor_mul(w2_b[:, hi * 1024:(hi + 1) * 1024],
                                        w2_b[:, hi * 1024:(hi + 1) * 1024], lsb[:])
                S.done(inst, "V")
            ve(_)

            def _(t):
                S.bar(t)
                inst = t.matmul(PS[:, 0:8], gpre_sb[:], cnt_sb[:], start=True, stop=True)
                S.done(inst, "T")
            tn(_)

            def _(v):
                S.bar(v)
                v.tensor_tensor(bsel[0:16, 0:8], PS[:, 0:8], ehot_sb[:, :], ALU.mult)
                inst = v.tensor_tensor(bsel[0:16, 8:16], cnt_sb[:], ehot_sb[:, :], ALU.mult)
                S.done(inst, "V")
            ve(_)

            def _(v):
                S.bar(v)
                v.tensor_reduce(gch_sb[:], bsel[0:16, 0:8].rearrange("p (u e) -> p u e", u=1),
                                AX.X, ALU.add)
                inst = v.tensor_reduce(ccol_sb[:], bsel[0:16, 8:16].rearrange("p (u e) -> p u e", u=1),
                                       AX.X, ALU.add)
                S.done(inst, "V")
            ve(_)

            def _(t):
                S.bar(t)
                inst = t.matmul(PS3[0:1, 0:1], onesc16[:], ccol_sb[:], start=True, stop=True)
                S.done(inst, "T")
            tn(_)

            def _(v):
                S.bar(v)
                inst = v.tensor_copy(tvrow[:], PS3[0:1, 0:1])
                S.done(inst, "V")
            ve(_)

            def _(t):
                S.bar(t)
                inst = t.matmul(P0[:, 0:1], onesf_r[:], tvrow[0:1, :], start=True, stop=True)
                S.done(inst, "T")
            tn(_)

            def _(v):
                S.bar(v)
                inst = v.tensor_copy(tv_sb[:], P0[:, 0:1])
                S.done(inst, "V")
            ve(_)

            def _(sync):
                S.bar(sync)
                sync.dma_start(out=gscr[:, :], in_=gch_sb[:]).then_inc(S.sem["Q"], 16)
                S.cnt["Q"] += 16
            sy(_)

            def _(sync):
                S.bar(sync)
                inst = sync.dma_start(out=grow_sb[:], in_=gscr.rearrange("r u -> u r")[:, :],
                                      allow_slow_non_contiguous=True)
                inst.then_inc(S.sem["Q"], 16)
                S.cnt["Q"] += 16
            sy(_)

            def _(t):
                S.bar(t)
                t.matmul(P0[:, 0:16], onesf_r[:], grow_sb[0:1, :], start=True, stop=True)
                inst = t.matmul(P1[:, 0:16], onesf_r[:], base_sb[0:1, :], start=True, stop=True)
                S.done(inst, "T")
            tn(_)

            def _(v):
                S.bar(v)
                v.tensor_copy(gb_sb[:], P0[:, 0:16])
                inst = v.tensor_copy(bb_sb[:], P1[:, 0:16])
                S.done(inst, "V")
            ve(_)

            for hh in range(2):
                cs = slice(hh * 40, (hh + 1) * 40)
                b3 = bsel[:].rearrange("p (c j) -> p c j", j=16)
                sl3 = slots[:, cs].rearrange("p (c u) -> p c u", u=1).broadcast_to([128, 40, 16])
                g3 = gb_sb[:].rearrange("p (u j) -> p u j", u=1).broadcast_to([128, 40, 16])
                j3 = jpat[:].rearrange("p (u j) -> p u j", u=1).broadcast_to([128, 40, 16])
                i3 = srcr[:, cs].rearrange("p (c u) -> p c u", u=1).broadcast_to([128, 40, 16])
                bb3 = bb_sb[:].rearrange("p (u j) -> p u j", u=1).broadcast_to([128, 40, 16])

                def _(v, b3=b3, sl3=sl3, g3=g3):
                    S.bar(v)
                    inst = v.tensor_tensor(b3, sl3, g3, ALU.is_ge)
                    S.done(inst, "V")
                ve(_)

                def _(v, b3=b3, cs=cs):
                    S.bar(v)
                    inst = v.tensor_reduce(srcr[:, cs], b3, AX.X, ALU.add)
                    S.done(inst, "V")
                ve(_)

                def _(v, b3=b3, g3=g3, cs=cs):
                    S.bar(v)
                    v.tensor_scalar_add(srcr[:, cs], srcr[:, cs], -1.0)
                    inst = v.tensor_tensor(b3, b3, g3, ALU.mult)
                    S.done(inst, "V")
                ve(_)

                def _(v, b3=b3, j3=j3, i3=i3, cs=cs):
                    S.bar(v)
                    v.tensor_reduce(gsel[:, cs], b3, AX.X, ALU.max)
                    inst = v.tensor_tensor(b3, j3, i3, ALU.is_equal)
                    S.done(inst, "V")
                ve(_)

                def _(v, b3=b3, bb3=bb3):
                    S.bar(v)
                    inst = v.tensor_tensor(b3, b3, bb3, ALU.mult)
                    S.done(inst, "V")
                ve(_)

                def _(v, b3=b3, cs=cs):
                    S.bar(v)
                    inst = v.tensor_reduce(wmask[:, cs], b3, AX.X, ALU.add)
                    S.done(inst, "V")
                ve(_)

            def _(v):
                S.bar(v)
                inst = v.tensor_sub(gsel[:], slots[:], gsel[:])
                S.done(inst, "V")
            ve(_)

            def _(v):
                S.bar(v)
                inst = v.tensor_add(srcr[:], wmask[:], gsel[:])
                S.done(inst, "V")
            ve(_)

            def _(v):
                S.bar(v)
                v.tensor_scalar(out=wmask[:], in0=slots[:], scalar1=tv_sb[:, 0:1],
                                scalar2=None, op0=ALU.is_lt)
                inst = v.tensor_single_scalar(tokf[:], slots[:], float(CAPACITY), ALU.is_lt)
                S.done(inst, "V")
            ve(_)

            def _(v):
                S.bar(v)
                inst = v.tensor_mul(wmask[:], wmask[:], tokf[:])
                S.done(inst, "V")
            ve(_)

            def _(v):
                S.bar(v)
                inst = v.tensor_mul(srcr[:], srcr[:], wmask[:])
                S.done(inst, "V")
            ve(_)

            def _(v):
                S.bar(v)
                inst = v.tensor_copy(srci[:], srcr[:])
                S.done(inst, "V")
            ve(_)

            def _(g):
                S.bar(g)
                for ch in range(80):
                    inst = g.indirect_dma_start(
                        tokw[:, 2 * ch:2 * ch + 2], None,
                        smap_all[:, :], IndirectOffsetOnAxis(srci[:, ch:ch + 1], 0))
                    inst.then_inc(S.sem["P"], 16)
                S.cnt["P"] += 80 * 16
            gp(_)

            def _(v):
                S.bar(v)
                t3 = tokw[:].rearrange("p (c u) -> p c u", u=2)
                w3 = wmask[:].rearrange("p (c u) -> p c u", u=1)
                v.tensor_tensor(tokf[:].rearrange("p (c u) -> p c u", u=1), t3[:, :, 0:1], w3, ALU.mult)
                inst = v.tensor_tensor(wf[:].rearrange("p (c u) -> p c u", u=1), t3[:, :, 1:2], w3, ALU.mult)
                S.done(inst, "V")
            ve(_)

            def _(v):
                S.bar(v)
                inst = v.tensor_copy(toki[:], tokf[:])
                S.done(inst, "V")
            ve(_)

            ctx2b.close()
            h_b = e2(nc.sbuf_tensor("h_b", [128, 32 * 512], BF16))
            toks0 = e2(nc.sbuf_tensor("toks0", [128, 4096], BF16))
            toks1 = e2(nc.sbuf_tensor("toks1", [128, 4096], BF16))
            tfm = e2(nc.sbuf_tensor("tfm", [128, 8 * 512], BF16))
            sres = e2(nc.sbuf_tensor("sres", [128, 4 * 1024], BF16))

            NGRP = 20
            T0 = S.cnt["T"]
            A0 = S.cnt["A"]
            V0 = S.cnt["V"]
            O0 = S.cnt["O"]

            def t_round_m(g, r):
                return T0 + g * 44 + r + 1

            def t_fc1_m(g, hi):
                return T0 + g * 44 + 4 + hi + 1

            def t_fc2_m(g, s):
                return T0 + g * 44 + 36 + s + 1

            def a_gelu_m(g, hi):
                return A0 + g * 32 + hi + 1

            def v_evac_m(g, r):
                return V0 + g * 12 + r + 1

            def v_epi_m(g, s):
                return V0 + g * 12 + 4 + s + 1

            def _(g):
                S.bar(g)
                for j in range(4):
                    inst = g.indirect_dma_start(
                        toks0[:, j * 1024:(j + 1) * 1024], None,
                        xn2_all[:, :],
                        IndirectOffsetOnAxis(toki[:, j:j + 1], 0))
                    inst.then_inc(S.sem["P"], 16)
                S.cnt["P"] += 4 * 16
            gp(_)
            P0c = S.cnt["P"]

            def p_gath_m(g):
                return P0c + g * 64

            def o_store_m(g):
                return O0 + (g + 1) * 64

            for grp in range(NGRP):
                tok_cur = toks0 if grp % 2 == 0 else toks1
                tok_nxt = toks1 if grp % 2 == 0 else toks0

                if grp + 1 < NGRP:
                    def _(g, grp=grp, tok_nxt=tok_nxt):
                        if grp >= 1:
                            g.wait_ge(S.sem["V"], v_epi_m(grp - 1, 7))
                        for j in range(4):
                            inst = g.indirect_dma_start(
                                tok_nxt[:, j * 1024:(j + 1) * 1024], None,
                                xn2_all[:, :],
                                IndirectOffsetOnAxis(
                                    toki[:, (grp + 1) * 4 + j:(grp + 1) * 4 + j + 1], 0))
                            inst.then_inc(S.sem["P"], 16)
                        S.cnt["P"] += 4 * 16
                    gp(_)

                def _(t, grp=grp, tok_cur=tok_cur):
                    t.wait_ge(S.sem["P"], p_gath_m(grp))
                    for r in range(4):
                        gi = grp * 4 + r
                        if gi >= 2:
                            g2, r2 = divmod(gi - 2, 4)
                            t.wait_ge(S.sem["V"], v_evac_m(g2, r2))
                        pt = PT2a if gi % 2 == 0 else PT2b
                        for ci in range(8):
                            inst = t.transpose(
                                pt[:, ci * 128:(ci + 1) * 128],
                                tok_cur[:, r * 1024 + ci * 128:r * 1024 + (ci + 1) * 128],
                                idb_s[:])
                        S.done(inst, "T")
                        assert S.cnt["T"] == t_round_m(grp, r)
                tn(_)

                def _(v, grp=grp):
                    for r in range(4):
                        gi = grp * 4 + r
                        v.wait_ge(S.sem["T"], t_round_m(grp, r))
                        pt = PT2a if gi % 2 == 0 else PT2b
                        for ci in range(8):
                            inst = v.tensor_copy(
                                tfm[:, ci * 512 + r * 128:ci * 512 + (r + 1) * 128],
                                pt[:, ci * 128:(ci + 1) * 128])
                        S.done(inst, "V")
                        assert S.cnt["V"] == v_evac_m(grp, r)
                ve(_)

                def _(t, grp=grp):
                    t.wait_ge(S.sem["V"], v_evac_m(grp, 3))
                    for hi in range(32):
                        b = grp * 32 + hi
                        if b >= 2:
                            g2, h2 = divmod(b - 2, 32)
                            t.wait_ge(S.sem["A"], a_gelu_m(g2, h2))
                        p = P0 if hi % 2 == 0 else P1
                        for ci in range(8):
                            inst = t.matmul(p[:, :], w1_b[:, ci * 4096 + hi * 128:
                                                          ci * 4096 + (hi + 1) * 128],
                                            tfm[:, ci * 512:(ci + 1) * 512],
                                            start=(ci == 0), stop=(ci == 7))
                        S.done(inst, "T")
                        assert S.cnt["T"] == t_fc1_m(grp, hi)
                    t.wait_ge(S.sem["A"], a_gelu_m(grp, 31))
                    for s in range(8):
                        j, hf = divmod(s, 2)
                        c = grp * 8 + s
                        if c >= 2:
                            g2, s2 = divmod(c - 2, 8)
                            t.wait_ge(S.sem["V"], v_epi_m(g2, s2))
                        p = P2 if s % 2 == 0 else P3
                        for hi in range(32):
                            t.matmul(p[:, :],
                                     h_b[:, hi * 512 + j * 128:hi * 512 + (j + 1) * 128],
                                     w2_b[:, hi * 1024 + hf * 512:hi * 1024 + (hf + 1) * 512],
                                     start=(hi == 0), stop=False)
                        inst = t.matmul(p[:, :], ones_r[:],
                                        b2lsb[0:1, hf * 512:(hf + 1) * 512],
                                        start=False, stop=True)
                        S.done(inst, "T")
                        assert S.cnt["T"] == t_fc2_m(grp, s)
                tn(_)

                def _(s_, grp=grp):
                    for hi in range(32):
                        s_.wait_ge(S.sem["T"], t_fc1_m(grp, hi))
                        p = P0 if hi % 2 == 0 else P1
                        inst = s_.activation(out=h_b[:, hi * 512:(hi + 1) * 512], in_=p[:, :],
                                             func=AF.Gelu, bias=b1_s[:, hi:hi + 1])
                        S.done(inst, "A")
                        assert S.cnt["A"] == a_gelu_m(grp, hi)
                sc(_)

                def _(v, grp=grp, tok_cur=tok_cur):
                    for s in range(8):
                        j, hf = divmod(s, 2)
                        v.wait_ge(S.sem["T"], t_fc2_m(grp, s))
                        if s == 0 and grp >= 1:
                            v.wait_ge(S.sem["O"], o_store_m(grp - 1))
                        p = P2 if s % 2 == 0 else P3
                        sl = slice(j * 1024 + hf * 512, j * 1024 + (hf + 1) * 512)
                        v.tensor_add(bsel[:, 0:512], p[:, :], tok_cur[:, sl])
                        inst = v.tensor_scalar(out=sres[:, sl], in0=bsel[:, 0:512],
                                               scalar1=wf[:, grp * 4 + j:grp * 4 + j + 1],
                                               scalar2=None, op0=ALU.mult)
                        S.done(inst, "V")
                        assert S.cnt["V"] == v_epi_m(grp, s)
                ve(_)

                def _(sync, grp=grp):
                    sync.wait_ge(S.sem["V"], v_epi_m(grp, 7))
                    for j in range(4):
                        r0 = grp * 512 + j * 128
                        inst = sync.dma_start(out=sres_loc[r0:r0 + 128, :],
                                              in_=sres[:, j * 1024:(j + 1) * 1024])
                        inst.then_inc(S.sem["O"], 16)
                    S.cnt["O"] += 4 * 16
                    assert S.cnt["O"] == o_store_m(grp)
                sy(_)

            def _(g):
                S.bar(g)
                g.collective_compute(
                    "AllGather", ALU.bypass, replica_groups=RG,
                    ins=[sres_loc.ap().opt()], outs=[sres_all.ap().opt()],
                ).then_inc(S.sem["C"])
                S.cnt["C"] += 1
            gp(_)

        with ExitStack() as ctx3:
            e3x = ctx3.enter_context
            s16_sb = e3x(nc.sbuf_tensor("s16_sb", [16, 2], F32))
            g1row = e3x(nc.sbuf_tensor("g1row", [1, 8], F32))
            g2row = e3x(nc.sbuf_tensor("g2row", [1, 8], F32))
            gb1 = e3x(nc.sbuf_tensor("gb1", [128, 8], F32))
            gb2 = e3x(nc.sbuf_tensor("gb2", [128, 8], F32))
            epat3 = e3x(nc.sbuf_tensor("epat3", [128, 512], F32))
            scr3 = e3x(nc.sbuf_tensor("scr3", [128, 512], F32))
            scrb3 = e3x(nc.sbuf_tensor("scrb3", [128, 512], F32))
            keep3 = e3x(nc.sbuf_tensor("keep3", [128, 64], F32))
            off1 = e3x(nc.sbuf_tensor("off1", [128, 64], F32))
            off2 = e3x(nc.sbuf_tensor("off2", [128, 64], F32))
            oi1 = e3x(nc.sbuf_tensor("oi1", [128, 64], I32))
            oi2 = e3x(nc.sbuf_tensor("oi2", [128, 64], I32))
            gt = e3x(nc.sbuf_tensor("gt", [128, 1024], BF16))
            x2tile = e3x(nc.sbuf_tensor("x2tile", [128, 1024], BF16))
            ytile = e3x(nc.sbuf_tensor("ytile", [128, 1024], F32))
            PB = e3x(nc.psum_tensor("PB", [128, 16], F32))
            PSb = e3x(nc.psum_tensor("PSb", [1, 16], F32))

            def _(sync):
                S.bar(sync)
                sync.dma_start(out=s16_sb[:], in_=s16own[:, :]).then_inc(S.sem["Q"], 16)
                S.cnt["Q"] += 16
            sy(_)

            def _(g):
                S.bar(g)
                inst = g.iota(epat3[:], [[0, 64], [1, 8]], base=0, channel_multiplier=0,
                              allow_small_or_imprecise_dtypes=True)
                inst.then_inc(S.sem["G"], 1)
                S.cnt["G"] += 1
            gp(_)

            def _(t):
                S.bar(t)
                inst = t.matmul(PSb[0:1, 0:8], s16_sb[:, 0:1], cnt_sb[:], start=True, stop=True)
                S.done(inst, "T")
            tn(_)

            def _(v):
                S.bar(v)
                inst = v.tensor_copy(g1row[:], PSb[0:1, 0:8])
                S.done(inst, "V")
            ve(_)

            def _(t):
                S.bar(t)
                inst = t.matmul(PSb[0:1, 0:8], s16_sb[:, 1:2], cnt_sb[:], start=True, stop=True)
                S.done(inst, "T")
            tn(_)

            def _(v):
                S.bar(v)
                inst = v.tensor_copy(g2row[:], PSb[0:1, 0:8])
                S.done(inst, "V")
            ve(_)

            def _(t):
                S.bar(t)
                t.matmul(PB[:, 0:8], onesf_r[:], g1row[0:1, :], start=True, stop=False)
                inst = t.matmul(PB[:, 8:16], onesf_r[:], g2row[0:1, :], start=False, stop=True)
                S.done(inst, "T")
            tn(_)

            def _(v):
                S.bar(v)
                v.tensor_copy(gb1[:], PB[:, 0:8])
                inst = v.tensor_copy(gb2[:], PB[:, 8:16])
                S.done(inst, "V")
            ve(_)

            for k in range(2):
                m3 = meta_sb[:].rearrange("p (c q) -> p c q", q=8)
                e3 = epat3[:].rearrange("p (c e) -> p c e", e=8)
                s3 = scr3[:].rearrange("p (c e) -> p c e", e=8)
                s3b = scrb3[:].rearrange("p (c e) -> p c e", e=8)
                idx = m3[:, :, k:k + 1]
                po = m3[:, :, 2 + k:3 + k]
                va = m3[:, :, 4 + k:5 + k]
                gb = gb1 if k == 0 else gb2
                off = off1 if k == 0 else off2
                oi = oi1 if k == 0 else oi2
                o3 = off[:].rearrange("p (c u) -> p c u", u=1)
                k3 = keep3[:].rearrange("p (c u) -> p c u", u=1)

                def _(v, s3=s3, e3=e3, idx=idx):
                    S.bar(v)
                    inst = v.tensor_tensor(s3, e3, idx.broadcast_to([128, 64, 8]),
                                           ALU.is_equal)
                    S.done(inst, "V")
                ve(_)

                def _(v, s3=s3, s3b=s3b, gb=gb):
                    S.bar(v)
                    inst = v.tensor_tensor(s3b, s3,
                                           gb[:].rearrange("p (u e) -> p u e", u=1)
                                           .broadcast_to([128, 64, 8]), ALU.mult)
                    S.done(inst, "V")
                ve(_)

                def _(v, off=off, s3b=s3b):
                    S.bar(v)
                    inst = v.tensor_reduce(off[:], s3b, AX.X, ALU.add)
                    S.done(inst, "V")
                ve(_)

                def _(v, o3=o3, po=po, idx=idx):
                    S.bar(v)
                    v.tensor_tensor(o3, o3, po, ALU.add)
                    inst = v.tensor_scalar(
                        out=scrb3[:, 0:64].rearrange("p (c u) -> p c u", u=1), in0=idx,
                        scalar1=float(CPAD), scalar2=None, op0=ALU.mult)
                    S.done(inst, "V")
                ve(_)

                def _(v, off=off):
                    S.bar(v)
                    v.tensor_single_scalar(keep3[:], off[:], float(CAPACITY), ALU.is_lt)
                    inst = v.tensor_add(off[:], off[:], scrb3[:, 0:64])
                    S.done(inst, "V")
                ve(_)

                def _(v, k3=k3, va=va):
                    S.bar(v)
                    inst = v.tensor_tensor(k3, k3, va, ALU.mult)
                    S.done(inst, "V")
                ve(_)

                def _(v, off=off):
                    S.bar(v)
                    v.tensor_mul(off[:], off[:], keep3[:])
                    inst = v.tensor_scalar(out=scrb3[:, 64:128], in0=keep3[:],
                                           scalar1=-float(CPAD - 1), scalar2=float(CPAD - 1),
                                           op0=ALU.mult, op1=ALU.add)
                    S.done(inst, "V")
                ve(_)

                def _(v, off=off):
                    S.bar(v)
                    inst = v.tensor_add(off[:], off[:], scrb3[:, 64:128])
                    S.done(inst, "V")
                ve(_)

                def _(v, oi=oi, off=off):
                    S.bar(v)
                    inst = v.tensor_copy(oi[:], off[:])
                    S.done(inst, "V")
                ve(_)

            for ch in range(64):
                def _(g, ch=ch):
                    S.bar(g)
                    g.indirect_dma_start(
                        gt[:, :], None,
                        sres_all[:, :], IndirectOffsetOnAxis(oi1[:, ch:ch + 1], 0)
                    ).then_inc(S.sem["P"], 16)
                    S.cnt["P"] += 16
                    g.wait_ge(S.sem["P"], S.cnt["P"])
                    inst = g.indirect_dma_start(
                        gt[:, :], None,
                        sres_all[:, :], IndirectOffsetOnAxis(oi2[:, ch:ch + 1], 0),
                        compute_op=ALU.add)
                    inst.then_inc(S.sem["P"], 16)
                    S.cnt["P"] += 16
                gp(_)

                def _(sync, ch=ch):
                    S.bar(sync, skip=("P",))
                    inst = sync.dma_start(out=x2tile[:], in_=x2_loc[ch * 128:(ch + 1) * 128, :])
                    inst.then_inc(S.sem["Q"], 16)
                    S.cnt["Q"] += 16
                sy(_)

                def _(v, ch=ch):
                    S.bar(v)
                    inst = v.tensor_add(ytile[:], gt[:], x2tile[:])
                    S.done(inst, "V")
                ve(_)

                def _(sync, ch=ch):
                    S.bar(sync)
                    inst = sync.dma_start(out=yout[ch * 128:(ch + 1) * 128, :], in_=ytile[:])
                    inst.then_inc(S.sem["O"], 16)
                    S.cnt["O"] += 16
                sy(_)

            def _(sync):
                S.bar(sync)
                sync.nop()
            sy(_)
    return nc



_NC_CACHE = {}


def _get_nc(name, builder):
    if name not in _NC_CACHE:
        _NC_CACHE[name] = builder()
    return _NC_CACHE[name]


def kernel(L0_attn_x, L0_attn_W_qkv, L0_attn_RF, L0_attn_W_out, L0_attn_b_out,
           L0_ffn_router_w, L0_ffn_router_b, L0_ffn_fc1_w, L0_ffn_fc1_b,
           L0_ffn_fc2_w, L0_ffn_fc2_b, L0_ffn_layer_scale):
    x = np.asarray(L0_attn_x)
    wqkv = np.ascontiguousarray(np.asarray(L0_attn_W_qkv))
    rf = np.asarray(L0_attn_RF)
    wout = np.ascontiguousarray(np.asarray(L0_attn_W_out))
    bout = np.asarray(L0_attn_b_out)
    rw = np.asarray(L0_ffn_router_w)
    rb = np.asarray(L0_ffn_router_b)
    fc1w = np.asarray(L0_ffn_fc1_w)
    fc1b = np.asarray(L0_ffn_fc1_b)
    fc2w = np.asarray(L0_ffn_fc2_w)
    fc2b = np.asarray(L0_ffn_fc2_b)
    ls = np.asarray(L0_ffn_layer_scale)

    rfbd = np.zeros((8, 128, 128), np.float32)
    for pr in range(8):
        rfbd[pr, 0:64, 0:64] = rf[2 * pr]
        rfbd[pr, 64:128, 64:128] = rf[2 * pr + 1]
    rfbd = rfbd.reshape(8 * 128, 128)
    lt = np.triu(np.ones((128, 128), np.float32), 1).astype(bfdt)
    idb = np.eye(128, dtype=np.float32).astype(bfdt)
    idf = np.eye(128, dtype=np.float32)
    boutT = np.ascontiguousarray(bout.reshape(8, 128).T)
    rwt = np.ascontiguousarray(rw.T)
    rbrow = np.ascontiguousarray(rb.reshape(1, 8))

    gpre = np.zeros((16, 16), np.float32)
    for cp in range(8):
        for kp in range(2):
            for k in range(2):
                for c in range(8):
                    if (kp < k) or (kp == k and cp < c):
                        gpre[cp * 2 + kp, k * 8 + c] = 1.0

    nc = _get_nc("FUSED", build_fused)
    ins = []
    for c in range(8):
        xs = np.ascontiguousarray(x[2 * c:2 * c + 2].reshape(TPC, DM).T)
        ch = np.arange(64)[None, :]
        p = np.arange(128)[:, None]
        i_local = (ch % 32) * 128 + p
        bsel_t = ch // 32
        tok = np.where(
            bsel_t == 0, c * 4096 + i_local,
            np.where(i_local < 2048,
                     32768 + c * 2048 + i_local,
                     49152 + c * 2048 + (i_local - 2048))).astype(np.float32)
        basr = np.zeros((1, 16), np.float32)
        for k in range(2):
            for cc in range(8):
                basr[0, k * 8 + cc] = cc * SLOTROWS + (k * 8 + c) * 8192
        ehot = np.zeros((1, 8), np.float32)
        ehot[0, c] = 1.0
        s16o = np.zeros((16, 2), np.float32)
        for cp in range(8):
            for kp in range(2):
                for k in range(2):
                    if (kp < k) or (kp == k and cp < c):
                        s16o[cp * 2 + kp, k] = 1.0
        ins.append({
            "xfm": xs, "wqkv": wqkv, "wout": wout, "boutT": boutT,
            "rfbd": rfbd, "rwt": rwt, "rbrow": rbrow, "tokid": tok,
            "c_lt": lt, "c_idb": idb, "c_idf": idf,
            "w1t": np.ascontiguousarray(fc1w[c].T),
            "w2t": np.ascontiguousarray(fc2w[c].T),
            "b1c": np.ascontiguousarray(fc1b[c].reshape(32, 128).T),
            "lsr": np.ascontiguousarray(ls[c:c + 1]),
            "b2r": np.ascontiguousarray(fc2b[c:c + 1]),
            "c_gpre": gpre, "c_ehot": ehot, "c_base": basr,
            "s16own": s16o,
        })
    res = run_bass_kernel_spmd(nc, ins, core_ids=list(range(8))).results

    y = np.concatenate([np.asarray(res[c]["yout"]) for c in range(8)], axis=0)
    return y.reshape(B_GLOBAL, SEQ, DM)
